# version 03
# baseline (speedup 1.0000x reference)
import contextlib

import numpy as np
import ml_dtypes
import concourse.bass as bass
import concourse.mybir as mybir
from concourse.bass_utils import run_bass_kernel_spmd

F32 = mybir.dt.float32
BF16 = mybir.dt.bfloat16
AF = mybir.ActivationFunctionType
ALU = mybir.AluOpType
AX = mybir.AxisListType

NCORE = 8
BQ = 2
L = 2048
T = BQ * L
DM = 1024
CH = 128
NCHS = L // CH

PI = float(np.pi)
MAGIC = 12582912.0

BF = ml_dtypes.bfloat16


def _bf(a):
    return np.ascontiguousarray(np.asarray(a)).astype(BF)


def _f32(a):
    return np.ascontiguousarray(np.asarray(a, dtype=np.float32))


class Trk:
    def __init__(self, sem):
        self.sem = sem
        self.count = 0


class Prog:

    ENGS = ("sync", "gpsimd", "tensor", "vector", "scalar")

    def __init__(self, nc, self_trks=None):
        self.nc = nc
        self.ops = {e: [] for e in self.ENGS}
        self.self_trks = self_trks or {}

    def on(self, eng, fn, wait=(), inc=None, inc_n=1):
        ws = []
        strk = self.self_trks.get(eng)
        if strk is not None:
            if inc is None:
                inc = strk
            if strk.count > 0:
                ws.append((strk, strk.count))
        for w in wait:
            if isinstance(w, tuple):
                ws.append((w[0], w[1]))
            else:
                ws.append((w, w.count))
        self.ops[eng].append((ws, fn, inc, inc_n))
        if inc is not None:
            inc.count += inc_n
        return inc.count if inc is not None else None

    def dma(self, eng, out, in_, trk, wait=(), **kw):
        self.on(eng, lambda e, o=out, i=in_, kw=kw: e.dma_start(out=o, in_=i, **kw),
                wait=wait, inc=trk, inc_n=16)

    def play(self, block):
        for engname in self.ENGS:
            ops = self.ops[engname]

            def body(eng, ops=ops):
                lastw = {}
                for ws, fn, inc, inc_n in ops:
                    for (t, v) in ws:
                        if v > 0 and lastw.get(id(t), 0) < v:
                            eng.wait_ge(t.sem, v)
                            lastw[id(t)] = v
                    ins = fn(eng)
                    if inc is not None:
                        ins.then_inc(inc.sem, inc_n)

            getattr(block, engname)(body)


def make_consts():
    c = {}
    c["eye_bf"] = _bf(np.eye(128))
    c["eye_f32"] = _f32(np.eye(128))
    c["triu_incl"] = _f32(np.triu(np.ones((128, 128))))
    c["tril_strict"] = _f32(np.tril(np.ones((128, 128)), -1))
    c["mask_u_strict_neg"] = _bf(-np.triu(np.ones((128, 128)), 1))
    c["mask_l_strict_neg"] = _bf(-np.tril(np.ones((128, 128)), -1))
    c["mask_u_incl"] = _bf(np.triu(np.ones((128, 128))))
    c["freqs"] = _f32(np.broadcast_to(1.0 / 10000.0 ** (np.arange(16) / 16.0), (128, 16)))
    return c



def build_l0_front():
    nc = bass.Bass("TRN2", target_bir_lowering=False, debug=False)
    x = nc.dram_tensor("x", [T, DM], F32, kind="ExternalInput").ap()
    w_zxg = nc.dram_tensor("w_zxg", [DM, 4096], F32, kind="ExternalInput").ap()
    w_kv = nc.dram_tensor("w_kv", [DM, 2080], F32, kind="ExternalInput").ap()
    b_zxg = nc.dram_tensor("b_zxg", [128, 32], F32, kind="ExternalInput").ap()
    b_kv = nc.dram_tensor("b_kv", [128, 2080], F32, kind="ExternalInput").ap()
    rmsw = nc.dram_tensor("rmsw", [128, 8], F32, kind="ExternalInput").ap()
    convw = nc.dram_tensor("convw", [128, 64], F32, kind="ExternalInput").ap()
    convb = nc.dram_tensor("convb", [128, 16], F32, kind="ExternalInput").ap()
    gatew = nc.dram_tensor("gatew", [2048, 112], F32, kind="ExternalInput").ap()
    gateb = nc.dram_tensor("gateb", [128, 112], F32, kind="ExternalInput").ap()
    ldsb = nc.dram_tensor("ldsb", [128, 16], F32, kind="ExternalInput").ap()
    freqs = nc.dram_tensor("freqs", [128, 16], F32, kind="ExternalInput").ap()
    qw = nc.dram_tensor("qw", [2048, 2048], F32, kind="ExternalInput").ap()
    eye32 = nc.dram_tensor("eye32", [128, 128], F32, kind="ExternalInput").ap()
    triu_i = nc.dram_tensor("triu_i", [128, 128], F32, kind="ExternalInput").ap()
    tril_s = nc.dram_tensor("tril_s", [128, 128], F32, kind="ExternalInput").ap()

    xn_tm = nc.dram_tensor("xn_tm", [T, DM], BF16, kind="ExternalOutput").ap()
    xr0_tm = nc.dram_tensor("xr0_tm", [T, DM], BF16, kind="ExternalOutput").ap()
    z_fm = nc.dram_tensor("z_fm", [16, 128, T], BF16, kind="ExternalOutput").ap()
    xc_fm = nc.dram_tensor("xc_fm", [16, 128, T], BF16, kind="ExternalOutput").ap()
    ktm = nc.dram_tensor("ktm", [T, 2048], BF16, kind="ExternalOutput").ap()
    qtm = nc.dram_tensor("qtm", [T, 2048], BF16, kind="ExternalOutput").ap()
    vt_o = nc.dram_tensor("vt_o", [T, 32], F32, kind="ExternalOutput").ap()
    gcols = nc.dram_tensor("gcols", [T, 9, 16], F32, kind="ExternalOutput").ap()
    ghm = nc.dram_tensor("ghm", [BQ, 3, 16, L], F32, kind="ExternalOutput").ap()

    with contextlib.ExitStack() as _es:
        wzs = _es.enter_context(nc.sbuf_tensor("wzs", [128, 2, 8, 1024], BF16))
        wkv_sb = _es.enter_context(nc.sbuf_tensor("wkv_sb", [128, 8, 2080], BF16))
        gw_sb = _es.enter_context(nc.sbuf_tensor("gw_sb", [128, 16, 112], BF16))
        qwb = _es.enter_context(nc.sbuf_tensor("qwb", [128, 2, 16, 512], BF16))
        diag_sb = _es.enter_context(nc.sbuf_tensor("diag_sb", [128, 64, 128], BF16))
        bz_sb = _es.enter_context(nc.sbuf_tensor("bz_sb", [128, 32], F32))
        bkv_sb = _es.enter_context(nc.sbuf_tensor("bkv_sb", [128, 2080], F32))
        rw_sb = _es.enter_context(nc.sbuf_tensor("rw_sb", [128, 8], F32))
        cw_sb = _es.enter_context(nc.sbuf_tensor("cw_sb", [128, 64], F32))
        cb_sb = _es.enter_context(nc.sbuf_tensor("cb_sb", [128, 16], F32))
        gb_sb = _es.enter_context(nc.sbuf_tensor("gb_sb", [128, 112], F32))
        lds_sb = _es.enter_context(nc.sbuf_tensor("lds_sb", [128, 16], F32))
        fq_sb = _es.enter_context(nc.sbuf_tensor("fq_sb", [128, 16], F32))
        eye_sb = _es.enter_context(nc.sbuf_tensor("eye_sb", [128, 128], F32))
        triu_sb = _es.enter_context(nc.sbuf_tensor("triu_sb", [128, 128], F32))
        tril_sb = _es.enter_context(nc.sbuf_tensor("tril_sb", [128, 128], F32))
        xt = _es.enter_context(nc.sbuf_tensor("xt", [128, 2, DM], BF16))
        cols = _es.enter_context(nc.sbuf_tensor("cols", [128, 16], F32))
        xnt = _es.enter_context(nc.sbuf_tensor("xnt", [128, 2, DM], BF16))
        xrt = _es.enter_context(nc.sbuf_tensor("xrt", [128, 2, DM], BF16))
        xr_fm = _es.enter_context(nc.sbuf_tensor("xr_fm", [128, 8, 512], BF16))
        xg_buf = _es.enter_context(nc.sbuf_tensor("xg_buf", [128, 16, 515], BF16))
        xc_sb = _es.enter_context(nc.sbuf_tensor("xc_sb", [128, 16, 512], BF16))
        z_sb = _es.enter_context(nc.sbuf_tensor("z_sb", [128, 2, 512], BF16))
        kv_buf = _es.enter_context(nc.sbuf_tensor("kv_buf", [128, 2080], F32))
        vsave = _es.enter_context(nc.sbuf_tensor("vsave", [128, 4, 32], F32))
        sq_sb = _es.enter_context(nc.sbuf_tensor("sq_sb", [128, 2048], F32))
        nrm_sb = _es.enter_context(nc.sbuf_tensor("nrm_sb", [128, 16], F32))
        kq_bf = _es.enter_context(nc.sbuf_tensor("kq_bf", [128, 2048], BF16))
        qtmp = _es.enter_context(nc.sbuf_tensor("qtmp", [128, 512], F32))
        gt = _es.enter_context(nc.sbuf_tensor("gt", [128, 112], F32))
        gsc = _es.enter_context(nc.sbuf_tensor("gsc", [128, 40, 16], F32))
        stage = _es.enter_context(nc.sbuf_tensor("stage", [128, 9, 16], F32))
        vt_sb = _es.enter_context(nc.sbuf_tensor("vt_sb", [128, 32], F32))
        ghm_sb = _es.enter_context(nc.sbuf_tensor("ghm_sb", [48, 128], F32))
        ps0 = _es.enter_context(nc.psum_tensor("ps0", [128, 512], F32))
        ps1 = _es.enter_context(nc.psum_tensor("ps1", [128, 512], F32))
        psg = _es.enter_context(nc.psum_tensor("psg", [128, 112], F32))
        psc = _es.enter_context(nc.psum_tensor("psc", [128, 64], F32))
        pst = _es.enter_context(nc.psum_tensor("pst", [48, 128], F32))
        s_ld = _es.enter_context(nc.semaphore("s_ld"))
        s_gp = _es.enter_context(nc.semaphore("s_gp"))
        s_te = _es.enter_context(nc.semaphore("s_te"))
        s_ve = _es.enter_context(nc.semaphore("s_ve"))
        s_sc = _es.enter_context(nc.semaphore("s_sc"))
        block = _es.enter_context(nc.Block())
        ld = Trk(s_ld); gp = Trk(s_gp); te = Trk(s_te); ve = Trk(s_ve); sct = Trk(s_sc)
        P = Prog(nc, self_trks={"vector": ve, "scalar": sct})

        for ap_, t_ in ((b_zxg, bz_sb), (b_kv, bkv_sb), (rmsw, rw_sb), (convw, cw_sb),
                        (convb, cb_sb), (gateb, gb_sb), (ldsb, lds_sb), (freqs, fq_sb),
                        (eye32, eye_sb), (triu_i, triu_sb), (tril_s, tril_sb)):
            P.dma("sync", t_[:], ap_[:], ld)
        w_consts = (ld, ld.count)

        for k in range(8):
            P.dma("gpsimd", wkv_sb[:, k, :], w_kv[k * 128:(k + 1) * 128, :], gp)
        for k in range(16):
            P.dma("gpsimd", gw_sb[:, k, :], gatew[k * 128:(k + 1) * 128, :], gp)
        w_weights = (gp, gp.count)

        for fc in range(16):
            for j in range(4):
                P.on("vector", lambda e, fc=fc, j=j: e.tensor_scalar(
                    out=diag_sb[:, fc * 4 + j, :], in0=eye_sb[:],
                    scalar1=cw_sb[:, j * 16 + fc:j * 16 + fc + 1], scalar2=None, op0=ALU.mult),
                    wait=[w_consts], inc=ve)
        P.on("scalar", lambda e: e.activation(out=lds_sb[:], in_=lds_sb[:], func=AF.Exp),
             wait=[w_consts], inc=sct)
        w_diag = (ve, ve.count)

        def c(i):
            return cols[:, i:i + 1]

        p0_parity_guard = [None, None]
        for i in range(32):
            par = i % 2
            wl = [p0_parity_guard[par]] if p0_parity_guard[par] is not None else []
            P.dma("gpsimd", xt[:, par, :], x[i * 128:(i + 1) * 128, :], gp, wait=wl)
            wx = (gp, gp.count)
            xti = xt[:, par, :]
            P.on("vector", lambda e, xti=xti, par=par: e.tensor_scalar(
                out=xnt[:, par, :], in0=xti, scalar1=1.0, scalar2=None,
                op0=ALU.mult, op1=ALU.add, accum_out=c(0)), wait=[wx], inc=ve)
            P.on("vector", lambda e, xti=xti, par=par: e.scalar_tensor_tensor(
                out=xrt[:, par, :], in0=xti, scalar=1.0, in1=xti,
                op0=ALU.bypass, op1=ALU.mult, accum_out=c(1)), inc=ve)
            P.on("vector", lambda e: e.tensor_scalar(out=c(2), in0=c(0), scalar1=1.0 / DM, scalar2=None, op0=ALU.mult), inc=ve)
            P.on("vector", lambda e: e.tensor_scalar(out=c(3), in0=c(1), scalar1=1.0 / DM, scalar2=None, op0=ALU.mult), inc=ve)
            P.on("vector", lambda e: e.tensor_scalar(out=c(4), in0=c(2), scalar1=c(2), scalar2=None, op0=ALU.mult), inc=ve)
            P.on("vector", lambda e: e.tensor_tensor(out=c(5), in0=c(3), in1=c(4), op=ALU.subtract), inc=ve)
            P.on("vector", lambda e: e.tensor_scalar(out=c(6), in0=c(5), scalar1=1e-5, scalar2=None, op0=ALU.add), inc=ve)
            P.on("scalar", lambda e: e.activation(out=c(7), in_=c(6), func=AF.Ln), wait=[ve], inc=sct)
            P.on("scalar", lambda e: e.activation(out=c(8), in_=c(7), func=AF.Exp, scale=-0.5), inc=sct)
            P.on("vector", lambda e: e.reciprocal(out=c(9), in_=c(6)), wait=[sct], inc=ve)
            P.on("vector", lambda e: e.tensor_tensor(out=c(10), in0=c(5), in1=c(9), op=ALU.mult), inc=ve)
            P.on("vector", lambda e: e.tensor_scalar(out=c(11), in0=c(10), scalar1=1e-6, scalar2=None, op0=ALU.add), inc=ve)
            P.on("scalar", lambda e: e.activation(out=c(12), in_=c(11), func=AF.Ln), wait=[ve], inc=sct)
            P.on("scalar", lambda e: e.activation(out=c(13), in_=c(12), func=AF.Exp, scale=-0.5), inc=sct)
            P.on("vector", lambda e: e.engine_nop(), wait=[sct], inc=ve)
            P.on("vector", lambda e: e.tensor_tensor(out=c(14), in0=c(8), in1=c(13), op=ALU.mult), inc=ve)
            P.on("vector", lambda e, xti=xti, par=par: e.tensor_scalar(
                out=xnt[:, par, :], in0=xti, scalar1=c(2), scalar2=c(8), op0=ALU.subtract, op1=ALU.mult), inc=ve)
            P.on("vector", lambda e, xti=xti, par=par: e.tensor_scalar(
                out=xrt[:, par, :], in0=xti, scalar1=c(2), scalar2=c(14), op0=ALU.subtract, op1=ALU.mult), inc=ve)
            wv = (ve, ve.count)
            P.dma("gpsimd", xn_tm[i * 128:(i + 1) * 128, :], xnt[:, par, :], gp, wait=[wv])
            P.dma("gpsimd", xr0_tm[i * 128:(i + 1) * 128, :], xrt[:, par, :], gp, wait=[wv])
            p0_parity_guard[par] = (gp, gp.count)
        w_p0_done = (gp, gp.count)

        psmain = [ps0, ps1]
        ps_guard = [None, None]
        psi = [0]

        def next_ps():
            i = psi[0] % 2
            psi[0] += 1
            return psmain[i], i

        def mm_acc(lhsTs, rhss, n, wait_extra=()):
            ps, i = next_ps()
            w = list(wait_extra)
            if ps_guard[i] is not None:
                w.append(ps_guard[i])
            for j in range(n):
                P.on("tensor", lambda e, a=lhsTs(j), b=rhss(j), ps=ps, j=j, n=n: e.matmul(
                    ps[:, 0:b.shape[-1]], a, b, start=(j == 0), stop=(j == n - 1)),
                    wait=(w if j == 0 else ()), inc=(te if j == n - 1 else None))
            return ps, i, (te, te.count)

        w_xr_ld = {}
        for g in range(8):
            wprev = [(te, te.count), (ve, ve.count)] if g > 0 else [w_p0_done]
            if g == 0:
                wprev = [w_p0_done]
            for dc in range(8):
                for st4 in range(4):
                    tok0 = g * 512 + st4 * 128
                    P.dma("sync", xr_fm[:, dc, st4 * 128:(st4 + 1) * 128],
                          xr0_tm[tok0:tok0 + 128, dc * 128:(dc + 1) * 128], ld,
                          wait=(wprev if (dc == 0 and st4 == 0) else ()), transpose=True)
            w_xr = (ld, ld.count)
            for dc in range(8):
                P.on("vector", lambda e, dc=dc: e.tensor_scalar(
                    out=xr_fm[:, dc, :], in0=xr_fm[:, dc, :], scalar1=rw_sb[:, dc:dc + 1],
                    scalar2=None, op0=ALU.mult), wait=[w_xr, w_weights], inc=ve)
            w_rms = (ve, ve.count)

            if g % 4 == 0:
                P.on("vector", lambda e: e.memset(xg_buf[:, :, 0:3], 0.0),
                     wait=[(te, te.count)], inc=ve)
            else:
                P.on("vector", lambda e: e.tensor_copy(
                    out=xg_buf[:, :, 0:3], in_=xg_buf[:, :, 512:515]),
                    wait=[(te, te.count)], inc=ve)
            w_halo = (ve, ve.count)

            for ob in range(4):
                wpar = (g * 4 + ob) % 2
                P.dma("gpsimd", wzs[:, wpar, :, :],
                      w_zxg.rearrange("(k p) c -> p k c", p=128)[:, :, ob * 1024:(ob + 1) * 1024],
                      gp, wait=[(te, te.count)])
                w_wz = (gp, gp.count)
                for oci in range(8):
                    oc = ob * 8 + oci
                    ps, pi, wmm = mm_acc(
                        lambda k, oci=oci, wpar=wpar: wzs[:, wpar, k, oci * 128:(oci + 1) * 128],
                        lambda k, g=g: xr_fm[:, k, :], 8,
                        wait_extra=[w_rms, w_halo, w_wz])
                    if oc < 16:
                        zpar = oc % 2
                        P.on("scalar", lambda e, ps=ps, oc=oc, zpar=zpar: e.activation(
                            out=z_sb[:, zpar, :], in_=ps[:], func=AF.Identity,
                            bias=bz_sb[:, oc:oc + 1]), wait=[wmm, (gp, gp.count)], inc=sct)
                        wz2 = (sct, sct.count)
                        P.dma("gpsimd", z_fm[oc, :, g * 512:(g + 1) * 512], z_sb[:, zpar, :], gp, wait=[wz2])
                    else:
                        P.on("scalar", lambda e, ps=ps, oc=oc: e.activation(
                            out=xg_buf[:, oc - 16, 3:515], in_=ps[:], func=AF.Identity,
                            bias=bz_sb[:, oc:oc + 1]), wait=[wmm], inc=sct)
                    ps_guard[pi] = (sct, sct.count)

            w_xg = (sct, sct.count)
            for fc in range(16):
                ps, pi, wmm = mm_acc(
                    lambda j, fc=fc: diag_sb[:, fc * 4 + j, :],
                    lambda j, fc=fc: xg_buf[:, fc, j:j + 512], 4,
                    wait_extra=[w_xg, w_diag])
                P.on("scalar", lambda e, ps=ps, fc=fc: e.activation(
                    out=xc_sb[:, fc, :], in_=ps[:], func=AF.Silu,
                    bias=cb_sb[:, fc:fc + 1]), wait=[wmm, (gp, gp.count)], inc=sct)
                wxc = (sct, sct.count)
                P.dma("gpsimd", xc_fm[fc, :, g * 512:(g + 1) * 512], xc_sb[:, fc, :], gp, wait=[wxc])
                ps_guard[pi] = wxc

            w_conv = (sct, sct.count)

            kvcols = [(0, 512), (512, 1024), (1024, 1536), (1536, 2048), (2048, 2080)]
            for st4 in range(4):
                for (c0, c1) in kvcols:
                    ps, pi, wmm = mm_acc(
                        lambda k, st4=st4: xr_fm[:, k, st4 * 128:(st4 + 1) * 128],
                        lambda k, c0=c0, c1=c1: wkv_sb[:, k, c0:c1], 8,
                        wait_extra=[])
                    P.on("vector", lambda e, ps=ps, c0=c0, c1=c1: e.tensor_tensor(
                        out=kv_buf[:, c0:c1], in0=ps[:, 0:c1 - c0], in1=bkv_sb[:, c0:c1],
                        op=ALU.add), wait=[wmm], inc=ve)
                    ps_guard[pi] = (ve, ve.count)
                P.on("vector", lambda e: e.tensor_tensor(
                    out=sq_sb[:], in0=kv_buf[:, 0:2048], in1=kv_buf[:, 0:2048], op=ALU.mult), inc=ve)
                P.on("vector", lambda e: e.tensor_reduce(
                    out=nrm_sb[:], in_=sq_sb[:].rearrange("p (h d) -> p h d", d=128),
                    axis=AX.X, op=ALU.add), inc=ve)
                P.on("scalar", lambda e: e.activation(out=nrm_sb[:], in_=nrm_sb[:], func=AF.Ln),
                     wait=[ve], inc=sct)
                P.on("scalar", lambda e: e.activation(out=nrm_sb[:], in_=nrm_sb[:], func=AF.Exp, scale=0.5), inc=sct)
                P.on("vector", lambda e: e.tensor_scalar(
                    out=nrm_sb[:], in0=nrm_sb[:], scalar1=1e-6, scalar2=None, op0=ALU.add),
                    wait=[sct], inc=ve)
                P.on("vector", lambda e: e.reciprocal(out=nrm_sb[:], in_=nrm_sb[:]), inc=ve)
                for h in range(16):
                    P.on("vector", lambda e, h=h: e.tensor_scalar(
                        out=kq_bf[:, h * 128:(h + 1) * 128], in0=kv_buf[:, h * 128:(h + 1) * 128],
                        scalar1=nrm_sb[:, h:h + 1], scalar2=None, op0=ALU.mult),
                        wait=([(gp, gp.count)] if h == 0 else ()), inc=ve)
                P.on("vector", lambda e, st4=st4: e.tensor_copy(
                    out=vsave[:, st4, :], in_=kv_buf[:, 2048:2080]), inc=ve)
                wkn = (ve, ve.count)
                tok0 = g * 512 + st4 * 128
                P.dma("gpsimd", ktm[tok0:tok0 + 128, :], kq_bf[:], gp, wait=[wkn])

            for cbq in range(4):
                par = (g * 4 + cbq) % 2
                wqld = [(te, te.count)]
                P.dma("gpsimd", qwb[:, par, :, :],
                      qw.rearrange("(k p) c -> p k c", p=128)[:, :, cbq * 512:(cbq + 1) * 512],
                      gp, wait=wqld)
                wq = (gp, gp.count)
                for st4 in range(4):
                    ps, pi, wmm = mm_acc(
                        lambda k, st4=st4: xc_sb[:, k, st4 * 128:(st4 + 1) * 128],
                        lambda k, par=par: qwb[:, par, k, :], 16,
                        wait_extra=[wq, w_conv])
                    P.on("vector", lambda e, ps=ps: e.tensor_copy(out=qtmp[:], in_=ps[:]),
                         wait=[wmm, (gp, gp.count)], inc=ve)
                    ps_guard[pi] = (ve, ve.count)
                    P.on("vector", lambda e: e.tensor_tensor(
                        out=sq_sb[:, 0:512], in0=qtmp[:], in1=qtmp[:], op=ALU.mult), inc=ve)
                    P.on("vector", lambda e: e.tensor_reduce(
                        out=nrm_sb[:, 0:4], in_=sq_sb[:, 0:512].rearrange("p (h d) -> p h d", d=128),
                        axis=AX.X, op=ALU.add), inc=ve)
                    P.on("scalar", lambda e: e.activation(out=nrm_sb[:, 0:4], in_=nrm_sb[:, 0:4], func=AF.Ln),
                         wait=[ve], inc=sct)
                    P.on("scalar", lambda e: e.activation(out=nrm_sb[:, 0:4], in_=nrm_sb[:, 0:4], func=AF.Exp, scale=0.5), inc=sct)
                    P.on("vector", lambda e: e.tensor_scalar(
                        out=nrm_sb[:, 0:4], in0=nrm_sb[:, 0:4], scalar1=1e-6, scalar2=None, op0=ALU.add),
                        wait=[sct], inc=ve)
                    P.on("vector", lambda e: e.reciprocal(out=nrm_sb[:, 0:4], in_=nrm_sb[:, 0:4]), inc=ve)
                    for h in range(4):
                        P.on("vector", lambda e, h=h: e.tensor_scalar(
                            out=kq_bf[:, h * 128:(h + 1) * 128], in0=qtmp[:, h * 128:(h + 1) * 128],
                            scalar1=nrm_sb[:, h:h + 1], scalar2=None, op0=ALU.mult),
                            wait=([(gp, gp.count)] if h == 0 else ()), inc=ve)
                    wqn = (ve, ve.count)
                    tok0 = g * 512 + st4 * 128
                    P.dma("gpsimd", qtm[tok0:tok0 + 128, cbq * 512:(cbq + 1) * 512],
                          kq_bf[:, 0:512], gp, wait=[wqn])

            for st4 in range(4):
                ps_g = psg
                for k in range(16):
                    P.on("tensor", lambda e, k=k, st4=st4: e.matmul(
                        psg[:], xc_sb[:, k, st4 * 128:(st4 + 1) * 128], gw_sb[:, k, :],
                        start=(k == 0), stop=(k == 15)),
                        wait=([(ve, ve.count), (sct, sct.count)] if k == 0 else ()),
                        inc=(te if k == 15 else None))
                wgm = (te, te.count)
                P.on("vector", lambda e: e.tensor_tensor(out=gt[:], in0=psg[:], in1=gb_sb[:], op=ALU.add),
                     wait=[wgm, (gp, gp.count)], inc=ve)
                G = lambda i: gsc[:, i, :]
                q_ = lambda i: gt[:, i * 16:(i + 1) * 16]

                def V(fn, *, w=(), eng="vector"):
                    P.on(eng, fn, wait=w, inc=(ve if eng == "vector" else sct))

                for (src, dst) in ((0, 0), (4, 1)):
                    V(lambda e, s=src, d=dst: e.scalar_tensor_tensor(out=G(30 + d), in0=q_(s), scalar=-1.0, in1=q_(s), op0=ALU.mult, op1=ALU.max))
                    V(lambda e, d=dst: e.activation(out=G(30 + d), in_=G(30 + d), func=AF.Exp, scale=-1.0), eng="scalar", w=[ve])
                    V(lambda e, d=dst: e.activation(out=G(30 + d), in_=G(30 + d), func=AF.Ln, bias=1.0), eng="scalar")
                    V(lambda e, s=src, d=dst: e.tensor_scalar(out=G(32 + d), in0=q_(s), scalar1=0.0, scalar2=None, op0=ALU.max), w=[sct])
                    V(lambda e, d=dst: e.tensor_tensor(out=G(d), in0=G(32 + d), in1=G(30 + d), op=ALU.add))
                V(lambda e: e.scalar_tensor_tensor(out=G(34), in0=q_(5), scalar=-1.0, in1=q_(5), op0=ALU.mult, op1=ALU.max))
                V(lambda e: e.activation(out=G(34), in_=G(34), func=AF.Exp, scale=-1.0), eng="scalar", w=[ve])
                V(lambda e: e.activation(out=G(34), in_=G(34), func=AF.Ln, bias=1.0), eng="scalar")
                V(lambda e: e.tensor_scalar(out=G(3), in0=q_(5), scalar1=0.0, scalar2=None, op0=ALU.min), w=[sct])
                V(lambda e: e.tensor_tensor(out=G(2), in0=G(3), in1=G(34), op=ALU.subtract))
                V(lambda e: e.tensor_tensor(out=G(4), in0=G(1), in1=lds_sb[:], op=ALU.mult))
                V(lambda e: e.tensor_scalar(out=G(4), in0=G(4), scalar1=1e-4, scalar2=None, op0=ALU.add))
                V(lambda e: e.activation(out=G(5), in_=q_(1), func=AF.Tanh), eng="scalar", w=[ve])
                V(lambda e: e.tensor_scalar(out=G(5), in0=G(5), scalar1=1.0, scalar2=None, op0=ALU.add), w=[sct])
                V(lambda e: e.tensor_tensor(out=G(5), in0=G(5), in1=fq_sb[:], op=ALU.mult))
                V(lambda e: e.tensor_tensor(out=G(6), in0=G(4), in1=G(0), op=ALU.mult))
                V(lambda e: e.tensor_scalar(out=G(6), in0=G(6), scalar1=0.5, scalar2=None, op0=ALU.mult))
                V(lambda e: e.tensor_tensor(out=G(7), in0=G(4), in1=G(5), op=ALU.mult))
                V(lambda e: e.tensor_scalar(out=G(7), in0=G(7), scalar1=0.5, scalar2=None, op0=ALU.mult))
                V(lambda e: e.tensor_scalar(out=G(8), in0=G(6), scalar1=1.0, scalar2=None, op0=ALU.add))
                V(lambda e: e.tensor_tensor(out=G(8), in0=G(8), in1=G(8), op=ALU.mult))
                V(lambda e: e.tensor_tensor(out=G(10), in0=G(7), in1=G(7), op=ALU.mult))
                V(lambda e: e.tensor_tensor(out=G(8), in0=G(8), in1=G(10), op=ALU.add))
                V(lambda e: e.reciprocal(out=G(9), in_=G(8)))
                V(lambda e: e.tensor_tensor(out=G(11), in0=G(6), in1=G(6), op=ALU.mult))
                V(lambda e: e.tensor_tensor(out=G(11), in0=G(11), in1=G(10), op=ALU.add))
                V(lambda e: e.tensor_scalar(out=G(11), in0=G(11), scalar1=-1.0, scalar2=1.0, op0=ALU.mult, op1=ALU.add))
                V(lambda e: e.tensor_tensor(out=G(11), in0=G(11), in1=G(9), op=ALU.mult))
                V(lambda e: e.tensor_scalar(out=G(12), in0=G(7), scalar1=2.0, scalar2=None, op0=ALU.mult))
                V(lambda e: e.tensor_tensor(out=G(12), in0=G(12), in1=G(9), op=ALU.mult))
                V(lambda e: e.tensor_tensor(out=G(13), in0=G(11), in1=G(11), op=ALU.mult))
                V(lambda e: e.tensor_tensor(out=G(10), in0=G(12), in1=G(12), op=ALU.mult))
                V(lambda e: e.tensor_tensor(out=G(13), in0=G(13), in1=G(10), op=ALU.add))
                V(lambda e: e.activation(out=G(13), in_=G(13), func=AF.Ln), eng="scalar", w=[ve])
                V(lambda e: e.activation(out=G(13), in_=G(13), func=AF.Exp, scale=0.5), eng="scalar")
                V(lambda e: e.tensor_scalar(out=G(14), in0=G(13), scalar1=1e-8, scalar2=1.0, op0=ALU.max, op1=ALU.min), w=[sct])
                V(lambda e: e.activation(out=G(14), in_=G(14), func=AF.Ln), eng="scalar", w=[ve])
                V(lambda e: e.activation(out=G(15), in_=q_(6), func=AF.Sigmoid), eng="scalar")
                V(lambda e: e.tensor_tensor(out=G(16), in0=G(15), in1=G(14), op=ALU.mult), w=[sct])
                V(lambda e: e.tensor_scalar(out=G(16), in0=G(16), scalar1=8.0, scalar2=None, op0=ALU.mult))
                V(lambda e: e.activation(out=G(17), in_=G(16), func=AF.Exp), eng="scalar", w=[ve])
                V(lambda e: e.tensor_scalar(out=G(18), in0=G(11), scalar1=0.0, scalar2=None, op0=ALU.is_ge))
                V(lambda e: e.tensor_scalar(out=G(19), in0=G(18), scalar1=2.0, scalar2=-1.0, op0=ALU.mult, op1=ALU.add))
                V(lambda e: e.scalar_tensor_tensor(out=G(20), in0=G(11), scalar=-1.0, in1=G(11), op0=ALU.mult, op1=ALU.max))
                V(lambda e: e.tensor_scalar(out=G(20), in0=G(20), scalar1=1e-30, scalar2=None, op0=ALU.max))
                V(lambda e: e.reciprocal(out=G(20), in_=G(20)))
                V(lambda e: e.tensor_tensor(out=G(21), in0=G(12), in1=G(20), op=ALU.mult))
                V(lambda e: e.tensor_tensor(out=G(21), in0=G(21), in1=G(19), op=ALU.mult))
                V(lambda e: e.activation(out=G(21), in_=G(21), func=AF.Arctan), eng="scalar", w=[ve])
                V(lambda e: e.tensor_scalar(out=G(22), in0=G(18), scalar1=-PI, scalar2=PI, op0=ALU.mult, op1=ALU.add), w=[sct])
                V(lambda e: e.tensor_tensor(out=G(21), in0=G(21), in1=G(22), op=ALU.add))
                V(lambda e: e.activation(out=stage[:, 5, :], in_=q_(5), func=AF.Sigmoid), eng="scalar")
                V(lambda e: e.activation(out=G(23), in_=q_(2), func=AF.Sigmoid), eng="scalar")
                V(lambda e: e.activation(out=stage[:, 6, :], in_=q_(3), func=AF.Sigmoid), eng="scalar")
                V(lambda e: e.tensor_tensor(out=G(24), in0=G(17), in1=G(17), op=ALU.mult), w=[sct])
                V(lambda e: e.tensor_scalar(out=G(24), in0=G(24), scalar1=-1.0, scalar2=1.0, op0=ALU.mult, op1=ALU.add))
                V(lambda e: e.tensor_scalar(out=G(24), in0=G(24), scalar1=0.0, scalar2=1.0, op0=ALU.max, op1=ALU.min))
                V(lambda e: e.tensor_scalar(out=G(24), in0=G(24), scalar1=1e-6, scalar2=None, op0=ALU.add))
                V(lambda e: e.activation(out=G(24), in_=G(24), func=AF.Ln), eng="scalar", w=[ve])
                V(lambda e: e.activation(out=G(24), in_=G(24), func=AF.Exp, scale=0.5), eng="scalar")
                V(lambda e: e.tensor_tensor(out=G(24), in0=G(24), in1=G(23), op=ALU.mult), w=[sct])
                V(lambda e: e.tensor_copy(out=G(26), in_=G(16)))
                V(lambda e: e.tensor_copy(out=G(27), in_=G(21)))
                wgm2 = (ve, ve.count)
                P.on("tensor", lambda e: e.matmul(psc[:, 0:32], triu_sb[:], gsc[:, 26:28, :].rearrange("p a b -> p (a b)"), start=True, stop=True),
                     wait=[wgm2, (sct, sct.count)], inc=te)
                P.on("tensor", lambda e: e.matmul(psc[:, 32:64], tril_sb[:], gsc[:, 26:28, :].rearrange("p a b -> p (a b)"), start=True, stop=True), inc=te)
                wcs = (te, te.count)
                V(lambda e: e.tensor_copy(out=stage[:, 0, :], in_=psc[:, 0:16]), w=[wcs])
                V(lambda e: e.tensor_tensor(out=stage[:, 1, :], in0=stage[:, 0, :], in1=G(2), op=ALU.add))
                V(lambda e: e.activation(out=stage[:, 8, :], in_=stage[:, 0, :], func=AF.Exp), eng="scalar", w=[ve, wcs])
                V(lambda e: e.tensor_copy(out=stage[:, 2, :], in_=psc[:, 16:32]), w=[wcs])
                V(lambda e: e.activation(out=stage[:, 7, :], in_=psc[:, 32:48], func=AF.Exp), eng="scalar")
                V(lambda e: e.tensor_scalar(out=G(28), in0=psc[:, 16:32], scalar1=1.0 / (2 * PI), scalar2=None, op0=ALU.mult), w=[wcs])
                V(lambda e: e.tensor_scalar(out=G(29), in0=G(28), scalar1=MAGIC, scalar2=None, op0=ALU.add))
                V(lambda e: e.tensor_scalar(out=G(29), in0=G(29), scalar1=-MAGIC, scalar2=None, op0=ALU.add))
                V(lambda e: e.tensor_tensor(out=G(29), in0=G(28), in1=G(29), op=ALU.subtract))
                V(lambda e: e.tensor_scalar(out=G(29), in0=G(29), scalar1=2 * PI, scalar2=None, op0=ALU.mult))
                V(lambda e: e.activation(out=stage[:, 4, :], in_=G(29), func=AF.Sin), eng="scalar", w=[ve])
                V(lambda e: e.tensor_scalar(out=G(28), in0=G(28), scalar1=0.25, scalar2=None, op0=ALU.add))
                V(lambda e: e.tensor_scalar(out=G(25), in0=G(28), scalar1=MAGIC, scalar2=None, op0=ALU.add))
                V(lambda e: e.tensor_scalar(out=G(25), in0=G(25), scalar1=-MAGIC, scalar2=None, op0=ALU.add))
                V(lambda e: e.tensor_tensor(out=G(25), in0=G(28), in1=G(25), op=ALU.subtract))
                V(lambda e: e.tensor_scalar(out=G(25), in0=G(25), scalar1=2 * PI, scalar2=None, op0=ALU.mult))
                V(lambda e: e.activation(out=stage[:, 3, :], in_=G(25), func=AF.Sin), eng="scalar", w=[ve])
                V(lambda e, st4=st4: e.tensor_tensor(
                    out=G(30), in0=vsave[:, st4, :].rearrange("p (h s) -> p h s", s=2)[:, :, 0],
                    in1=G(24), op=ALU.mult), w=[sct])
                V(lambda e, st4=st4: e.tensor_tensor(
                    out=G(31), in0=vsave[:, st4, :].rearrange("p (h s) -> p h s", s=2)[:, :, 1],
                    in1=G(24), op=ALU.mult))
                V(lambda e: e.tensor_tensor(out=G(32), in0=G(30), in1=stage[:, 3, :], op=ALU.mult))
                V(lambda e: e.tensor_tensor(out=G(33), in0=G(31), in1=stage[:, 4, :], op=ALU.mult))
                V(lambda e: e.tensor_tensor(out=G(32), in0=G(32), in1=G(33), op=ALU.subtract))
                V(lambda e: e.tensor_tensor(out=G(33), in0=G(30), in1=stage[:, 4, :], op=ALU.mult))
                V(lambda e: e.tensor_tensor(out=G(34), in0=G(31), in1=stage[:, 3, :], op=ALU.mult))
                V(lambda e: e.tensor_tensor(out=G(33), in0=G(33), in1=G(34), op=ALU.add))
                V(lambda e: e.tensor_copy(out=vt_sb[:].rearrange("p (h s) -> p h s", s=2)[:, :, 0], in_=G(32)))
                V(lambda e: e.tensor_copy(out=vt_sb[:].rearrange("p (h s) -> p h s", s=2)[:, :, 1], in_=G(33)))
                wvt = (ve, ve.count)
                tok0 = g * 512 + st4 * 128
                P.dma("gpsimd", vt_o[tok0:tok0 + 128, :], vt_sb[:], gp, wait=[wvt])
                P.dma("gpsimd", gcols[tok0:tok0 + 128, :, :], stage[:], gp, wait=[wvt, (sct, sct.count)])
                P.on("tensor", lambda e: e.transpose(pst[:], stage[:, 0:3, :].rearrange("p a b -> p (a b)"), eye_sb[:]),
                     wait=[(ve, ve.count), (sct, sct.count)], inc=te)
                P.on("scalar", lambda e: e.activation(out=ghm_sb[:], in_=pst[:], func=AF.Identity),
                     wait=[te, (gp, gp.count)], inc=sct)
                wgh = (sct, sct.count)
                bq = tok0 // L
                lt = tok0 % L
                P.dma("gpsimd", ghm[bq, :, :, lt:lt + 128], ghm_sb[:], gp, wait=[wgh])

        P.on("gpsimd", lambda e: e.engine_nop(), wait=[(gp, gp.count)], inc=None)
        P.play(block)
    return nc


def run_l0_front(per_core_x, wts, consts):
    nc = build_l0_front()
    inp = {}
    w = wts
    inp["w_zxg"] = _f32(w["L0_attn_in_proj_w"][:, 0:4096])
    inp["w_kv"] = _f32(w["L0_attn_in_proj_w"][:, 4096:6176])
    inp["b_zxg"] = _f32(w["L0_attn_in_proj_b"][0:4096].reshape(32, 128).T)
    inp["b_kv"] = _f32(np.broadcast_to(w["L0_attn_in_proj_b"][4096:6176], (128, 2080)))
    inp["rmsw"] = _f32(w["L0_attn_rms_w"].reshape(8, 128).T)
    cw = np.asarray(w["L0_attn_conv_w"], np.float32)
    inp["convw"] = _f32(cw.reshape(16, 128, 4).transpose(1, 2, 0).reshape(128, 64))
    inp["convb"] = _f32(w["L0_attn_conv_b"].reshape(16, 128).T)
    inp["gatew"] = _f32(w["L0_attn_gate_w"])
    inp["gateb"] = _f32(np.broadcast_to(w["L0_attn_gate_b"], (128, 112)))
    inp["ldsb"] = _f32(np.broadcast_to(w["L0_attn_log_dt_scale"], (128, 16)))
    inp["freqs"] = consts["freqs"]
    inp["qw"] = _f32(w["L0_attn_Q_w"])
    inp["eye32"] = consts["eye_f32"]
    inp["triu_i"] = consts["triu_incl"]
    inp["tril_s"] = consts["tril_strict"]
    in_maps = [dict(inp, x=_f32(xc)) for xc in per_core_x]
    res = run_bass_kernel_spmd(nc, in_maps, core_ids=list(range(NCORE)))
    return res.results



def build_l0_core():
    nc = bass.Bass("TRN2", target_bir_lowering=False, debug=False)
    ktm = nc.dram_tensor("ktm", [T, 2048], BF16, kind="ExternalInput").ap()
    qtm = nc.dram_tensor("qtm", [T, 2048], BF16, kind="ExternalInput").ap()
    vt_i = nc.dram_tensor("vt_i", [T, 32], F32, kind="ExternalInput").ap()
    gcols = nc.dram_tensor("gcols", [T, 9, 16], F32, kind="ExternalInput").ap()
    ghm = nc.dram_tensor("ghm", [BQ, 3, 16, L], F32, kind="ExternalInput").ap()
    ones_i = nc.dram_tensor("ones_i", [1, 128], F32, kind="ExternalInput").ap()
    msku_n_i = nc.dram_tensor("msku_n_i", [128, 128], BF16, kind="ExternalInput").ap()
    mskl_n_i = nc.dram_tensor("mskl_n_i", [128, 128], BF16, kind="ExternalInput").ap()
    msku_i_i = nc.dram_tensor("msku_i_i", [128, 128], BF16, kind="ExternalInput").ap()
    retr_o = nc.dram_tensor("retr_o", [T, 32], F32, kind="ExternalOutput").ap()

    with contextlib.ExitStack() as _es:
        S = lambda n, sh, dt: _es.enter_context(nc.sbuf_tensor(n, sh, dt))
        PSm = lambda n, sh: _es.enter_context(nc.psum_tensor(n, sh, F32))
        ones_sb = S("ones_sb", [1, 128], F32)
        msku_n = S("msku_n", [128, 128], BF16)
        mskl_n = S("mskl_n", [128, 128], BF16)
        msku_i = S("msku_i", [128, 128], BF16)
        kfm = S("kfm", [128, 16, 128], BF16)
        qfm = S("qfm", [128, 16, 128], BF16)
        ktm_sb = S("ktm_sb", [128, 2048], BF16)
        vt_sb = S("vt_sb", [128, 32], F32)
        gc_sb = S("gc_sb", [128, 9, 16], F32)
        rows_sb = S("rows_sb", [1, 6144], F32)
        scl_row = S("scl_row", [1, 64], F32)
        trg_row = S("trg_row", [1, 48], F32)
        scl_cols = S("scl_cols", [128, 64], F32)
        e_sb = S("e_sb", [128, 3, 128], BF16)
        epre = S("epre", [128, 128], F32)
        t_sb = S("t_sb", [128, 128], BF16)
        ptb = S("ptb", [128, 16, 7, 128], BF16)
        pb = S("pb", [128, 128], BF16)
        m2b = S("m2b", [128, 16, 128], BF16)
        wfp = S("wfp", [128, 2, 32], F32)
        wt1 = S("wt1", [128, 32], F32)
        wt2 = S("wt2", [128, 32], F32)
        scl2 = S("scl2", [128, 2, 32], F32)
        wbb = S("wbb", [128, 32], BF16)
        udb = S("udb", [128, 32], BF16)
        retr_sb = S("retr_sb", [128, 32], F32)
        zf = S("zf", [128, 2, 32], F32)
        zb = S("zb", [128, 2, 32], BF16)
        tmpA = S("tmpA", [128, 2], F32)
        tmpB = S("tmpB", [128, 2], F32)
        tmpC = S("tmpC", [128, 2], F32)
        negs = S("negs", [128, 16], F32)
        psE = PSm("psE", [128, 256])
        psA = PSm("psA", [128, 128])
        psB = PSm("psB", [128, 128])
        psP2 = PSm("psP2", [128, 256])
        pss = PSm("pss", [128, 256])
        s_ld = _es.enter_context(nc.semaphore("s_ld"))
        s_gp = _es.enter_context(nc.semaphore("s_gp"))
        s_te = _es.enter_context(nc.semaphore("s_te"))
        s_ve = _es.enter_context(nc.semaphore("s_ve"))
        s_sc = _es.enter_context(nc.semaphore("s_sc"))
        block = _es.enter_context(nc.Block())

        ld = Trk(s_ld); gp = Trk(s_gp); te = Trk(s_te); ve = Trk(s_ve); sct = Trk(s_sc)
        P = Prog(nc, self_trks={"vector": ve, "scalar": sct})

        for ap_, t_ in ((ones_i, ones_sb), (msku_n_i, msku_n), (mskl_n_i, mskl_n), (msku_i_i, msku_i)):
            P.dma("sync", t_[:], ap_[:], ld)
        w_consts = (ld, ld.count)

        P.on("vector", lambda e: e.memset(zf[:], 0.0))
        P.on("vector", lambda e: e.memset(zb[:], 0.0))

        for c in range(NCHS):
            for b in range(BQ):
                tok0 = b * L + c * CH
                lt = c * CH
                if c == 0 and b > 0:
                    pass
                wpre = [(te, te.count), (ve, ve.count), (sct, sct.count), (gp, gp.count)]
                for h in range(16):
                    P.dma("sync", kfm[:, h, :], ktm[tok0:tok0 + 128, h * 128:(h + 1) * 128], ld,
                          wait=(wpre if h == 0 else ()), transpose=True)
                for h in range(16):
                    P.dma("sync", qfm[:, h, :], qtm[tok0:tok0 + 128, h * 128:(h + 1) * 128], ld,
                          transpose=True)
                P.dma("sync", ktm_sb[:], ktm[tok0:tok0 + 128, :], ld)
                P.dma("sync", vt_sb[:], vt_i[tok0:tok0 + 128, :], ld)
                P.dma("sync", gc_sb[:], gcols[tok0:tok0 + 128, :, :], ld)
                P.dma("sync", rows_sb[0:1, :].rearrange("o (q h t) -> o q h t", q=3, h=16),
                      ghm[b:b + 1, :, :, lt:lt + 128], ld)
                w_ld = (ld, ld.count)

                GC = rows_sb[0:1, 127:2048:128]
                THC = rows_sb[0:1, 4096 + 127:6144:128]
                P.on("scalar", lambda e, GC=GC: e.activation(out=scl_row[0:1, 0:16], in_=GC, func=AF.Exp), wait=[w_ld])
                P.on("vector", lambda e, THC=THC: e.tensor_scalar(out=trg_row[0:1, 0:16], in0=THC, scalar1=1.0 / (2 * PI), scalar2=None, op0=ALU.mult), wait=[w_ld])
                for (dsti, add25) in ((32, False), (16, True)):
                    if add25:
                        P.on("vector", lambda e: e.tensor_scalar(out=trg_row[0:1, 0:16], in0=trg_row[0:1, 0:16], scalar1=0.25, scalar2=None, op0=ALU.add), wait=[sct])
                    P.on("vector", lambda e: e.tensor_scalar(out=trg_row[0:1, 16:32], in0=trg_row[0:1, 0:16], scalar1=MAGIC, scalar2=None, op0=ALU.add), wait=[sct])
                    P.on("vector", lambda e: e.tensor_scalar(out=trg_row[0:1, 16:32], in0=trg_row[0:1, 16:32], scalar1=-MAGIC, scalar2=None, op0=ALU.add))
                    P.on("vector", lambda e: e.tensor_tensor(out=trg_row[0:1, 16:32], in0=trg_row[0:1, 0:16], in1=trg_row[0:1, 16:32], op=ALU.subtract))
                    P.on("vector", lambda e: e.tensor_scalar(out=trg_row[0:1, 16:32], in0=trg_row[0:1, 16:32], scalar1=2 * PI, scalar2=None, op0=ALU.mult))
                    P.on("scalar", lambda e, dsti=dsti: e.activation(out=scl_row[0:1, dsti:dsti + 16], in_=trg_row[0:1, 16:32], func=AF.Sin), wait=[ve])
                P.on("vector", lambda e: e.tensor_scalar(out=scl_row[0:1, 48:64], in0=scl_row[0:1, 32:48], scalar1=-1.0, scalar2=None, op0=ALU.mult), wait=[sct])
                P.on("tensor", lambda e: e.matmul(pss[:, 160:224], ones_sb[:], scl_row[:], start=True, stop=True),
                     wait=[(ve, ve.count), (sct, sct.count), w_consts], inc=te)
                P.on("vector", lambda e: e.tensor_copy(out=scl_cols[:], in_=pss[:, 160:224]), wait=[te])
                P.on("vector", lambda e: e.tensor_scalar(out=negs[:], in0=gc_sb[:, 4, :], scalar1=-1.0, scalar2=None, op0=ALU.mult))
                w_scl = (ve, ve.count)

                if c == 0:
                    pass

                for h in range(16):
                    Gcol = gc_sb[:, 0, h:h + 1]
                    Gbcol = gc_sb[:, 1, h:h + 1]
                    coscol = gc_sb[:, 3, h:h + 1]
                    sincol = gc_sb[:, 4, h:h + 1]
                    betacol = gc_sb[:, 5, h:h + 1]
                    selcol = gc_sb[:, 6, h:h + 1]
                    mdeccol = gc_sb[:, 7, h:h + 1]
                    mcol = gc_sb[:, 8, h:h + 1]
                    growG = rows_sb[0:1, h * 128:(h + 1) * 128]
                    growGb = rows_sb[0:1, 2048 + h * 128:2048 + (h + 1) * 128]
                    P.on("tensor", lambda e, r=growGb: e.matmul(psE[:, 0:128], ones_sb[:], r, start=True, stop=True),
                         wait=[(ve, ve.count), (sct, sct.count)], inc=te)
                    P.on("tensor", lambda e, r=growG: e.matmul(psE[:, 128:256], ones_sb[:], r, start=True, stop=True), inc=te)
                    wE = (te, te.count)
                    P.on("vector", lambda e, Gcol=Gcol: e.tensor_scalar(out=epre[:], in0=psE[:, 0:128], scalar1=Gcol, scalar2=0.0, op0=ALU.subtract, op1=ALU.min), wait=[wE])
                    P.on("scalar", lambda e: e.activation(out=e_sb[:, 0, :], in_=epre[:], func=AF.Exp), wait=[ve])
                    P.on("vector", lambda e, Gbcol=Gbcol: e.tensor_scalar(out=epre[:], in0=psE[:, 128:256], scalar1=Gbcol, scalar2=0.0, op0=ALU.subtract, op1=ALU.max), wait=[sct])
                    P.on("scalar", lambda e: e.activation(out=e_sb[:, 1, :], in_=epre[:], func=AF.Exp, scale=-1.0), wait=[ve])
                    P.on("vector", lambda e, Gcol=Gcol: e.tensor_scalar(out=epre[:], in0=psE[:, 128:256], scalar1=Gcol, scalar2=0.0, op0=ALU.subtract, op1=ALU.min), wait=[sct])
                    P.on("scalar", lambda e: e.activation(out=e_sb[:, 2, :], in_=epre[:], func=AF.Exp), wait=[ve])
                    wEe = (sct, sct.count)
                    P.on("tensor", lambda e, h=h: e.matmul(psA[:], kfm[:, h, :], kfm[:, h, :], start=True, stop=True),
                         wait=[w_ld, (ve, ve.count)], inc=te)
                    P.on("tensor", lambda e, h=h: e.matmul(psB[:], kfm[:, h, :], qfm[:, h, :], start=True, stop=True), inc=te)
                    wG = (te, te.count)
                    P.on("vector", lambda e: e.tensor_tensor(out=t_sb[:], in0=psA[:], in1=e_sb[:, 0, :], op=ALU.mult), wait=[wG, wEe])
                    P.on("vector", lambda e, h=h: e.tensor_tensor(out=ptb[:, h, 0, :], in0=t_sb[:], in1=msku_n[:], op=ALU.mult))
                    P.on("vector", lambda e: e.tensor_tensor(out=t_sb[:], in0=psA[:], in1=e_sb[:, 1, :], op=ALU.mult))
                    P.on("vector", lambda e: e.tensor_tensor(out=pb[:], in0=t_sb[:], in1=mskl_n[:], op=ALU.mult))
                    P.on("vector", lambda e: e.tensor_tensor(out=t_sb[:], in0=psB[:], in1=e_sb[:, 2, :], op=ALU.mult))
                    P.on("vector", lambda e, h=h: e.tensor_tensor(out=m2b[:, h, :], in0=t_sb[:], in1=msku_i[:], op=ALU.mult))
                    wM = (ve, ve.count)
                    for k in range(1, 7):
                        P.on("tensor", lambda e, k=k, h=h: e.matmul(psP2[:, 0:128], ptb[:, h, k - 1, :], pb[:], start=True, stop=True),
                             wait=[wM if k == 1 else (ve, ve.count)], inc=te)
                        P.on("tensor", lambda e, k=k, h=h: e.matmul(psP2[:, 128:256], pb[:], ptb[:, h, k - 1, :], start=True, stop=True), inc=te)
                        P.on("vector", lambda e: e.tensor_copy(out=pb[:], in_=psP2[:, 0:128]), wait=[te])
                        P.on("vector", lambda e, k=k, h=h: e.tensor_copy(out=ptb[:, h, k, :], in_=psP2[:, 128:256]))
                    wSq = (ve, ve.count)
                w_allh = (ve, ve.count)
                for h in range(16):
                    P.on("tensor", lambda e, h=h, b=b: e.matmul(pss[:, 2 * h:2 * h + 2], kfm[:, h, :], zb[:, b, 2 * h:2 * h + 2], start=True, stop=True),
                         wait=([w_allh] if h == 0 else ()), inc=(te if h == 15 else None))
                    P.on("tensor", lambda e, h=h, b=b: e.matmul(pss[:, 32 + 2 * h:32 + 2 * h + 2], qfm[:, h, :], zb[:, b, 2 * h:2 * h + 2], start=True, stop=True), inc=(te if h == 15 else None))
                wZK = (te, te.count)
                mpair = gc_sb[:, 8, :].rearrange("p (h o) -> p h o", o=1)
                bpair = gc_sb[:, 5, :].rearrange("p (h o) -> p h o", o=1)
                mdpair = gc_sb[:, 7, :].rearrange("p (h o) -> p h o", o=1)
                def bcmul(out2, in2, colpair, w=()):
                    a2, b2_ = _bc(in2.rearrange("p (h s) -> p h s", s=2), colpair)
                    P.on("vector", lambda e, a2=a2, b2_=b2_, out2=out2: e.tensor_tensor(
                        out=out2.rearrange("p (h s) -> p h s", s=2), in0=a2, in1=b2_, op=ALU.mult), wait=list(w))
                bcmul(wt1[:], pss[:, 0:32], mpair, w=[wZK])
                P.on("vector", lambda e: e.tensor_tensor(out=wt2[:], in0=vt_sb[:], in1=wt1[:], op=ALU.subtract))
                bcmul(wfp[:, 0, :], wt2[:], bpair)
                P.on("vector", lambda e: e.tensor_copy(out=wbb[:], in_=wfp[:, 0, :]))
                for k in range(7):
                    for h in range(16):
                        P.on("tensor", lambda e, k=k, h=h: e.matmul(pss[:, 64 + 2 * h:64 + 2 * h + 2], ptb[:, h, k, :], wbb[:, 2 * h:2 * h + 2], start=True, stop=True),
                             wait=([(ve, ve.count)] if h == 0 else ()), inc=(te if h == 15 else None))
                    P.on("vector", lambda e, k=k: e.tensor_tensor(out=wfp[:, (k + 1) % 2, :], in0=pss[:, 64:96], in1=wfp[:, k % 2, :], op=ALU.add), wait=[te])
                    P.on("vector", lambda e: e.tensor_copy(out=wbb[:], in_=wfp[:, (k + 1) % 2, :]))
                wW = (ve, ve.count)
                ufin = wfp[:, 1, :]
                for h in range(16):
                    P.on("tensor", lambda e, h=h: e.matmul(pss[:, 96 + 2 * h:96 + 2 * h + 2], m2b[:, h, :], wbb[:, 2 * h:2 * h + 2], start=True, stop=True),
                         wait=([wW] if h == 0 else ()), inc=(te if h == 15 else None))
                wR = (te, te.count)
                bcmul(wt1[:], pss[:, 32:64], mpair, w=[wR])
                P.on("vector", lambda e: e.tensor_tensor(out=wt2[:], in0=pss[:, 96:128], in1=wt1[:], op=ALU.add))
                rr = wt2[:].rearrange("p (h s) -> p h s", s=2)[:, :, 0]
                ri = wt2[:].rearrange("p (h s) -> p h s", s=2)[:, :, 1]
                cosT16 = gc_sb[:, 3, :]
                sinT16 = gc_sb[:, 4, :]
                sel16 = gc_sb[:, 6, :]
                re_e = retr_sb[:].rearrange("p (h s) -> p h s", s=2)[:, :, 0]
                re_o = retr_sb[:].rearrange("p (h s) -> p h s", s=2)[:, :, 1]
                P.on("vector", lambda e, rr=rr: e.tensor_tensor(out=wt1[:, 0:16], in0=rr, in1=cosT16, op=ALU.mult))
                P.on("vector", lambda e, ri=ri: e.tensor_tensor(out=wt1[:, 16:32], in0=ri, in1=sinT16, op=ALU.mult))
                P.on("vector", lambda e: e.tensor_tensor(out=wt1[:, 0:16], in0=wt1[:, 0:16], in1=wt1[:, 16:32], op=ALU.add))
                P.on("vector", lambda e, re_e=re_e: e.tensor_tensor(out=re_e, in0=wt1[:, 0:16], in1=sel16, op=ALU.mult))
                P.on("vector", lambda e, ri=ri: e.tensor_tensor(out=wt1[:, 0:16], in0=ri, in1=cosT16, op=ALU.mult))
                P.on("vector", lambda e, rr=rr: e.tensor_tensor(out=wt1[:, 16:32], in0=rr, in1=sinT16, op=ALU.mult))
                P.on("vector", lambda e: e.tensor_tensor(out=wt1[:, 0:16], in0=wt1[:, 0:16], in1=wt1[:, 16:32], op=ALU.subtract))
                P.on("vector", lambda e, re_o=re_o: e.tensor_tensor(out=re_o, in0=wt1[:, 0:16], in1=sel16, op=ALU.mult))
                bcmul(wt1[:], ufin, mdpair)
                P.on("vector", lambda e: e.tensor_copy(out=udb[:], in_=wt1[:]))
                for h in range(16):
                    P.on("tensor", lambda e, h=h: e.matmul(pss[:, 128 + 2 * h:128 + 2 * h + 2], ktm_sb[:, h * 128:(h + 1) * 128], udb[:, 2 * h:2 * h + 2], start=True, stop=True),
                         wait=([(ve, ve.count)] if h == 0 else ()), inc=(te if h == 15 else None))
                wDZ = (te, te.count)
                for (si, c0) in ((0, 0), (1, 16), (2, 32)):
                    ap2 = scl_cols[:, c0:c0 + 16].rearrange("p (h o) -> p h o", o=1)
                    dst = [wt1, scl2[:, 0, :], scl2[:, 1, :]][si]
                    a2, b2_ = _bc(wt2[:].rearrange("p (h s) -> p h s", s=2), ap2)
                    P.on("vector", lambda e, b2_=b2_, dst=dst: e.scalar_tensor_tensor(
                        out=dst.rearrange("p (h s) -> p h s", s=2) if dst.shape[-1] == 32 else dst,
                        in0=b2_, scalar=1.0, in1=b2_, op0=ALU.bypass, op1=ALU.max),
                        wait=([wDZ, w_scl] if si == 0 else ()))
                P.on("vector", lambda e, b=b: e.tensor_tensor(out=wt2[:], in0=zf[:, b, :], in1=wt1[:], op=ALU.mult))
                P.on("vector", lambda e: e.tensor_tensor(out=wt2[:], in0=wt2[:], in1=pss[:, 128:160], op=ALU.add))
                tr_e = wt2[:].rearrange("p (h s) -> p h s", s=2)[:, :, 0]
                tr_o = wt2[:].rearrange("p (h s) -> p h s", s=2)[:, :, 1]
                zf_e = zf[:, b, :].rearrange("p (h s) -> p h s", s=2)[:, :, 0]
                zf_o = zf[:, b, :].rearrange("p (h s) -> p h s", s=2)[:, :, 1]
                cC16 = scl2[:, 0, :].rearrange("p (h s) -> p h s", s=2)[:, :, 0]
                sC16 = scl2[:, 1, :].rearrange("p (h s) -> p h s", s=2)[:, :, 0]
                P.on("vector", lambda e, tr_e=tr_e, cC16=cC16: e.tensor_tensor(out=wt1[:, 0:16], in0=tr_e, in1=cC16, op=ALU.mult))
                P.on("vector", lambda e, tr_o=tr_o, sC16=sC16: e.tensor_tensor(out=wt1[:, 16:32], in0=tr_o, in1=sC16, op=ALU.mult))
                P.on("vector", lambda e, zf_e=zf_e: e.tensor_tensor(out=zf_e, in0=wt1[:, 0:16], in1=wt1[:, 16:32], op=ALU.add))
                P.on("vector", lambda e, tr_o=tr_o, cC16=cC16: e.tensor_tensor(out=wt1[:, 0:16], in0=tr_o, in1=cC16, op=ALU.mult))
                P.on("vector", lambda e, tr_e=tr_e, sC16=sC16: e.tensor_tensor(out=wt1[:, 16:32], in0=tr_e, in1=sC16, op=ALU.mult))
                P.on("vector", lambda e, zf_o=zf_o: e.tensor_tensor(out=zf_o, in0=wt1[:, 0:16], in1=wt1[:, 16:32], op=ALU.subtract))
                P.on("vector", lambda e, b=b: e.tensor_copy(out=zb[:, b, :], in_=zf[:, b, :]))
                wRT = (ve, ve.count)
                P.dma("gpsimd", retr_o[tok0:tok0 + 128, :], retr_sb[:], gp, wait=[wRT])

        P.on("gpsimd", lambda e: e.engine_nop(), wait=[(gp, gp.count)], inc=None)
        P.play(block)
    return nc


def run_l0_core(front_results, consts):
    nc = build_l0_core()
    base = {
        "ones_i": _f32(np.ones((1, 128))),
        "msku_n_i": consts["mask_u_strict_neg"],
        "mskl_n_i": consts["mask_l_strict_neg"],
        "msku_i_i": consts["mask_u_incl"],
    }
    in_maps = []
    for r in front_results:
        m = dict(base)
        m["ktm"] = r["ktm"]; m["qtm"] = r["qtm"]; m["vt_i"] = r["vt_o"]
        m["gcols"] = r["gcols"]; m["ghm"] = r["ghm"]
        in_maps.append(m)
    res = run_bass_kernel_spmd(nc, in_maps, core_ids=list(range(NCORE)))
    return res.results


def _bc(ap3, col3):
    return bass.broadcast_tensor_aps(ap3, col3)



def build_l0_comb():
    nc = bass.Bass("TRN2", target_bir_lowering=False, debug=False)
    retr = nc.dram_tensor("retr", [T, 32], F32, kind="ExternalInput").ap()
    z_fm = nc.dram_tensor("z_fm", [16, 128, T], BF16, kind="ExternalInput").ap()
    xc_fm = nc.dram_tensor("xc_fm", [16, 128, T], BF16, kind="ExternalInput").ap()
    xn_tm = nc.dram_tensor("xn_tm", [T, DM], BF16, kind="ExternalInput").ap()
    wr = nc.dram_tensor("wr", [32, 2048], F32, kind="ExternalInput").ap()
    wo = nc.dram_tensor("wo", [2048, 1024], F32, kind="ExternalInput").ap()
    gnw = nc.dram_tensor("gnw", [128, 16], F32, kind="ExternalInput").ap()
    gnb = nc.dram_tensor("gnb", [128, 16], F32, kind="ExternalInput").ap()
    dpar = nc.dram_tensor("dpar", [128, 16], F32, kind="ExternalInput").ap()
    eye32 = nc.dram_tensor("eye32", [128, 128], F32, kind="ExternalInput").ap()
    ones1 = nc.dram_tensor("ones1", [1, 128], F32, kind="ExternalInput").ap()
    onecol = nc.dram_tensor("onecol", [128, 1], F32, kind="ExternalInput").ap()
    ones32 = nc.dram_tensor("ones32", [32, 1], F32, kind="ExternalInput").ap()
    s_fm = nc.dram_tensor("s_fm", [8, 128, T], BF16, kind="ExternalOutput").ap()

    with contextlib.ExitStack() as _es:
        S = lambda n, sh, dt: _es.enter_context(nc.sbuf_tensor(n, sh, dt))
        wr_sb = S("wr_sb", [32, 2048], F32)
        wo_sb = S("wo_sb", [128, 16, 1024], BF16)
        gnw_sb = S("gnw_sb", [128, 16], F32)
        gnb_sb = S("gnb_sb", [128, 16], F32)
        dp_sb = S("dp_sb", [128, 16], F32)
        eye_sb = S("eye_sb", [128, 128], F32)
        on1_sb = S("on1_sb", [1, 128], F32)
        onc_sb = S("onc_sb", [128, 1], F32)
        on32_sb = S("on32_sb", [32, 1], F32)
        rt_sb = S("rt_sb", [128, 2, 32], F32)
        rT_sb = S("rT_sb", [32, T], F32)
        G_sb = S("G_sb", [32, 2, 32], F32)
        cs_sb = S("cs_sb", [32, 2, 1], F32)
        GW_sb = S("GW_sb", [32, 2048], F32)
        prod_sb = S("prod_sb", [32, 2048], F32)
        srow = S("srow", [1, 4096], F32)
        trow = S("trow", [1, 64], F32)
        zbuf = S("zbuf", [128, 2, 512], BF16)
        xcb = S("xcb", [128, 2, 512], BF16)
        zs_sb = S("zs_sb", [128, 512], BF16)
        t32a = S("t32a", [128, 512], F32)
        gnc_sb = S("gnc_sb", [128, 64], F32)
        spre = S("spre", [128, 16, 512], BF16)
        xnf = S("xnf", [128, 8, 512], BF16)
        sout = S("sout", [128, 2, 512], BF16)
        ps_t = _es.enter_context(nc.psum_tensor("ps_t", [32, 128], F32))
        ps_g = _es.enter_context(nc.psum_tensor("ps_g", [32, 128], F32))
        ps_gw = _es.enter_context(nc.psum_tensor("ps_gw", [32, 512], F32))
        ps_row = _es.enter_context(nc.psum_tensor("ps_row", [1, 512], F32))
        ps_bc = _es.enter_context(nc.psum_tensor("ps_bc", [128, 64], F32))
        big0 = _es.enter_context(nc.psum_tensor("big0", [128, 512], F32))
        big1 = _es.enter_context(nc.psum_tensor("big1", [128, 512], F32))
        s_ld = _es.enter_context(nc.semaphore("s_ld"))
        s_gp = _es.enter_context(nc.semaphore("s_gp"))
        s_te = _es.enter_context(nc.semaphore("s_te"))
        s_ve = _es.enter_context(nc.semaphore("s_ve"))
        s_sc = _es.enter_context(nc.semaphore("s_sc"))
        block = _es.enter_context(nc.Block())
        ld = Trk(s_ld); gp = Trk(s_gp); te = Trk(s_te); ve = Trk(s_ve); sct = Trk(s_sc)
        P = Prog(nc, self_trks={"vector": ve, "scalar": sct})

        for ap_, t_ in ((wr, wr_sb), (gnw, gnw_sb), (gnb, gnb_sb), (dpar, dp_sb),
                        (eye32, eye_sb), (ones1, on1_sb), (onecol, onc_sb), (ones32, on32_sb)):
            P.dma("sync", t_[:], ap_[:], ld)
        for k in range(16):
            P.dma("gpsimd", wo_sb[:, k, :], wo[k * 128:(k + 1) * 128, :], gp)
        w_c = (ld, ld.count)
        w_w = (gp, gp.count)

        for i in range(32):
            b = i // 16
            par = i % 2
            P.dma("sync", rt_sb[:, par, :], retr[i * 128:(i + 1) * 128, :], ld,
                  wait=[(te, te.count)])
            w_rt = (ld, ld.count)
            P.on("tensor", lambda e, par=par: e.transpose(ps_t[:], rt_sb[:, par, :], eye_sb[:]),
                 wait=[w_rt, w_c, (sct, sct.count)], inc=te)
            P.on("scalar", lambda e, i=i: e.activation(out=rT_sb[:, i * 128:(i + 1) * 128], in_=ps_t[0:32, :], func=AF.Identity),
                 wait=[te], inc=sct)
            P.on("tensor", lambda e, par=par, b=b, i=i: e.matmul(
                ps_g[:, b * 32:(b + 1) * 32], rt_sb[:, par, :], rt_sb[:, par, :],
                start=(i % 16 == 0), stop=(i % 16 == 15), skip_group_check=True), inc=te)
            P.on("tensor", lambda e, par=par, b=b, i=i: e.matmul(
                ps_g[:, 64 + b:65 + b], rt_sb[:, par, :], onc_sb[:],
                start=(i % 16 == 0), stop=(i % 16 == 15), skip_group_check=True), inc=te)
        w_gram = (te, te.count)
        for b in range(BQ):
            P.on("vector", lambda e, b=b: e.tensor_copy(out=G_sb[:, b, :], in_=ps_g[:, b * 32:(b + 1) * 32]), wait=[w_gram])
            P.on("vector", lambda e, b=b: e.tensor_copy(out=cs_sb[:, b, :], in_=ps_g[:, 64 + b:65 + b]))

        for b in range(BQ):
            for j in range(4):
                P.on("tensor", lambda e, b=b, j=j: e.matmul(
                    ps_gw[:], G_sb[:, b, :], wr_sb[:, j * 512:(j + 1) * 512], start=True, stop=True),
                    wait=[(ve, ve.count)], inc=te)
                P.on("vector", lambda e, j=j: e.tensor_copy(out=GW_sb[:, j * 512:(j + 1) * 512], in_=ps_gw[:]), wait=[te])
            P.on("vector", lambda e: e.tensor_tensor(out=prod_sb[:], in0=GW_sb[:], in1=wr_sb[:], op=ALU.mult))
            for j in range(4):
                P.on("tensor", lambda e, j=j: e.matmul(
                    ps_row[:], on32_sb[:], prod_sb[:, j * 512:(j + 1) * 512], start=True, stop=True),
                    wait=[(ve, ve.count)], inc=te)
                P.on("vector", lambda e, j=j: e.tensor_copy(out=srow[0:1, j * 512:(j + 1) * 512], in_=ps_row[:]), wait=[te])
            for j in range(4):
                P.on("tensor", lambda e, b=b, j=j: e.matmul(
                    ps_row[:], cs_sb[:, b, :], wr_sb[:, j * 512:(j + 1) * 512], start=True, stop=True),
                    wait=[(ve, ve.count)], inc=te)
                P.on("vector", lambda e, j=j: e.tensor_copy(out=srow[0:1, 2048 + j * 512:2048 + (j + 1) * 512], in_=ps_row[:]), wait=[te])
            NRM = 1.0 / (L * 128.0)
            P.on("vector", lambda e: e.tensor_reduce(out=trow[0:1, 0:16], in_=srow[0:1, 0:2048].rearrange("o (g d) -> o g d", d=128), axis=AX.X, op=ALU.add))
            P.on("vector", lambda e: e.tensor_reduce(out=trow[0:1, 16:32], in_=srow[0:1, 2048:4096].rearrange("o (g d) -> o g d", d=128), axis=AX.X, op=ALU.add))
            P.on("vector", lambda e, NRM=NRM: e.tensor_scalar(out=trow[0:1, 0:16], in0=trow[0:1, 0:16], scalar1=NRM, scalar2=None, op0=ALU.mult))
            P.on("vector", lambda e, NRM=NRM: e.tensor_scalar(out=trow[0:1, 16:32], in0=trow[0:1, 16:32], scalar1=NRM, scalar2=None, op0=ALU.mult))
            P.on("vector", lambda e: e.tensor_tensor(out=trow[0:1, 32:48], in0=trow[0:1, 16:32], in1=trow[0:1, 16:32], op=ALU.mult))
            P.on("vector", lambda e: e.tensor_tensor(out=trow[0:1, 0:16], in0=trow[0:1, 0:16], in1=trow[0:1, 32:48], op=ALU.subtract))
            P.on("vector", lambda e: e.tensor_scalar(out=trow[0:1, 0:16], in0=trow[0:1, 0:16], scalar1=1e-5, scalar2=None, op0=ALU.add))
            P.on("scalar", lambda e: e.activation(out=trow[0:1, 0:16], in_=trow[0:1, 0:16], func=AF.Ln), wait=[ve])
            P.on("scalar", lambda e: e.activation(out=trow[0:1, 0:16], in_=trow[0:1, 0:16], func=AF.Exp, scale=-0.5))
            P.on("tensor", lambda e, b=b: e.matmul(ps_bc[:, 0:16], on1_sb[:], trow[0:1, 16:32], start=True, stop=True), wait=[(sct, sct.count)], inc=te)
            P.on("tensor", lambda e, b=b: e.matmul(ps_bc[:, 32:48], on1_sb[:], trow[0:1, 0:16], start=True, stop=True), inc=te)
            P.on("vector", lambda e, b=b: e.tensor_copy(out=gnc_sb[:, b * 32:b * 32 + 16], in_=ps_bc[:, 0:32][:, 0:16]), wait=[te])
            P.on("vector", lambda e, b=b: e.tensor_copy(out=gnc_sb[:, b * 32 + 16:b * 32 + 32], in_=ps_bc[:, 32:48]))

        for g in range(8):
            b = g // 4
            for dc in range(8):
                for st4 in range(4):
                    tok0 = g * 512 + st4 * 128
                    P.dma("sync", xnf[:, dc, st4 * 128:(st4 + 1) * 128],
                          xn_tm[tok0:tok0 + 128, dc * 128:(dc + 1) * 128], ld,
                          wait=([(te, te.count), (ve, ve.count)] if (dc == 0 and st4 == 0) else ()), transpose=True)
            w_xn = (ld, ld.count)
            for oc in range(16):
                par = oc % 2
                P.dma("gpsimd", zbuf[:, par, :], z_fm[oc, :, g * 512:(g + 1) * 512], gp,
                      wait=[(ve, ve.count)])
                P.dma("gpsimd", xcb[:, par, :], xc_fm[oc, :, g * 512:(g + 1) * 512], gp)
                w_z = (gp, gp.count)
                psY = [big0, big1][oc % 2]
                P.on("tensor", lambda e, oc=oc, g=g, psY=psY: e.matmul(
                    psY[:], wr_sb[:, oc * 128:(oc + 1) * 128], rT_sb[:, g * 512:(g + 1) * 512],
                    start=True, stop=True), wait=[(ve, ve.count), (sct, sct.count)], inc=te)
                w_y = (te, te.count)
                P.on("scalar", lambda e, par=par: e.activation(out=zs_sb[:], in_=zbuf[:, par, :], func=AF.Silu), wait=[w_z, (ve, ve.count)])
                P.on("vector", lambda e, psY=psY, b=b, oc=oc: e.tensor_scalar(
                    out=t32a[:], in0=psY[:], scalar1=gnc_sb[:, b * 32 + (oc // 1) % 16:b * 32 + oc % 16 + 1] if False else gnc_sb[:, b * 32 + oc:b * 32 + oc + 1],
                    scalar2=gnc_sb[:, b * 32 + 16 + oc:b * 32 + 16 + oc + 1],
                    op0=ALU.subtract, op1=ALU.mult), wait=[w_y])
                P.on("vector", lambda e, oc=oc: e.tensor_scalar(
                    out=t32a[:], in0=t32a[:], scalar1=gnw_sb[:, oc:oc + 1], scalar2=gnb_sb[:, oc:oc + 1],
                    op0=ALU.mult, op1=ALU.add))
                P.on("vector", lambda e: e.tensor_tensor(out=t32a[:], in0=t32a[:], in1=zs_sb[:], op=ALU.mult), wait=[sct])
                P.on("vector", lambda e, par=par, oc=oc: e.scalar_tensor_tensor(
                    out=spre[:, oc, :], in0=xcb[:, par, :], scalar=dp_sb[:, oc:oc + 1], in1=t32a[:],
                    op0=ALU.mult, op1=ALU.add))
            w_sp = (ve, ve.count)
            for od in range(8):
                psO = [big0, big1][od % 2]
                for k in range(16):
                    P.on("tensor", lambda e, od=od, k=k, psO=psO: e.matmul(
                        psO[:], wo_sb[:, k, od * 128:(od + 1) * 128], spre[:, k, :],
                        start=(k == 0), stop=(k == 15)),
                        wait=([w_sp, w_w, w_xn, (ve, ve.count)] if k == 0 else ()),
                        inc=(te if k == 15 else None))
                w_o = (te, te.count)
                P.on("vector", lambda e, od=od, psO=psO: e.tensor_tensor(
                    out=sout[:, od % 2, :], in0=psO[:], in1=xnf[:, od, :], op=ALU.add), wait=[w_o, (gp, gp.count)])
                w_so = (ve, ve.count)
                P.dma("gpsimd", s_fm[od, :, g * 512:(g + 1) * 512], sout[:, od % 2, :], gp, wait=[w_so])

        P.on("gpsimd", lambda e: e.engine_nop(), wait=[(gp, gp.count)], inc=None)
        P.play(block)
    return nc


def run_l0_comb(front_results, core_results, wts, consts):
    nc = build_l0_comb()
    base = {
        "wr": _f32(wts["L0_attn_readout_w"]),
        "wo": _f32(wts["L0_attn_out_w"]),
        "gnw": _f32(np.asarray(wts["L0_attn_gn_w"]).reshape(16, 128).T),
        "gnb": _f32(np.asarray(wts["L0_attn_gn_b"]).reshape(16, 128).T),
        "dpar": _f32(np.asarray(wts["L0_attn_D_param"]).reshape(16, 128).T),
        "eye32": consts["eye_f32"],
        "ones1": _f32(np.ones((1, 128))),
        "onecol": _f32(np.ones((128, 1))),
        "ones32": _f32(np.ones((32, 1))),
    }
    in_maps = []
    for fr, cr in zip(front_results, core_results):
        m = dict(base)
        m["retr"] = cr["retr_o"]
        m["z_fm"] = fr["z_fm"]; m["xc_fm"] = fr["xc_fm"]; m["xn_tm"] = fr["xn_tm"]
        in_maps.append(m)
    return run_bass_kernel_spmd(nc, in_maps, core_ids=list(range(NCORE))).results



def build_l0_ffn():
    nc = bass.Bass("TRN2", target_bir_lowering=False, debug=False)
    xn_tm = nc.dram_tensor("xn_tm", [T, DM], BF16, kind="ExternalInput").ap()
    s_fm = nc.dram_tensor("s_fm", [8, 128, T], BF16, kind="ExternalInput").ap()
    x_in = nc.dram_tensor("x_in", [T, DM], F32, kind="ExternalInput").ap()
    gw = nc.dram_tensor("gw", [DM, 8], F32, kind="ExternalInput").ap()
    w1 = nc.dram_tensor("w1", [8, DM, 2048], F32, kind="ExternalInput").ap()
    b1c = nc.dram_tensor("b1c", [128, 8, 16], F32, kind="ExternalInput").ap()
    l1g = nc.dram_tensor("l1g", [128, 8, 16], F32, kind="ExternalInput").ap()
    l1b = nc.dram_tensor("l1b", [128, 8, 16], F32, kind="ExternalInput").ap()
    w2 = nc.dram_tensor("w2", [8, 2048, DM], F32, kind="ExternalInput").ap()
    b2r = nc.dram_tensor("b2r", [8, DM], F32, kind="ExternalInput").ap()
    lngb = nc.dram_tensor("lngb", [128, DM], F32, kind="ExternalInput").ap()
    lnbb = nc.dram_tensor("lnbb", [128, DM], F32, kind="ExternalInput").ap()
    eye32 = nc.dram_tensor("eye32", [128, 128], F32, kind="ExternalInput").ap()
    ones1 = nc.dram_tensor("ones1", [1, 128], F32, kind="ExternalInput").ap()
    onebf = nc.dram_tensor("onebf", [128, 1], BF16, kind="ExternalInput").ap()
    x1_o = nc.dram_tensor("x1_o", [T, DM], F32, kind="ExternalOutput").ap()
    moe_fm = nc.dram_tensor("moe_fm", [8, 128, T], BF16, kind="Internal").ap()

    with contextlib.ExitStack() as _es:
        S = lambda n, sh, dt: _es.enter_context(nc.sbuf_tensor(n, sh, dt))
        gw_sb = S("gw_sb", [128, 8, 8], BF16)
        gwf_sb = S("gwf_sb", [128, 8, 8], F32)
        xg32 = S("xg32", [128, 1024], F32)
        xnfm32 = S("xnfm32", [128, 8, 128], F32)
        w1e = S("w1e", [128, 2, 8, 2048], BF16)
        w2e = S("w2e", [128, 16, 1024], BF16)
        b1_sb = S("b1_sb", [128, 8, 16], F32)
        l1g_sb = S("l1g_sb", [128, 8, 16], F32)
        l1b_sb = S("l1b_sb", [128, 8, 16], F32)
        b2_sb = S("b2_sb", [8, DM], BF16)
        lng_sb = S("lng_sb", [128, DM], F32)
        lnb_sb = S("lnb_sb", [128, DM], F32)
        eye_sb = S("eye_sb", [128, 128], F32)
        on1_sb = S("on1_sb", [1, 128], F32)
        onbf_sb = S("onbf_sb", [128, 1], BF16)
        xnf = S("xnf", [128, 8, 512], BF16)
        glog = S("glog", [128, 8], F32)
        gcolt = S("gcolt", [128, 8], F32)
        cmb = S("cmb", [128, 8], F32)
        cmbrow = S("cmbrow", [1, 8, 512], F32)
        cmbT8 = S("cmbT8", [8, 512], BF16)
        h_sb = S("h_sb", [128, 16, 512], BF16)
        hsq = S("hsq", [128, 16, 512], BF16)
        srow_f = S("srow_f", [1, 2, 512], F32)
        bmu_sb = S("bmu_sb", [128, 512], F32)
        brs_sb = S("brs_sb", [128, 512], F32)
        bcc_sb = S("bcc_sb", [128, 512], BF16)
        t32 = S("t32", [128, 512], F32)
        ysum = S("ysum", [128, 8, 512], F32)
        moes = S("moes", [128, 2, 512], BF16)
        mt = S("mt", [128, 8, 128], BF16)
        cols = S("cols4", [128, 16], F32)
        stm = S("stm", [128, 8, 128], BF16)
        pb_h = _es.enter_context(nc.psum_tensor("pb_h", [128, 512], F32))
        pb_aux = _es.enter_context(nc.psum_tensor("pb_aux", [128, 512], F32))
        pb_y = [_es.enter_context(nc.psum_tensor(f"pb_y{i}", [128, 512], F32)) for i in range(2)]
        pb_bc = _es.enter_context(nc.psum_tensor("pb_bc", [128, 512], F32))
        pb_s2 = _es.enter_context(nc.psum_tensor("pb_s2", [1, 512], F32))
        s_ld = _es.enter_context(nc.semaphore("s_ld"))
        s_gp = _es.enter_context(nc.semaphore("s_gp"))
        s_te = _es.enter_context(nc.semaphore("s_te"))
        s_ve = _es.enter_context(nc.semaphore("s_ve"))
        s_sc = _es.enter_context(nc.semaphore("s_sc"))
        block = _es.enter_context(nc.Block())
        ld = Trk(s_ld); gp = Trk(s_gp); te = Trk(s_te); ve = Trk(s_ve); sct = Trk(s_sc)
        P = Prog(nc, self_trks={"vector": ve, "scalar": sct})

        for ap_, t_ in ((b1c, b1_sb), (l1g, l1g_sb), (l1b, l1b_sb), (lngb, lng_sb),
                        (lnbb, lnb_sb), (eye32, eye_sb), (ones1, on1_sb)):
            P.dma("sync", t_[:], ap_[:], ld)
        P.dma("gpsimd", gw_sb[:], gw.rearrange("(k p) e -> p k e", p=128), gp)
        P.dma("sync", gwf_sb[:], gw.rearrange("(k p) e -> p k e", p=128), ld)
        P.dma("gpsimd", b2_sb[:], b2r[:], gp)
        P.dma("gpsimd", onbf_sb[:], onebf[:], gp)
        w_c = (ld, ld.count)
        w_cg = (gp, gp.count)

        w1_guard = {}
        for g in range(8):
            for dc in range(8):
                for st4 in range(4):
                    tok0 = g * 512 + st4 * 128
                    P.dma("sync", xnf[:, dc, st4 * 128:(st4 + 1) * 128],
                          xn_tm[tok0:tok0 + 128, dc * 128:(dc + 1) * 128], ld,
                          wait=([(te, te.count), (ve, ve.count)] if (dc == 0 and st4 == 0) else ()),
                          transpose=True)
            w_xn = (ld, ld.count)
            for st4 in range(4):
                tok0 = g * 512 + st4 * 128
                P.dma("gpsimd", xg32[:], x_in[tok0:tok0 + 128, :], gp,
                      wait=[(ve, ve.count), (sct, sct.count), (te, te.count)])
                w_xg = (gp, gp.count)
                P.on("vector", lambda e: e.tensor_scalar(
                    out=h_sb[:, 2:4, :].rearrange('p a b -> p (a b)'), in0=xg32[:], scalar1=1.0, scalar2=None,
                    op0=ALU.mult, op1=ALU.add, accum_out=cols[:, 0:1]), wait=[w_xg])
                P.on("vector", lambda e: e.scalar_tensor_tensor(
                    out=h_sb[:, 2:4, :].rearrange('p a b -> p (a b)'), in0=xg32[:], scalar=1.0, in1=xg32[:],
                    op0=ALU.bypass, op1=ALU.mult, accum_out=cols[:, 1:2]))
                P.on("vector", lambda e: e.tensor_scalar(out=cols[:, 2:3], in0=cols[:, 0:1], scalar1=1.0 / DM, scalar2=None, op0=ALU.mult))
                P.on("vector", lambda e: e.tensor_scalar(out=cols[:, 3:4], in0=cols[:, 1:2], scalar1=1.0 / DM, scalar2=None, op0=ALU.mult))
                P.on("vector", lambda e: e.tensor_scalar(out=cols[:, 4:5], in0=cols[:, 2:3], scalar1=cols[:, 2:3], scalar2=None, op0=ALU.mult))
                P.on("vector", lambda e: e.tensor_tensor(out=cols[:, 5:6], in0=cols[:, 3:4], in1=cols[:, 4:5], op=ALU.subtract))
                P.on("vector", lambda e: e.tensor_scalar(out=cols[:, 6:7], in0=cols[:, 5:6], scalar1=1e-5, scalar2=None, op0=ALU.add))
                P.on("scalar", lambda e: e.activation(out=cols[:, 7:8], in_=cols[:, 6:7], func=AF.Ln), wait=[ve])
                P.on("scalar", lambda e: e.activation(out=cols[:, 8:9], in_=cols[:, 7:8], func=AF.Exp, scale=-0.5))
                P.on("vector", lambda e: e.tensor_scalar(out=xg32[:], in0=xg32[:], scalar1=cols[:, 2:3], scalar2=cols[:, 8:9], op0=ALU.subtract, op1=ALU.mult), wait=[sct])
                wxn32 = (ve, ve.count)
                for k in range(8):
                    P.on("tensor", lambda e, k=k: e.transpose(pb_aux[:, 256:384], xg32[:, k * 128:(k + 1) * 128], eye_sb[:]),
                         wait=[wxn32, (sct, sct.count)], inc=te)
                    P.on("scalar", lambda e, k=k: e.activation(out=xnfm32[:, k, :], in_=pb_aux[:, 256:384], func=AF.Identity), wait=[te])
                w_fm32 = (sct, sct.count)
                for k in range(8):
                    P.on("tensor", lambda e, k=k, st4=st4: e.matmul(
                        pb_aux[:, 384:392], xnfm32[:, k, :], gwf_sb[:, k, :],
                        start=(k == 0), stop=(k == 7)),
                        wait=([w_fm32, w_c, (ve, ve.count)] if k == 0 else ()),
                        inc=(te if k == 7 else None))
                P.on("vector", lambda e: e.tensor_copy(out=glog[:], in_=pb_aux[:, 384:392]), wait=[te])
                P.on("vector", lambda e: e.reduce_max(out=cols[:, 0:1], in_=glog[:], axis=AX.X))
                P.on("vector", lambda e: e.tensor_scalar(out=cols[:, 1:2], in0=cols[:, 0:1], scalar1=-1.0, scalar2=None, op0=ALU.mult))
                P.on("scalar", lambda e: e.activation(out=gcolt[:], in_=glog[:], func=AF.Exp, bias=cols[:, 1:2]), wait=[ve])
                P.on("vector", lambda e: e.tensor_reduce(out=cols[:, 2:3], in_=gcolt[:], axis=AX.X, op=ALU.add), wait=[sct])
                P.on("vector", lambda e: e.reciprocal(out=cols[:, 3:4], in_=cols[:, 2:3]))
                P.on("vector", lambda e: e.tensor_scalar(out=gcolt[:], in0=gcolt[:], scalar1=cols[:, 3:4], scalar2=None, op0=ALU.mult))
                P.on("vector", lambda e: e.reduce_max(out=cols[:, 4:5], in_=gcolt[:], axis=AX.X))
                P.on("vector", lambda e: e.tensor_scalar(out=cmb[:], in0=gcolt[:], scalar1=cols[:, 4:5], scalar2=None, op0=ALU.is_ge))
                P.on("vector", lambda e: e.scalar_tensor_tensor(out=cmb[:], in0=cmb[:], scalar=-2.0, in1=gcolt[:], op0=ALU.mult, op1=ALU.add))
                P.on("vector", lambda e: e.reduce_max(out=cols[:, 5:6], in_=cmb[:], axis=AX.X))
                P.on("vector", lambda e: e.tensor_tensor(out=cols[:, 6:7], in0=cols[:, 4:5], in1=cols[:, 5:6], op=ALU.add))
                P.on("vector", lambda e: e.reciprocal(out=cols[:, 7:8], in_=cols[:, 6:7]))
                P.on("vector", lambda e: e.tensor_scalar(out=cmb[:], in0=gcolt[:], scalar1=cols[:, 5:6], scalar2=None, op0=ALU.is_ge))
                P.on("vector", lambda e: e.tensor_tensor(out=cmb[:], in0=cmb[:], in1=gcolt[:], op=ALU.mult))
                P.on("vector", lambda e: e.tensor_scalar(out=cmb[:], in0=cmb[:], scalar1=cols[:, 7:8], scalar2=None, op0=ALU.mult))
                wcmb = (ve, ve.count)
                P.on("tensor", lambda e: e.transpose(pb_aux[0:8, 128:256], cmb[:], eye_sb[:]),
                     wait=[wcmb, (sct, sct.count)], inc=te)
                P.on("scalar", lambda e, st4=st4: e.activation(out=cmbT8[:, st4 * 128:(st4 + 1) * 128], in_=pb_aux[0:8, 128:256], func=AF.Identity), wait=[te])
                for ei in range(8):
                    P.on("tensor", lambda e, ei=ei: e.transpose(pb_aux[0:1, 0:128], cmb[:, ei:ei + 1], eye_sb[:]),
                         wait=[(sct, sct.count)], inc=te)
                    P.on("scalar", lambda e, ei=ei, st4=st4: e.activation(out=cmbrow[0:1, ei, st4 * 128:(st4 + 1) * 128], in_=pb_aux[0:1, 0:128], func=AF.Identity), wait=[te])
            for ei in range(8):
                par = (g * 8 + ei) % 2
                if g == 0 and ei == 0:
                    P.dma("gpsimd", w1e[:, par, :, :], w1[ei].rearrange("(k p) f -> p k f", p=128), gp)
                w_we = w1_guard.get((g, ei), (gp, gp.count))
                ng, ne = (g, ei + 1) if ei < 7 else (g + 1, 0)
                if ng < 8:
                    P.dma("gpsimd", w1e[:, (g * 8 + ei + 1) % 2, :, :],
                          w1[ne].rearrange("(k p) f -> p k f", p=128), gp,
                          wait=[(te, te.count)])
                    w1_guard[(ng, ne)] = (gp, gp.count)
                P.dma("gpsimd", w2e[:], w2[ei].rearrange("(k p) f -> p k f", p=128), gp,
                      wait=[(te, te.count)])
                w_w2 = (gp, gp.count)
                hbanks = [pb_h, pb_aux]
                hguard = {0: None, 1: None}
                for hc in range(16):
                    hb = hbanks[hc % 2]
                    wl = [w_we] if hc == 0 else []
                    if hguard[hc % 2] is not None:
                        wl.append(hguard[hc % 2])
                    for k in range(8):
                        P.on("tensor", lambda e, k=k, hc=hc, hb=hb, par=par: e.matmul(
                            hb[:], w1e[:, par, k, hc * 128:(hc + 1) * 128], xnf[:, k, :],
                            start=(k == 0), stop=(k == 7)),
                            wait=(wl if k == 0 else ()),
                            inc=(te if k == 7 else None))
                    P.on("scalar", lambda e, hc=hc, ei=ei, hb=hb: e.activation(
                        out=h_sb[:, hc, :], in_=hb[:], func=AF.Identity,
                        bias=b1_sb[:, ei, hc:hc + 1]), wait=[(te, te.count)])
                    hguard[hc % 2] = (sct, sct.count)
                w_hsb = (sct, sct.count)
                for hc in range(16):
                    P.on("vector", lambda e, hc=hc: e.tensor_tensor(
                        out=hsq[:, hc, :], in0=h_sb[:, hc, :], in1=h_sb[:, hc, :], op=ALU.mult),
                        wait=([w_hsb, (te, te.count)] if hc == 0 else ()))
                w_sq = (ve, ve.count)
                for hc in range(16):
                    P.on("tensor", lambda e, hc=hc: e.matmul(
                        pb_bc[0:1, :], onbf_sb[:], h_sb[:, hc, :],
                        start=(hc == 0), stop=(hc == 15), skip_group_check=True),
                        wait=([w_sq, w_hsb] if hc == 0 else ()), inc=(te if hc == 15 else None))
                for hc in range(16):
                    P.on("tensor", lambda e, hc=hc: e.matmul(
                        pb_s2[:], onbf_sb[:], hsq[:, hc, :],
                        start=(hc == 0), stop=(hc == 15), skip_group_check=True),
                        inc=(te if hc == 15 else None))
                w_st = (te, te.count)
                P.on("vector", lambda e: e.tensor_scalar(out=srow_f[0:1, 0, :], in0=pb_bc[0:1, :], scalar1=1.0 / 2048, scalar2=None, op0=ALU.mult), wait=[w_st])
                P.on("vector", lambda e: e.tensor_scalar(out=srow_f[0:1, 1, :], in0=pb_s2[:], scalar1=1.0 / 2048, scalar2=None, op0=ALU.mult))
                P.on("vector", lambda e: e.tensor_tensor(out=t32[0:1, 0:512], in0=srow_f[0:1, 0, :], in1=srow_f[0:1, 0, :], op=ALU.mult))
                P.on("vector", lambda e: e.tensor_tensor(out=srow_f[0:1, 1, :], in0=srow_f[0:1, 1, :], in1=t32[0:1, 0:512], op=ALU.subtract))
                P.on("vector", lambda e: e.tensor_scalar(out=srow_f[0:1, 1, :], in0=srow_f[0:1, 1, :], scalar1=1e-5, scalar2=None, op0=ALU.add))
                P.on("scalar", lambda e: e.activation(out=srow_f[0:1, 1, :], in_=srow_f[0:1, 1, :], func=AF.Ln), wait=[ve])
                P.on("scalar", lambda e: e.activation(out=srow_f[0:1, 1, :], in_=srow_f[0:1, 1, :], func=AF.Exp, scale=-0.5))
                P.on("tensor", lambda e: e.matmul(pb_bc[:], on1_sb[:], srow_f[0:1, 0, :], start=True, stop=True),
                     wait=[(sct, sct.count), (ve, ve.count)], inc=te)
                P.on("vector", lambda e: e.tensor_copy(out=bmu_sb[:], in_=pb_bc[:]), wait=[te])
                P.on("tensor", lambda e: e.matmul(pb_bc[:], on1_sb[:], srow_f[0:1, 1, :], start=True, stop=True),
                     wait=[(ve, ve.count)], inc=te)
                P.on("vector", lambda e: e.tensor_copy(out=brs_sb[:], in_=pb_bc[:]), wait=[te])
                P.on("tensor", lambda e, ei=ei, g=g: e.matmul(pb_bc[:], on1_sb[:], cmbrow[0:1, ei, :], start=True, stop=True),
                     wait=[(ve, ve.count)], inc=te)
                P.on("vector", lambda e: e.tensor_copy(out=bcc_sb[:], in_=pb_bc[:]), wait=[te])
                for hc in range(16):
                    P.on("vector", lambda e, hc=hc: e.tensor_tensor(out=t32[:], in0=h_sb[:, hc, :], in1=bmu_sb[:], op=ALU.subtract),
                         wait=([(te, te.count)] if hc == 0 else ()))
                    P.on("vector", lambda e, hc=hc: e.scalar_tensor_tensor(
                        out=hsq[:, hc, :], in0=t32[:], scalar=1.0, in1=brs_sb[:], op0=ALU.bypass, op1=ALU.mult))
                w_nx = (ve, ve.count)
                for hc in range(16):
                    P.on("scalar", lambda e, hc=hc, ei=ei: e.activation(
                        out=h_sb[:, hc, :], in_=hsq[:, hc, :], func=AF.Gelu,
                        scale=l1g_sb[:, ei, hc:hc + 1], bias=l1b_sb[:, ei, hc:hc + 1]),
                        wait=([w_nx] if hc == 0 else ()))
                w_gl = (sct, sct.count)
                for hc in range(16):
                    P.on("vector", lambda e, hc=hc: e.tensor_tensor(out=hsq[:, hc, :], in0=h_sb[:, hc, :], in1=bcc_sb[:], op=ALU.mult),
                         wait=([w_gl] if hc == 0 else ()))
                w_hg = (ve, ve.count)
                for od in range(8):
                    psY = pb_y[od % 2]
                    first = (ei == 0)
                    if first:
                        P.on("tensor", lambda e, od=od, psY=psY: e.matmul(
                            psY[:], b2_sb[:, od * 128:(od + 1) * 128], cmbT8[:],
                            start=True, stop=False, skip_group_check=True),
                            wait=[w_hg, (sct, sct.count), (ve, ve.count)], inc=None)
                    for k in range(16):
                        P.on("tensor", lambda e, od=od, k=k, psY=psY, first=first: e.matmul(
                            psY[:], w2e[:, k, od * 128:(od + 1) * 128], hsq[:, k, :],
                            start=(k == 0 and not first), stop=(k == 15), skip_group_check=True),
                            wait=([w_hg, w_w2, (ve, ve.count)] if k == 0 else ()), inc=(te if k == 15 else None))
                    w_ymm = (te, te.count)
                    if first:
                        P.on("vector", lambda e, od=od, psY=psY: e.tensor_copy(out=ysum[:, od, :], in_=psY[:]), wait=[w_ymm])
                    else:
                        P.on("vector", lambda e, od=od, psY=psY: e.tensor_tensor(out=ysum[:, od, :], in0=ysum[:, od, :], in1=psY[:], op=ALU.add), wait=[w_ymm])
            for od in range(8):
                P.on("vector", lambda e, od=od: e.tensor_tensor(
                    out=moes[:, od % 2, :], in0=ysum[:, od, :], in1=xnf[:, od, :], op=ALU.add),
                    wait=[(gp, gp.count)])
                wmo = (ve, ve.count)
                P.dma("gpsimd", moe_fm[od, :, g * 512:(g + 1) * 512], moes[:, od % 2, :], gp, wait=[wmo])
        w_moe_all = (gp, gp.count)
        for i in range(32):
            for dc in range(8):
                P.dma("sync", mt[:, dc, :], moe_fm[dc, :, i * 128:(i + 1) * 128], ld,
                      wait=([w_moe_all, (ve, ve.count)] if dc == 0 else ()), transpose=True)
            for dc in range(8):
                P.dma("sync", stm[:, dc, :], s_fm[dc, :, i * 128:(i + 1) * 128], ld, transpose=True)
            P.dma("gpsimd", xg32[:], x_in[i * 128:(i + 1) * 128, :], gp, wait=[(ve, ve.count)])
            w_mt = (ld, ld.count)
            w_xf = (gp, gp.count)
            mt2 = mt[:].rearrange("p a b -> p (a b)")
            P.on("vector", lambda e: e.tensor_scalar(
                out=h_sb[:, 2:4, :].rearrange('p a b -> p (a b)'), in0=mt2, scalar1=1.0, scalar2=None,
                op0=ALU.mult, op1=ALU.add, accum_out=cols[:, 0:1]), wait=[w_mt])
            P.on("vector", lambda e: e.scalar_tensor_tensor(
                out=h_sb[:, 2:4, :].rearrange('p a b -> p (a b)'), in0=mt2, scalar=1.0, in1=mt2,
                op0=ALU.bypass, op1=ALU.mult, accum_out=cols[:, 1:2]))
            P.on("vector", lambda e: e.tensor_scalar(out=cols[:, 2:3], in0=cols[:, 0:1], scalar1=1.0 / DM, scalar2=None, op0=ALU.mult))
            P.on("vector", lambda e: e.tensor_scalar(out=cols[:, 3:4], in0=cols[:, 1:2], scalar1=1.0 / DM, scalar2=None, op0=ALU.mult))
            P.on("vector", lambda e: e.tensor_scalar(out=cols[:, 4:5], in0=cols[:, 2:3], scalar1=cols[:, 2:3], scalar2=None, op0=ALU.mult))
            P.on("vector", lambda e: e.tensor_tensor(out=cols[:, 5:6], in0=cols[:, 3:4], in1=cols[:, 4:5], op=ALU.subtract))
            P.on("vector", lambda e: e.tensor_scalar(out=cols[:, 6:7], in0=cols[:, 5:6], scalar1=1e-5, scalar2=None, op0=ALU.add))
            P.on("scalar", lambda e: e.activation(out=cols[:, 7:8], in_=cols[:, 6:7], func=AF.Ln), wait=[ve])
            P.on("scalar", lambda e: e.activation(out=cols[:, 8:9], in_=cols[:, 7:8], func=AF.Exp, scale=-0.5))
            P.on("vector", lambda e: e.tensor_scalar(out=ysum[:, 0:2, :].rearrange('p a b -> p (a b)'), in0=mt2, scalar1=cols[:, 2:3], scalar2=cols[:, 8:9], op0=ALU.subtract, op1=ALU.mult), wait=[sct])
            P.on("vector", lambda e: e.tensor_tensor(out=ysum[:, 0:2, :].rearrange('p a b -> p (a b)'), in0=ysum[:, 0:2, :].rearrange('p a b -> p (a b)'), in1=lng_sb[:], op=ALU.mult), wait=[w_c])
            P.on("vector", lambda e: e.tensor_tensor(out=ysum[:, 0:2, :].rearrange('p a b -> p (a b)'), in0=ysum[:, 0:2, :].rearrange('p a b -> p (a b)'), in1=lnb_sb[:], op=ALU.add))
            P.on("scalar", lambda e: e.activation(out=h_sb[:, 0:2, :].rearrange('p a b -> p (a b)'), in_=ysum[:, 0:2, :].rearrange('p a b -> p (a b)'), func=AF.Silu), wait=[ve])
            P.on("vector", lambda e: e.tensor_tensor(
                out=ysum[:, 0:2, :].rearrange('p a b -> p (a b)'), in0=stm[:].rearrange("p a b -> p (a b)"), in1=h_sb[:, 0:2, :].rearrange('p a b -> p (a b)'), op=ALU.mult),
                wait=[sct, w_mt])
            P.on("vector", lambda e: e.tensor_tensor(out=ysum[:, 0:2, :].rearrange('p a b -> p (a b)'), in0=ysum[:, 0:2, :].rearrange('p a b -> p (a b)'), in1=xg32[:], op=ALU.add), wait=[w_xf])
            wx1 = (ve, ve.count)
            P.dma("gpsimd", x1_o[i * 128:(i + 1) * 128, :], ysum[:, 0:2, :].rearrange('p a b -> p (a b)'), gp, wait=[wx1])

        P.on("gpsimd", lambda e: e.engine_nop(), wait=[(gp, gp.count)], inc=None)
        P.play(block)
    return nc


def run_l0_ffn(front_results, comb_results, per_core_x, wts, consts):
    nc = build_l0_ffn()
    b1 = np.asarray(wts["L0_ffn_b1"], np.float32)
    l1gv = np.asarray(wts["L0_ffn_ln1_g"], np.float32)
    l1bv = np.asarray(wts["L0_ffn_ln1_b"], np.float32)
    base = {
        "gw": _f32(wts["L0_ffn_gate_w"]),
        "w1": _f32(wts["L0_ffn_w1"]),
        "b1c": _f32(np.stack([b1[e].reshape(16, 128).T for e in range(8)], axis=1)),
        "l1g": _f32(np.stack([l1gv[e].reshape(16, 128).T for e in range(8)], axis=1)),
        "l1b": _f32(np.stack([l1bv[e].reshape(16, 128).T for e in range(8)], axis=1)),
        "w2": _f32(wts["L0_ffn_w2"]),
        "b2r": _f32(wts["L0_ffn_b2"]),
        "lngb": _f32(np.broadcast_to(wts["L0_ffn_ln_g"], (128, DM))),
        "lnbb": _f32(np.broadcast_to(wts["L0_ffn_ln_b"], (128, DM))),
        "eye32": consts["eye_f32"],
        "ones1": _f32(np.ones((1, 128))),
        "onebf": _bf(np.ones((128, 1))),
    }
    in_maps = []
    for fr, cb, xc in zip(front_results, comb_results, per_core_x):
        m = dict(base)
        m["xn_tm"] = fr["xn_tm"]; m["s_fm"] = cb["s_fm"]; m["x_in"] = _f32(xc)
        in_maps.append(m)
    return run_bass_kernel_spmd(nc, in_maps, core_ids=list(range(NCORE))).results



def build_l1_front():
    nc = bass.Bass("TRN2", target_bir_lowering=False, debug=False)
    x1 = nc.dram_tensor("x1", [T, DM], F32, kind="ExternalInput").ap()
    winT = nc.dram_tensor("winT", [DM, 4352], F32, kind="ExternalInput").ap()
    nw = nc.dram_tensor("nw", [128, 8], F32, kind="ExternalInput").ap()
    nb = nc.dram_tensor("nb", [128, 8], F32, kind="ExternalInput").ap()
    convw = nc.dram_tensor("convw", [128, 72], F32, kind="ExternalInput").ap()
    convb = nc.dram_tensor("convb", [128, 18], F32, kind="ExternalInput").ap()
    wdt = nc.dram_tensor("wdt", [DM, 32], F32, kind="ExternalInput").ap()
    dtb = nc.dram_tensor("dtb", [128, 32], F32, kind="ExternalInput").ap()
    alog = nc.dram_tensor("alog", [128, 32], F32, kind="ExternalInput").ap()
    eye32 = nc.dram_tensor("eye32", [128, 128], F32, kind="ExternalInput").ap()
    triu_i = nc.dram_tensor("triu_i", [128, 128], F32, kind="ExternalInput").ap()
    tril_s = nc.dram_tensor("tril_s", [128, 128], F32, kind="ExternalInput").ap()
    xn1_tm = nc.dram_tensor("xn1_tm", [T, DM], BF16, kind="ExternalOutput").ap()
    z2_fm = nc.dram_tensor("z2_fm", [16, 128, T], BF16, kind="ExternalOutput").ap()
    xs_fm = nc.dram_tensor("xs_fm", [16, 128, T], BF16, kind="ExternalOutput").ap()
    b_fm = nc.dram_tensor("b_fm", [128, T], BF16, kind="ExternalOutput").ap()
    c_fm = nc.dram_tensor("c_fm", [128, T], BF16, kind="ExternalOutput").ap()
    dtc = nc.dram_tensor("dtc", [T, 4, 32], F32, kind="ExternalOutput").ap()
    ahm = nc.dram_tensor("ahm", [BQ, 32, L], F32, kind="ExternalOutput").ap()
    xr2_tm = nc.dram_tensor("xr2_tm", [T, DM], BF16, kind="Internal").ap()

    with contextlib.ExitStack() as _es:
        S = lambda n, sh, dt: _es.enter_context(nc.sbuf_tensor(n, sh, dt))
        wzs = S("wzs", [128, 2, 8, 1024], BF16)
        wdt_sb = S("wdt_sb", [128, 8, 32], BF16)
        nw_sb = S("nw_sb", [128, 8], F32)
        nb_sb = S("nb_sb", [128, 8], F32)
        cw_sb = S("cw_sb", [128, 72], F32)
        cb_sb = S("cb_sb", [128, 18], F32)
        dtb_sb = S("dtb_sb", [128, 32], F32)
        na_sb = S("na_sb", [128, 32], F32)
        eye_sb = S("eye_sb", [128, 128], F32)
        triu_sb = S("triu_sb", [128, 128], F32)
        tril_sb = S("tril_sb", [128, 128], F32)
        diag_sb = S("diag_sb", [128, 72, 128], BF16)
        xt = S("xt", [128, 2, DM], F32)
        cols = S("cols5", [128, 16], F32)
        xnt = S("xnt", [128, 2, DM], BF16)
        xrt = S("xrt", [128, 2, DM], BF16)
        xw_fm = S("xw_fm", [128, 8, 512], BF16)
        xg_buf = S("xgb5", [128, 18, 515], BF16)
        z_sb = S("z_sb5", [128, 2, 512], BF16)
        o_sb = S("o_sb5", [128, 2, 512], BF16)
        gsc = S("gsc5", [128, 12, 32], F32)
        stage = S("stage5", [128, 4, 32], F32)
        ps0 = _es.enter_context(nc.psum_tensor("p50", [128, 512], F32))
        ps1 = _es.enter_context(nc.psum_tensor("p51", [128, 512], F32))
        psd = _es.enter_context(nc.psum_tensor("p5d", [128, 32], F32))
        psc = _es.enter_context(nc.psum_tensor("p5c", [128, 64], F32))
        pst = _es.enter_context(nc.psum_tensor("p5t", [32, 128], F32))
        s_ld = _es.enter_context(nc.semaphore("s_ld"))
        s_gp = _es.enter_context(nc.semaphore("s_gp"))
        s_te = _es.enter_context(nc.semaphore("s_te"))
        s_ve = _es.enter_context(nc.semaphore("s_ve"))
        s_sc = _es.enter_context(nc.semaphore("s_sc"))
        block = _es.enter_context(nc.Block())
        ld = Trk(s_ld); gp = Trk(s_gp); te = Trk(s_te); ve = Trk(s_ve); sct = Trk(s_sc)
        P = Prog(nc, self_trks={"vector": ve, "scalar": sct})

        for ap_, t_ in ((nw, nw_sb), (nb, nb_sb), (convw, cw_sb), (convb, cb_sb),
                        (dtb, dtb_sb), (alog, na_sb), (eye32, eye_sb), (triu_i, triu_sb),
                        (tril_s, tril_sb)):
            P.dma("sync", t_[:], ap_[:], ld)
        for k in range(8):
            P.dma("gpsimd", wdt_sb[:, k, :], wdt[k * 128:(k + 1) * 128, :], gp)
        w_c = (ld, ld.count)
        w_cg = (gp, gp.count)
        P.on("scalar", lambda e: e.activation(out=na_sb[:], in_=na_sb[:], func=AF.Exp), wait=[w_c])
        P.on("vector", lambda e: e.tensor_scalar(out=na_sb[:], in0=na_sb[:], scalar1=-1.0, scalar2=None, op0=ALU.mult), wait=[sct])
        for fc in range(18):
            for j in range(4):
                P.on("vector", lambda e, fc=fc, j=j: e.tensor_scalar(
                    out=diag_sb[:, fc * 4 + j, :], in0=eye_sb[:],
                    scalar1=cw_sb[:, j * 18 + fc:j * 18 + fc + 1], scalar2=None, op0=ALU.mult),
                    wait=[w_c], inc=ve)
        w_diag = (ve, ve.count)

        def c(i):
            return cols[:, i:i + 1]

        p0g = [None, None]
        for i in range(32):
            par = i % 2
            wl = [p0g[par]] if p0g[par] is not None else []
            P.dma("gpsimd", xt[:, par, :], x1[i * 128:(i + 1) * 128, :], gp, wait=wl)
            wx = (gp, gp.count)
            xti = xt[:, par, :]
            P.on("vector", lambda e, xti=xti, par=par: e.tensor_scalar(
                out=xnt[:, par, :], in0=xti, scalar1=1.0, scalar2=None,
                op0=ALU.mult, op1=ALU.add, accum_out=c(0)), wait=[wx], inc=ve)
            P.on("vector", lambda e, xti=xti, par=par: e.scalar_tensor_tensor(
                out=xrt[:, par, :], in0=xti, scalar=1.0, in1=xti,
                op0=ALU.bypass, op1=ALU.mult, accum_out=c(1)), inc=ve)
            P.on("vector", lambda e: e.tensor_scalar(out=c(2), in0=c(0), scalar1=1.0 / DM, scalar2=None, op0=ALU.mult), inc=ve)
            P.on("vector", lambda e: e.tensor_scalar(out=c(3), in0=c(1), scalar1=1.0 / DM, scalar2=None, op0=ALU.mult), inc=ve)
            P.on("vector", lambda e: e.tensor_scalar(out=c(4), in0=c(2), scalar1=c(2), scalar2=None, op0=ALU.mult), inc=ve)
            P.on("vector", lambda e: e.tensor_tensor(out=c(5), in0=c(3), in1=c(4), op=ALU.subtract), inc=ve)
            P.on("vector", lambda e: e.tensor_scalar(out=c(6), in0=c(5), scalar1=1e-5, scalar2=None, op0=ALU.add), inc=ve)
            P.on("scalar", lambda e: e.activation(out=c(7), in_=c(6), func=AF.Ln), wait=[ve], inc=sct)
            P.on("scalar", lambda e: e.activation(out=c(8), in_=c(7), func=AF.Exp, scale=-0.5), inc=sct)
            P.on("vector", lambda e: e.reciprocal(out=c(9), in_=c(6)), wait=[sct], inc=ve)
            P.on("vector", lambda e: e.tensor_tensor(out=c(10), in0=c(5), in1=c(9), op=ALU.mult), inc=ve)
            P.on("vector", lambda e: e.tensor_scalar(out=c(11), in0=c(10), scalar1=1e-5, scalar2=None, op0=ALU.add), inc=ve)
            P.on("scalar", lambda e: e.activation(out=c(12), in_=c(11), func=AF.Ln), wait=[ve], inc=sct)
            P.on("scalar", lambda e: e.activation(out=c(13), in_=c(12), func=AF.Exp, scale=-0.5), inc=sct)
            P.on("vector", lambda e: e.tensor_tensor(out=c(14), in0=c(8), in1=c(13), op=ALU.mult), wait=[sct], inc=ve)
            P.on("vector", lambda e, xti=xti, par=par: e.tensor_scalar(
                out=xnt[:, par, :], in0=xti, scalar1=c(2), scalar2=c(8), op0=ALU.subtract, op1=ALU.mult), inc=ve)
            P.on("vector", lambda e, xti=xti, par=par: e.tensor_scalar(
                out=xrt[:, par, :], in0=xti, scalar1=c(2), scalar2=c(14), op0=ALU.subtract, op1=ALU.mult), inc=ve)
            wv = (ve, ve.count)
            P.dma("gpsimd", xn1_tm[i * 128:(i + 1) * 128, :], xnt[:, par, :], gp, wait=[wv])
            P.dma("gpsimd", xr2_tm[i * 128:(i + 1) * 128, :], xrt[:, par, :], gp, wait=[wv])
            p0g[par] = (gp, gp.count)
        w_p0 = (gp, gp.count)

        psmain = [ps0, ps1]
        ps_guard = [None, None]
        psi = [0]

        def next_ps():
            i = psi[0] % 2
            psi[0] += 1
            return psmain[i], i

        def mm_acc(lhsTs, rhss, n, wait_extra=()):
            ps, i = next_ps()
            w = list(wait_extra)
            if ps_guard[i] is not None:
                w.append(ps_guard[i])
            for j in range(n):
                P.on("tensor", lambda e, a=lhsTs(j), b=rhss(j), ps=ps, j=j, n=n: e.matmul(
                    ps[:, 0:b.shape[-1]], a, b, start=(j == 0), stop=(j == n - 1)),
                    wait=(w if j == 0 else ()), inc=(te if j == n - 1 else None))
            return ps, i, (te, te.count)

        CBLK = [(0, 1024), (1024, 2048), (2048, 3072), (3072, 4096), (4096, 4352)]
        for g in range(8):
            wprev = [(te, te.count), (ve, ve.count)] if g > 0 else [w_p0]
            for dc in range(8):
                for st4 in range(4):
                    tok0 = g * 512 + st4 * 128
                    P.dma("sync", xw_fm[:, dc, st4 * 128:(st4 + 1) * 128],
                          xr2_tm[tok0:tok0 + 128, dc * 128:(dc + 1) * 128], ld,
                          wait=(wprev if (dc == 0 and st4 == 0) else ()), transpose=True)
            w_xr = (ld, ld.count)
            for dc in range(8):
                P.on("vector", lambda e, dc=dc: e.tensor_scalar(
                    out=xw_fm[:, dc, :], in0=xw_fm[:, dc, :], scalar1=nw_sb[:, dc:dc + 1],
                    scalar2=nb_sb[:, dc:dc + 1], op0=ALU.mult, op1=ALU.add), wait=[w_xr, w_c], inc=ve)
            w_rms = (ve, ve.count)
            if g % 4 == 0:
                P.on("vector", lambda e: e.memset(xg_buf[:, :, 0:3], 0.0), wait=[(te, te.count)], inc=ve)
            else:
                P.on("vector", lambda e: e.tensor_copy(out=xg_buf[:, :, 0:3], in_=xg_buf[:, :, 512:515]),
                     wait=[(te, te.count)], inc=ve)
            w_halo = (ve, ve.count)
            for bi, (c0, c1) in enumerate(CBLK[:5]):
                wpar = (g * 5 + bi) % 2
                ncol = c1 - c0
                P.dma("gpsimd", wzs[:, wpar, :, 0:ncol],
                      winT.rearrange("(k p) c -> p k c", p=128)[:, :, c0:c1],
                      gp, wait=[(te, te.count)])
                w_wz = (gp, gp.count)
                for oci in range(ncol // 128):
                    oc = (c0 // 128) + oci
                    ps, pi, wmm = mm_acc(
                        lambda k, oci=oci, wpar=wpar: wzs[:, wpar, k, oci * 128:(oci + 1) * 128],
                        lambda k: xw_fm[:, k, :], 8,
                        wait_extra=[w_rms, w_halo, w_wz])
                    if oc < 16:
                        zpar = oc % 2
                        P.on("scalar", lambda e, ps=ps, zpar=zpar: e.activation(
                            out=z_sb[:, zpar, :], in_=ps[:], func=AF.Identity),
                            wait=[wmm, (gp, gp.count)], inc=sct)
                        wz2 = (sct, sct.count)
                        P.dma("gpsimd", z2_fm[oc, :, g * 512:(g + 1) * 512], z_sb[:, zpar, :], gp, wait=[wz2])
                    else:
                        P.on("scalar", lambda e, ps=ps, oc=oc: e.activation(
                            out=xg_buf[:, oc - 16, 3:515], in_=ps[:], func=AF.Identity),
                            wait=[wmm], inc=sct)
                    ps_guard[pi] = (sct, sct.count)
            w_xg = (sct, sct.count)
            for fc in range(18):
                ps, pi, wmm = mm_acc(
                    lambda j, fc=fc: diag_sb[:, fc * 4 + j, :],
                    lambda j, fc=fc: xg_buf[:, fc, j:j + 512], 4,
                    wait_extra=[w_xg, w_diag])
                P.on("scalar", lambda e, ps=ps, fc=fc: e.activation(
                    out=o_sb[:, fc % 2, :], in_=ps[:], func=AF.Silu,
                    bias=cb_sb[:, fc:fc + 1]), wait=[wmm, (gp, gp.count)], inc=sct)
                wxc = (sct, sct.count)
                if fc < 16:
                    P.dma("gpsimd", xs_fm[fc, :, g * 512:(g + 1) * 512], o_sb[:, fc % 2, :], gp, wait=[wxc])
                elif fc == 16:
                    P.dma("gpsimd", b_fm[:, g * 512:(g + 1) * 512], o_sb[:, fc % 2, :], gp, wait=[wxc])
                else:
                    P.dma("gpsimd", c_fm[:, g * 512:(g + 1) * 512], o_sb[:, fc % 2, :], gp, wait=[wxc])
                ps_guard[pi] = wxc
            for st4 in range(4):
                for k in range(8):
                    P.on("tensor", lambda e, k=k, st4=st4: e.matmul(
                        psd[:], xw_fm[:, k, st4 * 128:(st4 + 1) * 128], wdt_sb[:, k, :],
                        start=(k == 0), stop=(k == 7)),
                        wait=([w_cg, (ve, ve.count), (sct, sct.count), (gp, gp.count)] if k == 0 else ()),
                        inc=(te if k == 7 else None))
                wdm = (te, te.count)
                Gv = lambda i: gsc[:, i, :]
                P.on("vector", lambda e: e.tensor_tensor(out=Gv(0), in0=psd[:], in1=dtb_sb[:], op=ALU.add), wait=[wdm])
                P.on("vector", lambda e: e.scalar_tensor_tensor(out=Gv(1), in0=Gv(0), scalar=-1.0, in1=Gv(0), op0=ALU.mult, op1=ALU.max))
                P.on("scalar", lambda e: e.activation(out=Gv(1), in_=Gv(1), func=AF.Exp, scale=-1.0), wait=[ve])
                P.on("scalar", lambda e: e.activation(out=Gv(1), in_=Gv(1), func=AF.Ln, bias=1.0))
                P.on("vector", lambda e: e.tensor_scalar(out=Gv(2), in0=Gv(0), scalar1=0.0, scalar2=None, op0=ALU.max), wait=[sct])
                P.on("vector", lambda e: e.tensor_tensor(out=stage[:, 0, :], in0=Gv(2), in1=Gv(1), op=ALU.add))
                P.on("vector", lambda e: e.tensor_tensor(out=Gv(3), in0=stage[:, 0, :], in1=na_sb[:], op=ALU.mult))
                wga = (ve, ve.count)
                P.on("tensor", lambda e: e.matmul(psc[:, 0:32], triu_sb[:], gsc[:, 3, :], start=True, stop=True),
                     wait=[wga, (sct, sct.count)], inc=te)
                P.on("tensor", lambda e: e.matmul(psc[:, 32:64], tril_sb[:], gsc[:, 3, :], start=True, stop=True), inc=te)
                wcs = (te, te.count)
                P.on("vector", lambda e: e.tensor_copy(out=stage[:, 1, :], in_=psc[:, 0:32]), wait=[wcs])
                P.on("scalar", lambda e: e.activation(out=stage[:, 2, :], in_=psc[:, 0:32], func=AF.Exp), wait=[wcs, ve])
                P.on("scalar", lambda e: e.activation(out=stage[:, 3, :], in_=psc[:, 32:64], func=AF.Exp))
                wst = (sct, sct.count)
                tok0 = g * 512 + st4 * 128
                P.dma("gpsimd", dtc[tok0:tok0 + 128, :, :], stage[:], gp, wait=[wst, (ve, ve.count)])
                P.on("tensor", lambda e: e.transpose(pst[:], stage[:, 1, :], eye_sb[:]),
                     wait=[(ve, ve.count), (sct, sct.count)], inc=te)
                P.on("scalar", lambda e: e.activation(out=gsc[:, 8, :].rearrange("p a -> p a") if False else gsc[0:32, 8:12, :].rearrange("p a b -> p (a b)"), in_=pst[:], func=AF.Identity),
                     wait=[te, (gp, gp.count)], inc=sct)
                wgh = (sct, sct.count)
                bq = tok0 // L
                lt = tok0 % L
                P.dma("gpsimd", ahm[bq, :, lt:lt + 128], gsc[0:32, 8:12, :].rearrange("p a b -> p (a b)"), gp, wait=[wgh])

        P.on("gpsimd", lambda e: e.engine_nop(), wait=[(gp, gp.count)], inc=None)
        P.play(block)
    return nc


def run_l1_front(x1_list, wts, consts):
    nc = build_l1_front()
    w = np.asarray(wts["L1_attn_in_proj_w"], np.float32)
    cw = np.asarray(wts["L1_attn_conv_w"], np.float32)
    base = {
        "winT": _f32(w[:4352].T),
        "wdt": _f32(w[4352:4384].T),
        "nw": _f32(np.asarray(wts["L1_attn_norm_w"]).reshape(8, 128).T),
        "nb": _f32(np.asarray(wts["L1_attn_norm_b"]).reshape(8, 128).T),
        "convw": _f32(cw.reshape(18, 128, 4).transpose(1, 2, 0).reshape(128, 72)),
        "convb": _f32(np.asarray(wts["L1_attn_conv_b"]).reshape(18, 128).T),
        "dtb": _f32(np.broadcast_to(wts["L1_attn_dt_bias"], (128, 32))),
        "alog": _f32(np.broadcast_to(wts["L1_attn_A_log"], (128, 32))),
        "eye32": consts["eye_f32"],
        "triu_i": consts["triu_incl"],
        "tril_s": consts["tril_strict"],
    }
    in_maps = [dict(base, x1=_f32(x1c)) for x1c in x1_list]
    return run_bass_kernel_spmd(nc, in_maps, core_ids=list(range(NCORE))).results



def build_l1_core():
    nc = bass.Bass("TRN2", target_bir_lowering=False, debug=False)
    xs_fm = nc.dram_tensor("xs_fm", [16, 128, T], BF16, kind="ExternalInput").ap()
    b_fm = nc.dram_tensor("b_fm", [128, T], BF16, kind="ExternalInput").ap()
    c_fm = nc.dram_tensor("c_fm", [128, T], BF16, kind="ExternalInput").ap()
    z2_fm = nc.dram_tensor("z2_fm", [16, 128, T], BF16, kind="ExternalInput").ap()
    dtc = nc.dram_tensor("dtc", [T, 4, 32], F32, kind="ExternalInput").ap()
    ahm = nc.dram_tensor("ahm", [BQ, 32, L], F32, kind="ExternalInput").ap()
    xn1_tm = nc.dram_tensor("xn1_tm", [T, DM], BF16, kind="ExternalInput").ap()
    dbc = nc.dram_tensor("dbc", [128, 32], F32, kind="ExternalInput").ap()
    rmsw = nc.dram_tensor("rmsw", [128, 16], F32, kind="ExternalInput").ap()
    woT = nc.dram_tensor("woT", [2048, DM], F32, kind="ExternalInput").ap()
    ones1 = nc.dram_tensor("ones1", [1, 128], F32, kind="ExternalInput").ap()
    onebf = nc.dram_tensor("onebf", [128, 1], BF16, kind="ExternalInput").ap()
    msku_i_i = nc.dram_tensor("msku_i_i", [128, 128], BF16, kind="ExternalInput").ap()
    s2_fm = nc.dram_tensor("s2_fm", [8, 128, T], BF16, kind="ExternalOutput").ap()
    ytm = nc.dram_tensor("ytm", [T, 2048], BF16, kind="Internal").ap()

    with contextlib.ExitStack() as _es:
        S = lambda n, sh, dt: _es.enter_context(nc.sbuf_tensor(n, sh, dt))
        msku_i = S("msku_i", [128, 128], BF16)
        on1_sb = S("on1_sb", [1, 128], F32)
        onbf_sb = S("onbf_sb", [128, 1], BF16)
        dbc_sb = S("dbc_sb", [128, 32], F32)
        rw_sb = S("rw_sb", [128, 16], F32)
        wo_sb = S("wo_sb", [128, 16, 1024], BF16)
        bfm = S("bfm", [128, 128], BF16)
        cfm = S("cfm", [128, 128], BF16)
        btm = S("btm", [128, 128], BF16)
        xstm = S("xstm", [128, 16, 128], BF16)
        xctm = S("xctm", [128, 2048], BF16)
        dtc_sb = S("dtc_sb", [128, 4, 32], F32)
        arow = S("arow", [1, 4096], F32)
        elrow = S("elrow", [1, 32], F32)
        elcols = S("elcols", [128, 32], F32)
        lh_sb = S("lh_sb", [128, 128], BF16)
        mh_sb = S("mh_sb", [128, 8, 128], BF16)
        cbm_sb = S("cbm_sb", [128, 128], BF16)
        epre6 = S("epre6", [128, 128], F32)
        bd_sb = S("bd_sb", [128, 128], BF16)
        prev_f = S("prev_f", [128, 2, 2048], F32)
        prev_b = S("prev_b", [128, 2, 2048], BF16)
        yt_sb = S("yt_sb", [128, 2048], BF16)
        t64 = S("t64", [128, 512], F32)
        y64 = S("y64", [128, 512], F32)
        yf_sb = S("yf_sb", [128, 2, 512], BF16)
        zb6 = S("zb6", [128, 2, 512], BF16)
        yg_sb = S("yg_sb", [128, 16, 512], BF16)
        sq6 = S("sq6", [128, 2, 512], BF16)
        srow6 = S("srow6", [1, 512], F32)
        br3 = S("br3", [128, 512], F32)
        xnf6 = S("xnf6", [128, 8, 512], BF16)
        so6 = S("so6", [128, 2, 512], BF16)
        psL = _es.enter_context(nc.psum_tensor("p6L", [128, 128], F32))
        psCB = _es.enter_context(nc.psum_tensor("p6CB", [128, 128], F32))
        psD = _es.enter_context(nc.psum_tensor("p6D", [128, 512], F32))
        psO = _es.enter_context(nc.psum_tensor("p6O", [128, 512], F32))
        psS = _es.enter_context(nc.psum_tensor("p6S", [128, 512], F32))
        psB = _es.enter_context(nc.psum_tensor("p6B", [128, 512], F32))
        psB2 = _es.enter_context(nc.psum_tensor("p6B2", [128, 512], F32))
        psR = _es.enter_context(nc.psum_tensor("p6R", [1, 512], F32))
        s_ld = _es.enter_context(nc.semaphore("s_ld"))
        s_gp = _es.enter_context(nc.semaphore("s_gp"))
        s_te = _es.enter_context(nc.semaphore("s_te"))
        s_ve = _es.enter_context(nc.semaphore("s_ve"))
        s_sc = _es.enter_context(nc.semaphore("s_sc"))
        block = _es.enter_context(nc.Block())
        ld = Trk(s_ld); gp = Trk(s_gp); te = Trk(s_te); ve = Trk(s_ve); sct = Trk(s_sc)
        P = Prog(nc, self_trks={"vector": ve, "scalar": sct})

        for ap_, t_ in ((msku_i_i, msku_i), (ones1, on1_sb), (dbc, dbc_sb), (rmsw, rw_sb)):
            P.dma("sync", t_[:], ap_[:], ld)
        P.dma("gpsimd", onbf_sb[:], onebf[:], gp)
        for k in range(16):
            P.dma("gpsimd", wo_sb[:, k, :], woT[k * 128:(k + 1) * 128, :], gp)
        w_c = (ld, ld.count)
        P.on("vector", lambda e: e.memset(prev_f[:], 0.0))
        P.on("vector", lambda e: e.memset(prev_b[:], 0.0))

        for ci in range(NCHS):
            for b in range(BQ):
                tok0 = b * L + ci * CH
                lt = ci * CH
                wpre = [(te, te.count), (ve, ve.count), (sct, sct.count), (gp, gp.count)]
                P.dma("sync", bfm[:], b_fm[:, tok0:tok0 + 128], ld, wait=wpre)
                P.dma("sync", cfm[:], c_fm[:, tok0:tok0 + 128], ld)
                P.dma("sync", btm[:], b_fm[:, tok0:tok0 + 128], ld, transpose=True)
                for ch in range(16):
                    P.dma("sync", xstm[:, ch, :], xs_fm[ch, :, tok0:tok0 + 128], ld, transpose=True)
                P.dma("sync", dtc_sb[:], dtc[tok0:tok0 + 128, :, :], ld)
                P.dma("sync", arow[0:1, :].rearrange("o (h t) -> o h t", h=32), ahm[b:b + 1, :, lt:lt + 128], ld)
                w_ld = (ld, ld.count)
                P.on("scalar", lambda e: e.activation(out=elrow[:], in_=arow[0:1, 127:4096:128], func=AF.Exp), wait=[w_ld])
                P.on("tensor", lambda e: e.matmul(psL[:, 0:32], on1_sb[:], elrow[:], start=True, stop=True),
                     wait=[(sct, sct.count), w_c, (ve, ve.count)], inc=te)
                P.on("vector", lambda e: e.tensor_copy(out=elcols[:], in_=psL[:, 0:32]), wait=[te])
                a3 = xctm[:].rearrange("p (h q) -> p h q", h=32)
                d3 = dtc_sb[:, 0, :].rearrange("p h -> p h 1" if False else "p (h o) -> p h o", o=1)
                aa, bb = _bc(xstm[:].rearrange("p a b -> p (a b)").rearrange("p (h q) -> p h q", h=32), d3)
                P.on("vector", lambda e, aa=aa, bb=bb, a3=a3: e.tensor_tensor(out=a3, in0=aa, in1=bb, op=ALU.mult), wait=[w_ld])
                w_xc = (ve, ve.count)
                P.on("tensor", lambda e: e.matmul(psCB[:], bfm[:], cfm[:], start=True, stop=True),
                     wait=[w_ld, (ve, ve.count)], inc=te)
                w_cb = (te, te.count)
                P.on("vector", lambda e: e.tensor_tensor(out=cbm_sb[:], in0=psCB[:], in1=msku_i[:], op=ALU.mult), wait=[w_cb])
                w_cbm = (ve, ve.count)
                xsfl = xstm[:].rearrange("p a b -> p (a b)")
                for hg in range(4):
                    for hh in range(8):
                        h = hg * 8 + hh
                        acol = dtc_sb[:, 1, h:h + 1]
                        P.on("tensor", lambda e, h=h: e.matmul(
                            psL[:], on1_sb[:], arow[0:1, h * 128:(h + 1) * 128], start=True, stop=True),
                            wait=[(ve, ve.count), (sct, sct.count)], inc=te)
                        P.on("vector", lambda e, acol=acol: e.tensor_scalar(
                            out=epre6[:], in0=psL[:], scalar1=acol, scalar2=0.0, op0=ALU.subtract, op1=ALU.min), wait=[te])
                        P.on("scalar", lambda e, hh=hh: e.activation(out=lh_sb[:], in_=epre6[:], func=AF.Exp), wait=[ve])
                        P.on("vector", lambda e, hh=hh: e.tensor_tensor(out=mh_sb[:, hh, :], in0=cbm_sb[:], in1=lh_sb[:], op=ALU.mult), wait=[sct, w_cbm])
                    wmh = (ve, ve.count)
                    for hh in range(8):
                        h = hg * 8 + hh
                        P.on("tensor", lambda e, h=h, hh=hh: e.matmul(
                            psD[:, hh * 64:(hh + 1) * 64], mh_sb[:, hh, :], xctm[:, h * 64:(h + 1) * 64], start=True, stop=True),
                            wait=([wmh, w_xc] if hh == 0 else ()), inc=(te if hh == 7 else None))
                        P.on("tensor", lambda e, h=h, hh=hh, b=b: e.matmul(
                            psO[:, hh * 64:(hh + 1) * 64], cfm[:], prev_b[:, b, h * 64:(h + 1) * 64], start=True, stop=True),
                            inc=(te if hh == 7 else None))
                    wdo = (te, te.count)
                    eap = dtc_sb[:, 2, hg * 8:(hg + 1) * 8].rearrange("p (h o) -> p h o", o=1)
                    dap = dbc_sb[:, hg * 8:(hg + 1) * 8].rearrange("p (h o) -> p h o", o=1)
                    sl = slice(hg * 512, (hg + 1) * 512)
                    a2, b2_ = _bc(psO[:].rearrange("p (h q) -> p h q", h=8), eap)
                    P.on("vector", lambda e, a2=a2, b2_=b2_: e.tensor_tensor(
                        out=t64[:].rearrange("p (h q) -> p h q", h=8), in0=a2, in1=b2_, op=ALU.mult), wait=[wdo])
                    P.on("vector", lambda e: e.tensor_tensor(out=y64[:], in0=t64[:], in1=psD[:], op=ALU.add))
                    a3, b3_ = _bc(xsfl[:, sl].rearrange("p (h q) -> p h q", h=8), dap)
                    P.on("vector", lambda e, a3=a3, b3_=b3_: e.tensor_tensor(
                        out=t64[:].rearrange("p (h q) -> p h q", h=8), in0=a3, in1=b3_, op=ALU.mult))
                    P.on("vector", lambda e, sl=sl: e.tensor_tensor(out=yt_sb[:, sl], in0=t64[:], in1=y64[:], op=ALU.add))
                wy = (ve, ve.count)
                P.dma("gpsimd", ytm[tok0:tok0 + 128, :], yt_sb[:], gp, wait=[wy])
                for hg in range(4):
                    for hh in range(8):
                        h = hg * 8 + hh
                        dscol = dtc_sb[:, 3, h:h + 1]
                        P.on("vector", lambda e, dscol=dscol: e.tensor_scalar(
                            out=bd_sb[:], in0=btm[:], scalar1=dscol, scalar2=None, op0=ALU.mult))
                        wbd = (ve, ve.count)
                        P.on("tensor", lambda e, h=h, hh=hh: e.matmul(
                            psS[:, hh * 64:(hh + 1) * 64], bd_sb[:], xctm[:, h * 64:(h + 1) * 64], start=True, stop=True),
                            wait=[wbd], inc=te)
                    wps = (te, te.count)
                    elap = elcols[:, hg * 8:(hg + 1) * 8].rearrange("p (h o) -> p h o", o=1)
                    sl = slice(hg * 512, (hg + 1) * 512)
                    a4, b4_ = _bc(prev_f[:, b, sl].rearrange("p (h q) -> p h q", h=8), elap)
                    P.on("vector", lambda e, a4=a4, b4_=b4_: e.tensor_tensor(
                        out=t64[:].rearrange("p (h q) -> p h q", h=8), in0=a4, in1=b4_, op=ALU.mult), wait=[wps])
                    P.on("vector", lambda e, sl=sl, b=b: e.tensor_tensor(
                        out=prev_f[:, b, sl], in0=t64[:], in1=psS[:], op=ALU.add))
                    P.on("vector", lambda e, sl=sl, b=b: e.tensor_copy(out=prev_b[:, b, sl], in_=prev_f[:, b, sl]))

        w_ssm = (gp, gp.count)
        for g in range(8):
            for dc in range(8):
                for st4 in range(4):
                    tok0 = g * 512 + st4 * 128
                    P.dma("sync", xnf6[:, dc, st4 * 128:(st4 + 1) * 128],
                          xn1_tm[tok0:tok0 + 128, dc * 128:(dc + 1) * 128], ld,
                          wait=([(te, te.count), (ve, ve.count)] if (dc == 0 and st4 == 0) else ()),
                          transpose=True)
            w_xn = (ld, ld.count)
            for ch in range(16):
                par = ch % 2
                for st4 in range(4):
                    tok0 = g * 512 + st4 * 128
                    P.dma("sync", yf_sb[:, par, st4 * 128:(st4 + 1) * 128],
                          ytm[tok0:tok0 + 128, ch * 128:(ch + 1) * 128], ld,
                          wait=([w_ssm, (ve, ve.count)] if st4 == 0 else ()), transpose=True)
                P.dma("gpsimd", zb6[:, par, :], z2_fm[ch, :, g * 512:(g + 1) * 512], gp,
                      wait=[(ve, ve.count), (sct, sct.count)])
                w_yz = [(ld, ld.count), (gp, gp.count)]
                P.on("scalar", lambda e, par=par: e.activation(out=sq6[:, 0, :], in_=zb6[:, par, :], func=AF.Silu),
                     wait=w_yz + [(ve, ve.count)])
                P.on("vector", lambda e, par=par, ch=ch: e.tensor_tensor(
                    out=yg_sb[:, ch, :], in0=yf_sb[:, par, :], in1=sq6[:, 0, :], op=ALU.mult), wait=[sct] + w_yz)
                P.on("vector", lambda e, ch=ch: e.tensor_tensor(
                    out=sq6[:, 1, :], in0=yg_sb[:, ch, :], in1=yg_sb[:, ch, :], op=ALU.mult))
                P.on("tensor", lambda e, ch=ch: e.matmul(
                    psR[:], onbf_sb[:], sq6[:, 1, :], start=(ch == 0), stop=(ch == 15), skip_group_check=True),
                    wait=[(ve, ve.count)], inc=te)
            w_sr = (te, te.count)
            P.on("vector", lambda e: e.tensor_scalar(out=srow6[:], in0=psR[:], scalar1=1.0 / 2048, scalar2=None, op0=ALU.mult), wait=[w_sr])
            P.on("vector", lambda e: e.tensor_scalar(out=srow6[:], in0=srow6[:], scalar1=1e-5, scalar2=None, op0=ALU.add))
            P.on("scalar", lambda e: e.activation(out=srow6[:], in_=srow6[:], func=AF.Ln), wait=[ve])
            P.on("scalar", lambda e: e.activation(out=srow6[:], in_=srow6[:], func=AF.Exp, scale=-0.5))
            P.on("tensor", lambda e: e.matmul(psB2[:], on1_sb[:], srow6[:], start=True, stop=True),
                 wait=[(sct, sct.count), (ve, ve.count)], inc=te)
            P.on("vector", lambda e: e.tensor_copy(out=br3[:], in_=psB2[:]), wait=[te])
            for ch in range(16):
                P.on("vector", lambda e, ch=ch: e.tensor_tensor(out=sq6[:, 0, :], in0=yg_sb[:, ch, :], in1=br3[:], op=ALU.mult))
                P.on("vector", lambda e, ch=ch: e.tensor_scalar(
                    out=yg_sb[:, ch, :], in0=sq6[:, 0, :], scalar1=rw_sb[:, ch:ch + 1], scalar2=None, op0=ALU.mult))
            w_yg = (ve, ve.count)
            for od in range(8):
                psY = [psB, psB2][od % 2]
                for k in range(16):
                    P.on("tensor", lambda e, od=od, k=k, psY=psY: e.matmul(
                        psY[:], wo_sb[:, k, od * 128:(od + 1) * 128], yg_sb[:, k, :],
                        start=(k == 0), stop=(k == 15), skip_group_check=True),
                        wait=([w_yg, (ve, ve.count)] if k == 0 else ()), inc=(te if k == 15 else None))
                w_om = (te, te.count)
                P.on("vector", lambda e, od=od, psY=psY: e.tensor_tensor(
                    out=so6[:, od % 2, :], in0=psY[:], in1=xnf6[:, od, :], op=ALU.add),
                    wait=[w_om, (gp, gp.count)])
                wso = (ve, ve.count)
                P.dma("gpsimd", s2_fm[od, :, g * 512:(g + 1) * 512], so6[:, od % 2, :], gp, wait=[wso])

        P.on("gpsimd", lambda e: e.engine_nop(), wait=[(gp, gp.count)], inc=None)
        P.play(block)
    return nc


def run_l1_core(l1f_results, wts, consts):
    nc = build_l1_core()
    base = {
        "dbc": _f32(np.broadcast_to(wts["L1_attn_D"], (128, 32))),
        "rmsw": _f32(np.asarray(wts["L1_attn_rms_w"]).reshape(16, 128).T),
        "woT": _f32(np.asarray(wts["L1_attn_out_proj_w"]).T),
        "ones1": _f32(np.ones((1, 128))),
        "onebf": _bf(np.ones((128, 1))),
        "msku_i_i": consts["mask_u_incl"],
    }
    in_maps = []
    for r in l1f_results:
        m = dict(base)
        for k in ("xs_fm", "b_fm", "c_fm", "z2_fm", "dtc", "ahm", "xn1_tm"):
            m[k] = r[k]
        in_maps.append(m)
    return run_bass_kernel_spmd(nc, in_maps, core_ids=list(range(NCORE))).results



def build_l1_ffn():
    nc = bass.Bass("TRN2", target_bir_lowering=False, debug=False)
    x1 = nc.dram_tensor("x1", [T, DM], F32, kind="ExternalInput").ap()
    s2_fm = nc.dram_tensor("s2_fm", [8, 128, T], BF16, kind="ExternalInput").ap()
    wg = nc.dram_tensor("wg", [2816, 1024], F32, kind="ExternalInput").ap()
    wu = nc.dram_tensor("wu", [2816, 1024], F32, kind="ExternalInput").ap()
    wd = nc.dram_tensor("wd", [1024, 2816], F32, kind="ExternalInput").ap()
    out_o = nc.dram_tensor("out_o", [T, DM], F32, kind="ExternalOutput").ap()
    xq_tm = nc.dram_tensor("xq_tm", [T, DM], BF16, kind="Internal").ap()
    wgq = nc.dram_tensor("wgq", [2816, 1024], BF16, kind="Internal").ap()
    wuq = nc.dram_tensor("wuq", [2816, 1024], BF16, kind="Internal").ap()
    wdq = nc.dram_tensor("wdq", [1024, 2816], BF16, kind="Internal").ap()
    h_fm = nc.dram_tensor("h_fm", [22, 128, T], BF16, kind="Internal").ap()
    hq_tm = nc.dram_tensor("hq_tm", [T, 2816], BF16, kind="Internal").ap()
    ff_fm = nc.dram_tensor("ff_fm", [8, 128, T], BF16, kind="Internal").ap()

    with contextlib.ExitStack() as _es:
        S = lambda n, sh, dt: _es.enter_context(nc.sbuf_tensor(n, sh, dt))
        wgT = S("wgT", [128, 8, 2816], BF16)
        wuT = S("wuT", [128, 8, 2816], BF16)
        wdT = S("wdT", [128, 22, 1024], BF16)
        qsrc = S("qsrc", [128, 2816], F32)
        qt1 = S("qt1", [128, 2816], F32)
        qt2 = S("qt2", [128, 2816], F32)
        qbm = S("qbm", [128, 176], F32)
        qrs = S("qrs", [128, 176], F32)
        qdst = S("qdst", [128, 2816], BF16)
        cols = S("cols7", [128, 16], F32)
        hfb = S("hfb", [128, 2, 512], BF16)
        htm = S("htm", [128, 22, 128], BF16)
        hq_fm = S("hq_fm", [128, 22, 512], BF16)
        ffb = S("ffb", [128, 2, 512], BF16)
        xf7 = S("xf7", [128, DM], F32)
        big0 = _es.enter_context(nc.psum_tensor("p70", [128, 512], F32))
        big1 = _es.enter_context(nc.psum_tensor("p71", [128, 512], F32))
        s_ld = _es.enter_context(nc.semaphore("s_ld"))
        s_gp = _es.enter_context(nc.semaphore("s_gp"))
        s_te = _es.enter_context(nc.semaphore("s_te"))
        s_ve = _es.enter_context(nc.semaphore("s_ve"))
        s_sc = _es.enter_context(nc.semaphore("s_sc"))
        block = _es.enter_context(nc.Block())
        s_pc = _es.enter_context(nc.semaphore("s_pc"))
        ld = Trk(s_ld); gp = Trk(s_gp); te = Trk(s_te); ve = Trk(s_ve); sct = Trk(s_sc)
        pc = Trk(s_pc)
        P = Prog(nc, self_trks={"vector": ve, "scalar": sct})

        def emit_qdq(srcap, dstap, W, wait0=()):
            NB = W // 16
            halves = [("vector", ve, 0, W, 0, NB)]
            def both(fn, w=()):
                for engname, trk, c0, c1, b0, b1 in halves:
                    if c1 <= c0:
                        continue
                    P.on(engname, fn(c0, c1, b0, b1), wait=list(w), inc=(None if engname == "vector" else trk))
            both(lambda c0, c1, b0, b1: (lambda e: e.scalar_tensor_tensor(
                out=qt1[:, c0:c1], in0=srcap[:, c0:c1], scalar=-1.0, in1=srcap[:, c0:c1],
                op0=ALU.mult, op1=ALU.max)), w=list(wait0))
            P.on("vector", lambda e, W=W, NB=NB: e.tensor_reduce(
                out=qbm[:, 0:NB], in_=qt1[:, 0:W].rearrange("p (n b) -> p n b", b=16),
                axis=AX.X, op=ALU.max))
            P.on("vector", lambda e: e.tensor_scalar(out=qbm[:, 0:NB], in0=qbm[:, 0:NB], scalar1=1.0 / 6.0, scalar2=1e-12, op0=ALU.mult, op1=ALU.max))
            P.on("vector", lambda e: e.reciprocal(out=qrs[:, 0:NB], in_=qbm[:, 0:NB]))
            wrs = (ve, ve.count)
            def bcm(dst, srct, colt):
                def mk(c0, c1, b0, b1):
                    a2, b2_ = _bc(srct[:, c0:c1].rearrange("p (n b) -> p n b", b=16),
                                  colt[:, b0:b1].rearrange("p (n o) -> p n o", o=1))
                    return lambda e: e.tensor_tensor(out=dst[:, c0:c1].rearrange("p (n b) -> p n b", b=16), in0=a2, in1=b2_, op=ALU.mult)
                return mk
            both(bcm(qt2, srcap, qrs), w=[wrs])
            both(lambda c0, c1, b0, b1: (lambda e: e.tensor_scalar(out=qsrc[:, c0:c1], in0=qt2[:, c0:c1], scalar1=2.249, scalar2=-2.249, op0=ALU.min, op1=ALU.max)))
            both(lambda c0, c1, b0, b1: (lambda e: e.tensor_scalar(out=qsrc[:, c0:c1], in0=qsrc[:, c0:c1], scalar1=2.0, scalar2=MAGIC, op0=ALU.mult, op1=ALU.add)))
            both(lambda c0, c1, b0, b1: (lambda e: e.tensor_scalar(out=qsrc[:, c0:c1], in0=qsrc[:, c0:c1], scalar1=-MAGIC, scalar2=None, op0=ALU.add)))
            both(lambda c0, c1, b0, b1: (lambda e: e.tensor_scalar(out=qsrc[:, c0:c1], in0=qsrc[:, c0:c1], scalar1=0.5, scalar2=None, op0=ALU.mult)))
            for th in (2.5, 3.5, 5.0, 5.0):
                both(lambda c0, c1, b0, b1, th=th: (lambda e: e.scalar_tensor_tensor(
                    out=qsrc[:, c0:c1], in0=qt2[:, c0:c1], scalar=th, in1=qsrc[:, c0:c1], op0=ALU.is_gt, op1=ALU.add)))
            both(lambda c0, c1, b0, b1: (lambda e: e.tensor_scalar(out=qt1[:, c0:c1], in0=qt2[:, c0:c1], scalar1=-1.0, scalar2=None, op0=ALU.mult)))
            both(lambda c0, c1, b0, b1: (lambda e: e.tensor_scalar(out=qt2[:, c0:c1], in0=qt1[:, c0:c1], scalar1=2.5, scalar2=None, op0=ALU.is_gt)))
            for th in (3.5, 5.0, 5.0):
                both(lambda c0, c1, b0, b1, th=th: (lambda e: e.scalar_tensor_tensor(
                    out=qt2[:, c0:c1], in0=qt1[:, c0:c1], scalar=th, in1=qt2[:, c0:c1], op0=ALU.is_gt, op1=ALU.add)))
            both(lambda c0, c1, b0, b1: (lambda e: e.tensor_tensor(out=qsrc[:, c0:c1], in0=qsrc[:, c0:c1], in1=qt2[:, c0:c1], op=ALU.subtract)))
            both(bcm(dstap, qsrc, qbm))
            return [(ve, ve.count), (pc, pc.count)]

        wq_guard = {}
        for (wname, wsrc, wdst, ntile, wid) in (("wg", wg, wgq, 22, 1024), ("wu", wu, wuq, 22, 1024), ("wd", wd, wdq, 8, 2816)):
            for ti in range(ntile):
                P.dma("gpsimd", qsrc[:, 0:wid], wsrc[ti * 128:(ti + 1) * 128, :], gp,
                      wait=[(ve, ve.count)])
                wq = emit_qdq(qsrc[:, 0:wid], qdst[:, 0:wid], wid, wait0=[(gp, gp.count)])
                P.dma("gpsimd", wdst[ti * 128:(ti + 1) * 128, :], qdst[:, 0:wid], gp, wait=wq)
            wq_guard[wname] = (gp, gp.count)
            if wname == "wg":
                for k in range(8):
                    for oc in range(22):
                        P.dma("sync", wgT[:, k, oc * 128:(oc + 1) * 128],
                              wgq[oc * 128:(oc + 1) * 128, k * 128:(k + 1) * 128], ld,
                              wait=([wq_guard["wg"]] if (k == 0 and oc == 0) else ()), transpose=True)
            elif wname == "wu":
                for k in range(8):
                    for oc in range(22):
                        P.dma("sync", wuT[:, k, oc * 128:(oc + 1) * 128],
                              wuq[oc * 128:(oc + 1) * 128, k * 128:(k + 1) * 128], ld,
                              wait=([wq_guard["wu"]] if (k == 0 and oc == 0) else ()), transpose=True)
            else:
                for k in range(22):
                    for oc in range(8):
                        P.dma("sync", wdT[:, k, oc * 128:(oc + 1) * 128],
                              wdq[oc * 128:(oc + 1) * 128, k * 128:(k + 1) * 128], ld,
                              wait=([wq_guard["wd"]] if (k == 0 and oc == 0) else ()), transpose=True)
        w_wT = (ld, ld.count)
        xq_guard = {}
        h_guard = {}
        hq_guard = {}
        def c(i):
            return cols[:, i:i + 1]
        for i in range(32):
            P.dma("gpsimd", xf7[:], x1[i * 128:(i + 1) * 128, :], gp, wait=[(ve, ve.count)])
            wx = (gp, gp.count)
            P.on("vector", lambda e: e.tensor_scalar(
                out=qt1[:, 0:DM], in0=xf7[:], scalar1=1.0, scalar2=None,
                op0=ALU.mult, op1=ALU.add, accum_out=c(0)), wait=[wx])
            P.on("vector", lambda e: e.scalar_tensor_tensor(
                out=qt2[:, 0:DM], in0=xf7[:], scalar=1.0, in1=xf7[:],
                op0=ALU.bypass, op1=ALU.mult, accum_out=c(1)))
            P.on("vector", lambda e: e.tensor_scalar(out=c(2), in0=c(0), scalar1=1.0 / DM, scalar2=None, op0=ALU.mult))
            P.on("vector", lambda e: e.tensor_scalar(out=c(3), in0=c(1), scalar1=1.0 / DM, scalar2=None, op0=ALU.mult))
            P.on("vector", lambda e: e.tensor_scalar(out=c(4), in0=c(2), scalar1=c(2), scalar2=None, op0=ALU.mult))
            P.on("vector", lambda e: e.tensor_tensor(out=c(5), in0=c(3), in1=c(4), op=ALU.subtract))
            P.on("vector", lambda e: e.tensor_scalar(out=c(6), in0=c(5), scalar1=1e-5, scalar2=None, op0=ALU.add))
            P.on("scalar", lambda e: e.activation(out=c(7), in_=c(6), func=AF.Ln), wait=[ve])
            P.on("scalar", lambda e: e.activation(out=c(8), in_=c(7), func=AF.Exp, scale=-0.5))
            P.on("vector", lambda e: e.tensor_scalar(out=qsrc[:, 0:DM], in0=xf7[:], scalar1=c(2), scalar2=c(8), op0=ALU.subtract, op1=ALU.mult), wait=[sct])
            wq = emit_qdq(qsrc[:, 0:DM], qdst[:, 0:DM], DM)
            P.dma("gpsimd", xq_tm[i * 128:(i + 1) * 128, :], qdst[:, 0:DM], gp, wait=wq)
            xq_guard[i] = (gp, gp.count)
        w_b = (gp, gp.count)
        for g in range(8):
            for dc in range(8):
                for st4 in range(4):
                    tok0 = g * 512 + st4 * 128
                    P.dma("sync", hq_fm[:, dc, st4 * 128:(st4 + 1) * 128],
                          xq_tm[tok0:tok0 + 128, dc * 128:(dc + 1) * 128], ld,
                          wait=([xq_guard[g * 4 + 3], (te, te.count), (ve, ve.count)] if (dc == 0 and st4 == 0) else ()),
                          transpose=True)
            w_xq = (ld, ld.count)
            for oc in range(22):
                for k in range(8):
                    P.on("tensor", lambda e, k=k, oc=oc: e.matmul(
                        big0[:], wgT[:, k, oc * 128:(oc + 1) * 128], hq_fm[:, k, :],
                        start=(k == 0), stop=(k == 7)),
                        wait=([w_xq, w_wT, (sct, sct.count), (ve, ve.count)] if k == 0 else ()),
                        inc=(te if k == 7 else None))
                for k in range(8):
                    P.on("tensor", lambda e, k=k, oc=oc: e.matmul(
                        big1[:], wuT[:, k, oc * 128:(oc + 1) * 128], hq_fm[:, k, :],
                        start=(k == 0), stop=(k == 7)), inc=(te if k == 7 else None))
                wmm = (te, te.count)
                P.on("scalar", lambda e: e.activation(out=qdst[:, 0:512], in_=big0[:], func=AF.Silu), wait=[wmm, (ve, ve.count)])
                P.on("vector", lambda e: e.tensor_copy(out=qdst[:, 512:1024], in_=big1[:]), wait=[wmm])
                P.on("vector", lambda e, oc=oc: e.tensor_tensor(out=hfb[:, oc % 2, :], in0=qdst[:, 0:512], in1=qdst[:, 512:1024], op=ALU.mult),
                     wait=[sct, (gp, gp.count)])
                whf = (ve, ve.count)
                P.dma("gpsimd", h_fm[oc, :, g * 512:(g + 1) * 512], hfb[:, oc % 2, :], gp, wait=[whf])
            h_guard[g] = (gp, gp.count)
        w_h = (gp, gp.count)
        for i in range(32):
            for ch in range(22):
                P.dma("sync", htm[:, ch, :], h_fm[ch, :, i * 128:(i + 1) * 128], ld,
                      wait=([h_guard[i // 4], (ve, ve.count)] if ch == 0 else ()), transpose=True)
            w_ht = (ld, ld.count)
            P.on("vector", lambda e: e.tensor_copy(out=qsrc[:], in_=htm[:].rearrange("p a b -> p (a b)")), wait=[w_ht])
            wq = emit_qdq(qsrc[:], qdst[:], 2816)
            P.dma("gpsimd", hq_tm[i * 128:(i + 1) * 128, :], qdst[:], gp, wait=wq)
            hq_guard[i] = (gp, gp.count)
        w_hq = (gp, gp.count)
        for g in range(8):
            for ch in range(22):
                for st4 in range(4):
                    tok0 = g * 512 + st4 * 128
                    P.dma("sync", hq_fm[:, ch, st4 * 128:(st4 + 1) * 128],
                          hq_tm[tok0:tok0 + 128, ch * 128:(ch + 1) * 128], ld,
                          wait=([hq_guard[g * 4 + 3], (te, te.count)] if (ch == 0 and st4 == 0) else ()),
                          transpose=True)
            w_hf = (ld, ld.count)
            for od in range(8):
                psY = [big0, big1][od % 2]
                for k in range(22):
                    P.on("tensor", lambda e, od=od, k=k, psY=psY: e.matmul(
                        psY[:], wdT[:, k, od * 128:(od + 1) * 128], hq_fm[:, k, :],
                        start=(k == 0), stop=(k == 21)),
                        wait=([w_hf, (ve, ve.count)] if k == 0 else ()), inc=(te if k == 21 else None))
                wdm = (te, te.count)
                P.on("vector", lambda e, od=od, psY=psY: e.tensor_copy(out=ffb[:, od % 2, :], in_=psY[:]),
                     wait=[wdm, (gp, gp.count)])
                wff = (ve, ve.count)
                P.dma("gpsimd", ff_fm[od, :, g * 512:(g + 1) * 512], ffb[:, od % 2, :], gp, wait=[wff])
        w_ff = (gp, gp.count)
        for i in range(32):
            for dc in range(8):
                P.dma("sync", htm[:, dc, :], ff_fm[dc, :, i * 128:(i + 1) * 128], ld,
                      wait=([w_ff, (ve, ve.count), (sct, sct.count)] if dc == 0 else ()), transpose=True)
            for dc in range(8):
                P.dma("sync", htm[:, 8 + dc, :], s2_fm[dc, :, i * 128:(i + 1) * 128], ld, transpose=True)
            P.dma("gpsimd", xf7[:], x1[i * 128:(i + 1) * 128, :], gp, wait=[(ve, ve.count)])
            w_f = (ld, ld.count)
            w_x = (gp, gp.count)
            P.on("scalar", lambda e: e.activation(out=qdst[:, 0:DM], in_=htm[:, 0:8, :].rearrange("p a b -> p (a b)"), func=AF.Silu),
                 wait=[w_f])
            P.on("vector", lambda e: e.tensor_tensor(out=qt2[:, 0:DM], in0=htm[:, 8:16, :].rearrange("p a b -> p (a b)"), in1=qdst[:, 0:DM], op=ALU.mult),
                 wait=[sct, w_f])
            P.on("vector", lambda e: e.tensor_tensor(out=qt2[:, 0:DM], in0=qt2[:, 0:DM], in1=xf7[:], op=ALU.add), wait=[w_x])
            wo7 = (ve, ve.count)
            P.dma("gpsimd", out_o[i * 128:(i + 1) * 128, :], qt2[:, 0:DM], gp, wait=[wo7])

        P.on("gpsimd", lambda e: e.engine_nop(), wait=[(gp, gp.count)], inc=None)
        P.play(block)
    return nc


def run_l1_ffn(x1_list, l1c_results, wts):
    nc = build_l1_ffn()
    base = {
        "wg": _f32(wts["L1_ffn_w_gate"]),
        "wu": _f32(wts["L1_ffn_w_up"]),
        "wd": _f32(wts["L1_ffn_w_down"]),
    }
    in_maps = []
    for x1c, cr in zip(x1_list, l1c_results):
        m = dict(base)
        m["x1"] = _f32(x1c)
        m["s2_fm"] = cr["s2_fm"]
        in_maps.append(m)
    return run_bass_kernel_spmd(nc, in_maps, core_ids=list(range(NCORE))).results



def kernel(**inputs):
    consts = make_consts()
    x = np.asarray(inputs["L0_attn_x"])
    per_core = [np.ascontiguousarray(x[2 * i:2 * i + 2].reshape(T, DM)) for i in range(NCORE)]
    fr = run_l0_front(per_core, inputs, consts)
    cr = run_l0_core(fr, consts)
    cb = run_l0_comb(fr, cr, inputs, consts)
    f4 = run_l0_ffn(fr, cb, per_core, inputs, consts)
    x1_list = [np.asarray(r["x1_o"]) for r in f4]
    l1f = run_l1_front(x1_list, inputs, consts)
    l1c = run_l1_core(l1f, inputs, consts)
    l1o = run_l1_ffn(x1_list, l1c, inputs)
    out = np.concatenate([np.asarray(r["out_o"]).reshape(2, L, DM) for r in l1o], axis=0)
    return out.astype(np.float32)


# version 04
# speedup vs baseline: 1.1760x; 1.1760x over previous
import contextlib

import numpy as np
import ml_dtypes
import concourse.bass as bass
import concourse.mybir as mybir
from concourse.bass_utils import run_bass_kernel_spmd

F32 = mybir.dt.float32
BF16 = mybir.dt.bfloat16
AF = mybir.ActivationFunctionType
ALU = mybir.AluOpType
AX = mybir.AxisListType

NCORE = 8
BQ = 2
L = 2048
T = BQ * L
DM = 1024
CH = 128
NCHS = L // CH

PI = float(np.pi)
MAGIC = 12582912.0

BF = ml_dtypes.bfloat16


def _bf(a):
    return np.ascontiguousarray(np.asarray(a)).astype(BF)


def _f32(a):
    return np.ascontiguousarray(np.asarray(a, dtype=np.float32))


class Trk:
    def __init__(self, sem):
        self.sem = sem
        self.count = 0


class Prog:

    ENGS = ("sync", "gpsimd", "tensor", "vector", "scalar")

    def __init__(self, nc, self_trks=None):
        self.nc = nc
        self.ops = {e: [] for e in self.ENGS}
        self.self_trks = self_trks or {}

    def on(self, eng, fn, wait=(), inc=None, inc_n=1):
        ws = []
        strk = self.self_trks.get(eng)
        if strk is not None:
            if inc is None:
                inc = strk
            if strk.count > 0:
                ws.append((strk, strk.count))
        for w in wait:
            if isinstance(w, tuple):
                ws.append((w[0], w[1]))
            else:
                ws.append((w, w.count))
        self.ops[eng].append((ws, fn, inc, inc_n))
        if inc is not None:
            inc.count += inc_n
        return inc.count if inc is not None else None

    def dma(self, eng, out, in_, trk, wait=(), **kw):
        self.on(eng, lambda e, o=out, i=in_, kw=kw: e.dma_start(out=o, in_=i, **kw),
                wait=wait, inc=trk, inc_n=16)

    def play(self, block):
        for engname in self.ENGS:
            ops = self.ops[engname]

            def body(eng, ops=ops):
                lastw = {}
                for ws, fn, inc, inc_n in ops:
                    for (t, v) in ws:
                        if v > 0 and lastw.get(id(t), 0) < v:
                            eng.wait_ge(t.sem, v)
                            lastw[id(t)] = v
                    ins = fn(eng)
                    if inc is not None:
                        ins.then_inc(inc.sem, inc_n)

            getattr(block, engname)(body)


def make_consts():
    c = {}
    c["eye_bf"] = _bf(np.eye(128))
    c["eye_f32"] = _f32(np.eye(128))
    c["triu_incl"] = _f32(np.triu(np.ones((128, 128))))
    c["tril_strict"] = _f32(np.tril(np.ones((128, 128)), -1))
    c["mask_u_strict_neg"] = _bf(-np.triu(np.ones((128, 128)), 1))
    c["mask_l_strict_neg"] = _bf(-np.tril(np.ones((128, 128)), -1))
    c["mask_u_incl"] = _bf(np.triu(np.ones((128, 128))))
    c["freqs"] = _f32(np.broadcast_to(1.0 / 10000.0 ** (np.arange(16) / 16.0), (128, 16)))
    return c



def build_l0_front():
    nc = bass.Bass("TRN2", target_bir_lowering=False, debug=False)
    x = nc.dram_tensor("x", [T, DM], F32, kind="ExternalInput").ap()
    w_zxg = nc.dram_tensor("w_zxg", [DM, 4096], F32, kind="ExternalInput").ap()
    w_kv = nc.dram_tensor("w_kv", [DM, 2080], F32, kind="ExternalInput").ap()
    b_zxg = nc.dram_tensor("b_zxg", [128, 32], F32, kind="ExternalInput").ap()
    b_kv = nc.dram_tensor("b_kv", [128, 2080], F32, kind="ExternalInput").ap()
    rmsw = nc.dram_tensor("rmsw", [128, 8], F32, kind="ExternalInput").ap()
    convw = nc.dram_tensor("convw", [128, 64], F32, kind="ExternalInput").ap()
    convb = nc.dram_tensor("convb", [128, 16], F32, kind="ExternalInput").ap()
    gatew = nc.dram_tensor("gatew", [2048, 112], F32, kind="ExternalInput").ap()
    gateb = nc.dram_tensor("gateb", [128, 112], F32, kind="ExternalInput").ap()
    ldsb = nc.dram_tensor("ldsb", [128, 16], F32, kind="ExternalInput").ap()
    freqs = nc.dram_tensor("freqs", [128, 16], F32, kind="ExternalInput").ap()
    qw = nc.dram_tensor("qw", [2048, 2048], F32, kind="ExternalInput").ap()
    eye32 = nc.dram_tensor("eye32", [128, 128], F32, kind="ExternalInput").ap()
    triu_i = nc.dram_tensor("triu_i", [128, 128], F32, kind="ExternalInput").ap()
    tril_s = nc.dram_tensor("tril_s", [128, 128], F32, kind="ExternalInput").ap()

    xn_tm = nc.dram_tensor("xn_tm", [T, DM], BF16, kind="ExternalOutput").ap()
    xr0_tm = nc.dram_tensor("xr0_tm", [T, DM], BF16, kind="ExternalOutput").ap()
    z_fm = nc.dram_tensor("z_fm", [16, 128, T], BF16, kind="ExternalOutput").ap()
    xc_fm = nc.dram_tensor("xc_fm", [16, 128, T], BF16, kind="ExternalOutput").ap()
    ktm = nc.dram_tensor("ktm", [T, 2048], BF16, kind="ExternalOutput").ap()
    qtm = nc.dram_tensor("qtm", [T, 2048], BF16, kind="ExternalOutput").ap()
    vt_o = nc.dram_tensor("vt_o", [T, 32], F32, kind="ExternalOutput").ap()
    gcols = nc.dram_tensor("gcols", [T, 9, 16], F32, kind="ExternalOutput").ap()
    ghm = nc.dram_tensor("ghm", [BQ, 3, 16, L], F32, kind="ExternalOutput").ap()

    with contextlib.ExitStack() as _es:
        wzs = _es.enter_context(nc.sbuf_tensor("wzs", [128, 2, 8, 1024], BF16))
        wkv_sb = _es.enter_context(nc.sbuf_tensor("wkv_sb", [128, 8, 2080], BF16))
        gw_sb = _es.enter_context(nc.sbuf_tensor("gw_sb", [128, 16, 112], BF16))
        qwb = _es.enter_context(nc.sbuf_tensor("qwb", [128, 2, 16, 512], BF16))
        diag_sb = _es.enter_context(nc.sbuf_tensor("diag_sb", [128, 64, 128], BF16))
        bz_sb = _es.enter_context(nc.sbuf_tensor("bz_sb", [128, 32], F32))
        bkv_sb = _es.enter_context(nc.sbuf_tensor("bkv_sb", [128, 2080], F32))
        rw_sb = _es.enter_context(nc.sbuf_tensor("rw_sb", [128, 8], F32))
        cw_sb = _es.enter_context(nc.sbuf_tensor("cw_sb", [128, 64], F32))
        cb_sb = _es.enter_context(nc.sbuf_tensor("cb_sb", [128, 16], F32))
        gb_sb = _es.enter_context(nc.sbuf_tensor("gb_sb", [128, 112], F32))
        lds_sb = _es.enter_context(nc.sbuf_tensor("lds_sb", [128, 16], F32))
        fq_sb = _es.enter_context(nc.sbuf_tensor("fq_sb", [128, 16], F32))
        eye_sb = _es.enter_context(nc.sbuf_tensor("eye_sb", [128, 128], F32))
        triu_sb = _es.enter_context(nc.sbuf_tensor("triu_sb", [128, 128], F32))
        tril_sb = _es.enter_context(nc.sbuf_tensor("tril_sb", [128, 128], F32))
        xt = _es.enter_context(nc.sbuf_tensor("xt", [128, 2, DM], BF16))
        cols = _es.enter_context(nc.sbuf_tensor("cols", [128, 16], F32))
        xnt = _es.enter_context(nc.sbuf_tensor("xnt", [128, 2, DM], BF16))
        xrt = _es.enter_context(nc.sbuf_tensor("xrt", [128, 2, DM], BF16))
        xr_fm = _es.enter_context(nc.sbuf_tensor("xr_fm", [128, 8, 512], BF16))
        xg_buf = _es.enter_context(nc.sbuf_tensor("xg_buf", [128, 16, 515], BF16))
        xc_sb = _es.enter_context(nc.sbuf_tensor("xc_sb", [128, 16, 512], BF16))
        z_sb = _es.enter_context(nc.sbuf_tensor("z_sb", [128, 2, 512], BF16))
        kv_buf = _es.enter_context(nc.sbuf_tensor("kv_buf", [128, 2080], F32))
        vsave = _es.enter_context(nc.sbuf_tensor("vsave", [128, 4, 32], F32))
        sq_sb = _es.enter_context(nc.sbuf_tensor("sq_sb", [128, 2048], F32))
        nrm_sb = _es.enter_context(nc.sbuf_tensor("nrm_sb", [128, 16], F32))
        kq_bf = _es.enter_context(nc.sbuf_tensor("kq_bf", [128, 2048], BF16))
        qtmp = _es.enter_context(nc.sbuf_tensor("qtmp", [128, 512], F32))
        gt = _es.enter_context(nc.sbuf_tensor("gt", [128, 112], F32))
        gsc = _es.enter_context(nc.sbuf_tensor("gsc", [128, 40, 16], F32))
        stage = _es.enter_context(nc.sbuf_tensor("stage", [128, 9, 16], F32))
        vt_sb = _es.enter_context(nc.sbuf_tensor("vt_sb", [128, 32], F32))
        ghm_sb = _es.enter_context(nc.sbuf_tensor("ghm_sb", [48, 128], F32))
        ps0 = _es.enter_context(nc.psum_tensor("ps0", [128, 512], F32))
        ps1 = _es.enter_context(nc.psum_tensor("ps1", [128, 512], F32))
        psg = _es.enter_context(nc.psum_tensor("psg", [128, 112], F32))
        psc = _es.enter_context(nc.psum_tensor("psc", [128, 64], F32))
        pst = _es.enter_context(nc.psum_tensor("pst", [48, 128], F32))
        s_ld = _es.enter_context(nc.semaphore("s_ld"))
        s_gp = _es.enter_context(nc.semaphore("s_gp"))
        s_te = _es.enter_context(nc.semaphore("s_te"))
        s_ve = _es.enter_context(nc.semaphore("s_ve"))
        s_sc = _es.enter_context(nc.semaphore("s_sc"))
        block = _es.enter_context(nc.Block())
        ld = Trk(s_ld); gp = Trk(s_gp); te = Trk(s_te); ve = Trk(s_ve); sct = Trk(s_sc)
        P = Prog(nc, self_trks={"vector": ve, "scalar": sct})

        for ap_, t_ in ((b_zxg, bz_sb), (b_kv, bkv_sb), (rmsw, rw_sb), (convw, cw_sb),
                        (convb, cb_sb), (gateb, gb_sb), (ldsb, lds_sb), (freqs, fq_sb),
                        (eye32, eye_sb), (triu_i, triu_sb), (tril_s, tril_sb)):
            P.dma("sync", t_[:], ap_[:], ld)
        w_consts = (ld, ld.count)

        for k in range(8):
            P.dma("gpsimd", wkv_sb[:, k, :], w_kv[k * 128:(k + 1) * 128, :], gp)
        for k in range(16):
            P.dma("gpsimd", gw_sb[:, k, :], gatew[k * 128:(k + 1) * 128, :], gp)
        w_weights = (gp, gp.count)

        for fc in range(16):
            for j in range(4):
                P.on("vector", lambda e, fc=fc, j=j: e.tensor_scalar(
                    out=diag_sb[:, fc * 4 + j, :], in0=eye_sb[:],
                    scalar1=cw_sb[:, j * 16 + fc:j * 16 + fc + 1], scalar2=None, op0=ALU.mult),
                    wait=[w_consts], inc=ve)
        P.on("scalar", lambda e: e.activation(out=lds_sb[:], in_=lds_sb[:], func=AF.Exp),
             wait=[w_consts], inc=sct)
        w_diag = (ve, ve.count)

        def c(i):
            return cols[:, i:i + 1]

        p0_parity_guard = [None, None]
        for i in range(32):
            par = i % 2
            wl = [p0_parity_guard[par]] if p0_parity_guard[par] is not None else []
            P.dma("gpsimd", xt[:, par, :], x[i * 128:(i + 1) * 128, :], gp, wait=wl)
            wx = (gp, gp.count)
            xti = xt[:, par, :]
            P.on("vector", lambda e, xti=xti, par=par: e.tensor_scalar(
                out=xnt[:, par, :], in0=xti, scalar1=1.0, scalar2=None,
                op0=ALU.mult, op1=ALU.add, accum_out=c(0)), wait=[wx], inc=ve)
            P.on("vector", lambda e, xti=xti, par=par: e.scalar_tensor_tensor(
                out=xrt[:, par, :], in0=xti, scalar=1.0, in1=xti,
                op0=ALU.bypass, op1=ALU.mult, accum_out=c(1)), inc=ve)
            P.on("vector", lambda e: e.tensor_scalar(out=c(2), in0=c(0), scalar1=1.0 / DM, scalar2=None, op0=ALU.mult), inc=ve)
            P.on("vector", lambda e: e.tensor_scalar(out=c(3), in0=c(1), scalar1=1.0 / DM, scalar2=None, op0=ALU.mult), inc=ve)
            P.on("vector", lambda e: e.tensor_scalar(out=c(4), in0=c(2), scalar1=c(2), scalar2=None, op0=ALU.mult), inc=ve)
            P.on("vector", lambda e: e.tensor_tensor(out=c(5), in0=c(3), in1=c(4), op=ALU.subtract), inc=ve)
            P.on("vector", lambda e: e.tensor_scalar(out=c(6), in0=c(5), scalar1=1e-5, scalar2=None, op0=ALU.add), inc=ve)
            P.on("scalar", lambda e: e.activation(out=c(7), in_=c(6), func=AF.Ln), wait=[ve], inc=sct)
            P.on("scalar", lambda e: e.activation(out=c(8), in_=c(7), func=AF.Exp, scale=-0.5), inc=sct)
            P.on("vector", lambda e: e.reciprocal(out=c(9), in_=c(6)), wait=[sct], inc=ve)
            P.on("vector", lambda e: e.tensor_tensor(out=c(10), in0=c(5), in1=c(9), op=ALU.mult), inc=ve)
            P.on("vector", lambda e: e.tensor_scalar(out=c(11), in0=c(10), scalar1=1e-6, scalar2=None, op0=ALU.add), inc=ve)
            P.on("scalar", lambda e: e.activation(out=c(12), in_=c(11), func=AF.Ln), wait=[ve], inc=sct)
            P.on("scalar", lambda e: e.activation(out=c(13), in_=c(12), func=AF.Exp, scale=-0.5), inc=sct)
            P.on("vector", lambda e: e.engine_nop(), wait=[sct], inc=ve)
            P.on("vector", lambda e: e.tensor_tensor(out=c(14), in0=c(8), in1=c(13), op=ALU.mult), inc=ve)
            P.on("vector", lambda e, xti=xti, par=par: e.tensor_scalar(
                out=xnt[:, par, :], in0=xti, scalar1=c(2), scalar2=c(8), op0=ALU.subtract, op1=ALU.mult), inc=ve)
            P.on("vector", lambda e, xti=xti, par=par: e.tensor_scalar(
                out=xrt[:, par, :], in0=xti, scalar1=c(2), scalar2=c(14), op0=ALU.subtract, op1=ALU.mult), inc=ve)
            wv = (ve, ve.count)
            P.dma("gpsimd", xn_tm[i * 128:(i + 1) * 128, :], xnt[:, par, :], gp, wait=[wv])
            P.dma("gpsimd", xr0_tm[i * 128:(i + 1) * 128, :], xrt[:, par, :], gp, wait=[wv])
            p0_parity_guard[par] = (gp, gp.count)
        w_p0_done = (gp, gp.count)

        psmain = [ps0, ps1]
        ps_guard = [None, None]
        psi = [0]

        def next_ps():
            i = psi[0] % 2
            psi[0] += 1
            return psmain[i], i

        def mm_acc(lhsTs, rhss, n, wait_extra=()):
            ps, i = next_ps()
            w = list(wait_extra)
            if ps_guard[i] is not None:
                w.append(ps_guard[i])
            for j in range(n):
                P.on("tensor", lambda e, a=lhsTs(j), b=rhss(j), ps=ps, j=j, n=n: e.matmul(
                    ps[:, 0:b.shape[-1]], a, b, start=(j == 0), stop=(j == n - 1)),
                    wait=(w if j == 0 else ()), inc=(te if j == n - 1 else None))
            return ps, i, (te, te.count)

        w_xr_ld = {}
        for g in range(8):
            wprev = [(te, te.count), (ve, ve.count)] if g > 0 else [w_p0_done]
            if g == 0:
                wprev = [w_p0_done]
            for dc in range(8):
                for st4 in range(4):
                    tok0 = g * 512 + st4 * 128
                    P.dma("sync", xr_fm[:, dc, st4 * 128:(st4 + 1) * 128],
                          xr0_tm[tok0:tok0 + 128, dc * 128:(dc + 1) * 128], ld,
                          wait=(wprev if (dc == 0 and st4 == 0) else ()), transpose=True)
            w_xr = (ld, ld.count)
            for dc in range(8):
                P.on("vector", lambda e, dc=dc: e.tensor_scalar(
                    out=xr_fm[:, dc, :], in0=xr_fm[:, dc, :], scalar1=rw_sb[:, dc:dc + 1],
                    scalar2=None, op0=ALU.mult), wait=[w_xr, w_weights], inc=ve)
            w_rms = (ve, ve.count)

            if g % 4 == 0:
                P.on("vector", lambda e: e.memset(xg_buf[:, :, 0:3], 0.0),
                     wait=[(te, te.count)], inc=ve)
            else:
                P.on("vector", lambda e: e.tensor_copy(
                    out=xg_buf[:, :, 0:3], in_=xg_buf[:, :, 512:515]),
                    wait=[(te, te.count)], inc=ve)
            w_halo = (ve, ve.count)

            for ob in range(4):
                wpar = (g * 4 + ob) % 2
                P.dma("gpsimd", wzs[:, wpar, :, :],
                      w_zxg.rearrange("(k p) c -> p k c", p=128)[:, :, ob * 1024:(ob + 1) * 1024],
                      gp, wait=[(te, te.count)])
                w_wz = (gp, gp.count)
                for oci in range(8):
                    oc = ob * 8 + oci
                    ps, pi, wmm = mm_acc(
                        lambda k, oci=oci, wpar=wpar: wzs[:, wpar, k, oci * 128:(oci + 1) * 128],
                        lambda k, g=g: xr_fm[:, k, :], 8,
                        wait_extra=[w_rms, w_halo, w_wz])
                    if oc < 16:
                        zpar = oc % 2
                        P.on("scalar", lambda e, ps=ps, oc=oc, zpar=zpar: e.activation(
                            out=z_sb[:, zpar, :], in_=ps[:], func=AF.Identity,
                            bias=bz_sb[:, oc:oc + 1]), wait=[wmm, (gp, gp.count)], inc=sct)
                        wz2 = (sct, sct.count)
                        P.dma("gpsimd", z_fm[oc, :, g * 512:(g + 1) * 512], z_sb[:, zpar, :], gp, wait=[wz2])
                    else:
                        P.on("scalar", lambda e, ps=ps, oc=oc: e.activation(
                            out=xg_buf[:, oc - 16, 3:515], in_=ps[:], func=AF.Identity,
                            bias=bz_sb[:, oc:oc + 1]), wait=[wmm], inc=sct)
                    ps_guard[pi] = (sct, sct.count)

            w_xg = (sct, sct.count)
            for fc in range(16):
                ps, pi, wmm = mm_acc(
                    lambda j, fc=fc: diag_sb[:, fc * 4 + j, :],
                    lambda j, fc=fc: xg_buf[:, fc, j:j + 512], 4,
                    wait_extra=[w_xg, w_diag])
                P.on("scalar", lambda e, ps=ps, fc=fc: e.activation(
                    out=xc_sb[:, fc, :], in_=ps[:], func=AF.Silu,
                    bias=cb_sb[:, fc:fc + 1]), wait=[wmm, (gp, gp.count)], inc=sct)
                wxc = (sct, sct.count)
                P.dma("gpsimd", xc_fm[fc, :, g * 512:(g + 1) * 512], xc_sb[:, fc, :], gp, wait=[wxc])
                ps_guard[pi] = wxc

            w_conv = (sct, sct.count)

            kvcols = [(0, 512), (512, 1024), (1024, 1536), (1536, 2048), (2048, 2080)]
            for st4 in range(4):
                for (c0, c1) in kvcols:
                    ps, pi, wmm = mm_acc(
                        lambda k, st4=st4: xr_fm[:, k, st4 * 128:(st4 + 1) * 128],
                        lambda k, c0=c0, c1=c1: wkv_sb[:, k, c0:c1], 8,
                        wait_extra=[])
                    P.on("vector", lambda e, ps=ps, c0=c0, c1=c1: e.tensor_tensor(
                        out=kv_buf[:, c0:c1], in0=ps[:, 0:c1 - c0], in1=bkv_sb[:, c0:c1],
                        op=ALU.add), wait=[wmm], inc=ve)
                    ps_guard[pi] = (ve, ve.count)
                P.on("vector", lambda e: e.tensor_tensor(
                    out=sq_sb[:], in0=kv_buf[:, 0:2048], in1=kv_buf[:, 0:2048], op=ALU.mult), inc=ve)
                P.on("vector", lambda e: e.tensor_reduce(
                    out=nrm_sb[:], in_=sq_sb[:].rearrange("p (h d) -> p h d", d=128),
                    axis=AX.X, op=ALU.add), inc=ve)
                P.on("scalar", lambda e: e.activation(out=nrm_sb[:], in_=nrm_sb[:], func=AF.Ln),
                     wait=[ve], inc=sct)
                P.on("scalar", lambda e: e.activation(out=nrm_sb[:], in_=nrm_sb[:], func=AF.Exp, scale=0.5), inc=sct)
                P.on("vector", lambda e: e.tensor_scalar(
                    out=nrm_sb[:], in0=nrm_sb[:], scalar1=1e-6, scalar2=None, op0=ALU.add),
                    wait=[sct], inc=ve)
                P.on("vector", lambda e: e.reciprocal(out=nrm_sb[:], in_=nrm_sb[:]), inc=ve)
                for h in range(16):
                    P.on("vector", lambda e, h=h: e.tensor_scalar(
                        out=kq_bf[:, h * 128:(h + 1) * 128], in0=kv_buf[:, h * 128:(h + 1) * 128],
                        scalar1=nrm_sb[:, h:h + 1], scalar2=None, op0=ALU.mult),
                        wait=([(gp, gp.count)] if h == 0 else ()), inc=ve)
                P.on("vector", lambda e, st4=st4: e.tensor_copy(
                    out=vsave[:, st4, :], in_=kv_buf[:, 2048:2080]), inc=ve)
                wkn = (ve, ve.count)
                tok0 = g * 512 + st4 * 128
                P.dma("gpsimd", ktm[tok0:tok0 + 128, :], kq_bf[:], gp, wait=[wkn])

            for cbq in range(4):
                par = (g * 4 + cbq) % 2
                wqld = [(te, te.count)]
                P.dma("gpsimd", qwb[:, par, :, :],
                      qw.rearrange("(k p) c -> p k c", p=128)[:, :, cbq * 512:(cbq + 1) * 512],
                      gp, wait=wqld)
                wq = (gp, gp.count)
                for st4 in range(4):
                    ps, pi, wmm = mm_acc(
                        lambda k, st4=st4: xc_sb[:, k, st4 * 128:(st4 + 1) * 128],
                        lambda k, par=par: qwb[:, par, k, :], 16,
                        wait_extra=[wq, w_conv])
                    P.on("vector", lambda e, ps=ps: e.tensor_copy(out=qtmp[:], in_=ps[:]),
                         wait=[wmm, (gp, gp.count)], inc=ve)
                    ps_guard[pi] = (ve, ve.count)
                    P.on("vector", lambda e: e.tensor_tensor(
                        out=sq_sb[:, 0:512], in0=qtmp[:], in1=qtmp[:], op=ALU.mult), inc=ve)
                    P.on("vector", lambda e: e.tensor_reduce(
                        out=nrm_sb[:, 0:4], in_=sq_sb[:, 0:512].rearrange("p (h d) -> p h d", d=128),
                        axis=AX.X, op=ALU.add), inc=ve)
                    P.on("scalar", lambda e: e.activation(out=nrm_sb[:, 0:4], in_=nrm_sb[:, 0:4], func=AF.Ln),
                         wait=[ve], inc=sct)
                    P.on("scalar", lambda e: e.activation(out=nrm_sb[:, 0:4], in_=nrm_sb[:, 0:4], func=AF.Exp, scale=0.5), inc=sct)
                    P.on("vector", lambda e: e.tensor_scalar(
                        out=nrm_sb[:, 0:4], in0=nrm_sb[:, 0:4], scalar1=1e-6, scalar2=None, op0=ALU.add),
                        wait=[sct], inc=ve)
                    P.on("vector", lambda e: e.reciprocal(out=nrm_sb[:, 0:4], in_=nrm_sb[:, 0:4]), inc=ve)
                    for h in range(4):
                        P.on("vector", lambda e, h=h: e.tensor_scalar(
                            out=kq_bf[:, h * 128:(h + 1) * 128], in0=qtmp[:, h * 128:(h + 1) * 128],
                            scalar1=nrm_sb[:, h:h + 1], scalar2=None, op0=ALU.mult),
                            wait=([(gp, gp.count)] if h == 0 else ()), inc=ve)
                    wqn = (ve, ve.count)
                    tok0 = g * 512 + st4 * 128
                    P.dma("gpsimd", qtm[tok0:tok0 + 128, cbq * 512:(cbq + 1) * 512],
                          kq_bf[:, 0:512], gp, wait=[wqn])

            for st4 in range(4):
                ps_g = psg
                for k in range(16):
                    P.on("tensor", lambda e, k=k, st4=st4: e.matmul(
                        psg[:], xc_sb[:, k, st4 * 128:(st4 + 1) * 128], gw_sb[:, k, :],
                        start=(k == 0), stop=(k == 15)),
                        wait=([(ve, ve.count), (sct, sct.count)] if k == 0 else ()),
                        inc=(te if k == 15 else None))
                wgm = (te, te.count)
                P.on("vector", lambda e: e.tensor_tensor(out=gt[:], in0=psg[:], in1=gb_sb[:], op=ALU.add),
                     wait=[wgm, (gp, gp.count)], inc=ve)
                G = lambda i: gsc[:, i, :]
                q_ = lambda i: gt[:, i * 16:(i + 1) * 16]

                def V(fn, *, w=(), eng="vector"):
                    P.on(eng, fn, wait=w, inc=(ve if eng == "vector" else sct))

                for (src, dst) in ((0, 0), (4, 1)):
                    V(lambda e, s=src, d=dst: e.scalar_tensor_tensor(out=G(30 + d), in0=q_(s), scalar=-1.0, in1=q_(s), op0=ALU.mult, op1=ALU.max))
                    V(lambda e, d=dst: e.activation(out=G(30 + d), in_=G(30 + d), func=AF.Exp, scale=-1.0), eng="scalar", w=[ve])
                    V(lambda e, d=dst: e.activation(out=G(30 + d), in_=G(30 + d), func=AF.Ln, bias=1.0), eng="scalar")
                    V(lambda e, s=src, d=dst: e.tensor_scalar(out=G(32 + d), in0=q_(s), scalar1=0.0, scalar2=None, op0=ALU.max), w=[sct])
                    V(lambda e, d=dst: e.tensor_tensor(out=G(d), in0=G(32 + d), in1=G(30 + d), op=ALU.add))
                V(lambda e: e.scalar_tensor_tensor(out=G(34), in0=q_(5), scalar=-1.0, in1=q_(5), op0=ALU.mult, op1=ALU.max))
                V(lambda e: e.activation(out=G(34), in_=G(34), func=AF.Exp, scale=-1.0), eng="scalar", w=[ve])
                V(lambda e: e.activation(out=G(34), in_=G(34), func=AF.Ln, bias=1.0), eng="scalar")
                V(lambda e: e.tensor_scalar(out=G(3), in0=q_(5), scalar1=0.0, scalar2=None, op0=ALU.min), w=[sct])
                V(lambda e: e.tensor_tensor(out=G(2), in0=G(3), in1=G(34), op=ALU.subtract))
                V(lambda e: e.tensor_tensor(out=G(4), in0=G(1), in1=lds_sb[:], op=ALU.mult))
                V(lambda e: e.tensor_scalar(out=G(4), in0=G(4), scalar1=1e-4, scalar2=None, op0=ALU.add))
                V(lambda e: e.activation(out=G(5), in_=q_(1), func=AF.Tanh), eng="scalar", w=[ve])
                V(lambda e: e.tensor_scalar(out=G(5), in0=G(5), scalar1=1.0, scalar2=None, op0=ALU.add), w=[sct])
                V(lambda e: e.tensor_tensor(out=G(5), in0=G(5), in1=fq_sb[:], op=ALU.mult))
                V(lambda e: e.tensor_tensor(out=G(6), in0=G(4), in1=G(0), op=ALU.mult))
                V(lambda e: e.tensor_scalar(out=G(6), in0=G(6), scalar1=0.5, scalar2=None, op0=ALU.mult))
                V(lambda e: e.tensor_tensor(out=G(7), in0=G(4), in1=G(5), op=ALU.mult))
                V(lambda e: e.tensor_scalar(out=G(7), in0=G(7), scalar1=0.5, scalar2=None, op0=ALU.mult))
                V(lambda e: e.tensor_scalar(out=G(8), in0=G(6), scalar1=1.0, scalar2=None, op0=ALU.add))
                V(lambda e: e.tensor_tensor(out=G(8), in0=G(8), in1=G(8), op=ALU.mult))
                V(lambda e: e.tensor_tensor(out=G(10), in0=G(7), in1=G(7), op=ALU.mult))
                V(lambda e: e.tensor_tensor(out=G(8), in0=G(8), in1=G(10), op=ALU.add))
                V(lambda e: e.reciprocal(out=G(9), in_=G(8)))
                V(lambda e: e.tensor_tensor(out=G(11), in0=G(6), in1=G(6), op=ALU.mult))
                V(lambda e: e.tensor_tensor(out=G(11), in0=G(11), in1=G(10), op=ALU.add))
                V(lambda e: e.tensor_scalar(out=G(11), in0=G(11), scalar1=-1.0, scalar2=1.0, op0=ALU.mult, op1=ALU.add))
                V(lambda e: e.tensor_tensor(out=G(11), in0=G(11), in1=G(9), op=ALU.mult))
                V(lambda e: e.tensor_scalar(out=G(12), in0=G(7), scalar1=2.0, scalar2=None, op0=ALU.mult))
                V(lambda e: e.tensor_tensor(out=G(12), in0=G(12), in1=G(9), op=ALU.mult))
                V(lambda e: e.tensor_tensor(out=G(13), in0=G(11), in1=G(11), op=ALU.mult))
                V(lambda e: e.tensor_tensor(out=G(10), in0=G(12), in1=G(12), op=ALU.mult))
                V(lambda e: e.tensor_tensor(out=G(13), in0=G(13), in1=G(10), op=ALU.add))
                V(lambda e: e.activation(out=G(13), in_=G(13), func=AF.Ln), eng="scalar", w=[ve])
                V(lambda e: e.activation(out=G(13), in_=G(13), func=AF.Exp, scale=0.5), eng="scalar")
                V(lambda e: e.tensor_scalar(out=G(14), in0=G(13), scalar1=1e-8, scalar2=1.0, op0=ALU.max, op1=ALU.min), w=[sct])
                V(lambda e: e.activation(out=G(14), in_=G(14), func=AF.Ln), eng="scalar", w=[ve])
                V(lambda e: e.activation(out=G(15), in_=q_(6), func=AF.Sigmoid), eng="scalar")
                V(lambda e: e.tensor_tensor(out=G(16), in0=G(15), in1=G(14), op=ALU.mult), w=[sct])
                V(lambda e: e.tensor_scalar(out=G(16), in0=G(16), scalar1=8.0, scalar2=None, op0=ALU.mult))
                V(lambda e: e.activation(out=G(17), in_=G(16), func=AF.Exp), eng="scalar", w=[ve])
                V(lambda e: e.tensor_scalar(out=G(18), in0=G(11), scalar1=0.0, scalar2=None, op0=ALU.is_ge))
                V(lambda e: e.tensor_scalar(out=G(19), in0=G(18), scalar1=2.0, scalar2=-1.0, op0=ALU.mult, op1=ALU.add))
                V(lambda e: e.scalar_tensor_tensor(out=G(20), in0=G(11), scalar=-1.0, in1=G(11), op0=ALU.mult, op1=ALU.max))
                V(lambda e: e.tensor_scalar(out=G(20), in0=G(20), scalar1=1e-30, scalar2=None, op0=ALU.max))
                V(lambda e: e.reciprocal(out=G(20), in_=G(20)))
                V(lambda e: e.tensor_tensor(out=G(21), in0=G(12), in1=G(20), op=ALU.mult))
                V(lambda e: e.tensor_tensor(out=G(21), in0=G(21), in1=G(19), op=ALU.mult))
                V(lambda e: e.activation(out=G(21), in_=G(21), func=AF.Arctan), eng="scalar", w=[ve])
                V(lambda e: e.tensor_scalar(out=G(22), in0=G(18), scalar1=-PI, scalar2=PI, op0=ALU.mult, op1=ALU.add), w=[sct])
                V(lambda e: e.tensor_tensor(out=G(21), in0=G(21), in1=G(22), op=ALU.add))
                V(lambda e: e.activation(out=stage[:, 5, :], in_=q_(5), func=AF.Sigmoid), eng="scalar")
                V(lambda e: e.activation(out=G(23), in_=q_(2), func=AF.Sigmoid), eng="scalar")
                V(lambda e: e.activation(out=stage[:, 6, :], in_=q_(3), func=AF.Sigmoid), eng="scalar")
                V(lambda e: e.tensor_tensor(out=G(24), in0=G(17), in1=G(17), op=ALU.mult), w=[sct])
                V(lambda e: e.tensor_scalar(out=G(24), in0=G(24), scalar1=-1.0, scalar2=1.0, op0=ALU.mult, op1=ALU.add))
                V(lambda e: e.tensor_scalar(out=G(24), in0=G(24), scalar1=0.0, scalar2=1.0, op0=ALU.max, op1=ALU.min))
                V(lambda e: e.tensor_scalar(out=G(24), in0=G(24), scalar1=1e-6, scalar2=None, op0=ALU.add))
                V(lambda e: e.activation(out=G(24), in_=G(24), func=AF.Ln), eng="scalar", w=[ve])
                V(lambda e: e.activation(out=G(24), in_=G(24), func=AF.Exp, scale=0.5), eng="scalar")
                V(lambda e: e.tensor_tensor(out=G(24), in0=G(24), in1=G(23), op=ALU.mult), w=[sct])
                V(lambda e: e.tensor_copy(out=G(26), in_=G(16)))
                V(lambda e: e.tensor_copy(out=G(27), in_=G(21)))
                wgm2 = (ve, ve.count)
                P.on("tensor", lambda e: e.matmul(psc[:, 0:32], triu_sb[:], gsc[:, 26:28, :].rearrange("p a b -> p (a b)"), start=True, stop=True),
                     wait=[wgm2, (sct, sct.count)], inc=te)
                P.on("tensor", lambda e: e.matmul(psc[:, 32:64], tril_sb[:], gsc[:, 26:28, :].rearrange("p a b -> p (a b)"), start=True, stop=True), inc=te)
                wcs = (te, te.count)
                V(lambda e: e.tensor_copy(out=stage[:, 0, :], in_=psc[:, 0:16]), w=[wcs])
                V(lambda e: e.tensor_tensor(out=stage[:, 1, :], in0=stage[:, 0, :], in1=G(2), op=ALU.add))
                V(lambda e: e.activation(out=stage[:, 8, :], in_=stage[:, 0, :], func=AF.Exp), eng="scalar", w=[ve, wcs])
                V(lambda e: e.tensor_copy(out=stage[:, 2, :], in_=psc[:, 16:32]), w=[wcs])
                V(lambda e: e.activation(out=stage[:, 7, :], in_=psc[:, 32:48], func=AF.Exp), eng="scalar")
                V(lambda e: e.tensor_scalar(out=G(28), in0=psc[:, 16:32], scalar1=1.0 / (2 * PI), scalar2=None, op0=ALU.mult), w=[wcs])
                V(lambda e: e.tensor_scalar(out=G(29), in0=G(28), scalar1=MAGIC, scalar2=None, op0=ALU.add))
                V(lambda e: e.tensor_scalar(out=G(29), in0=G(29), scalar1=-MAGIC, scalar2=None, op0=ALU.add))
                V(lambda e: e.tensor_tensor(out=G(29), in0=G(28), in1=G(29), op=ALU.subtract))
                V(lambda e: e.tensor_scalar(out=G(29), in0=G(29), scalar1=2 * PI, scalar2=None, op0=ALU.mult))
                V(lambda e: e.activation(out=stage[:, 4, :], in_=G(29), func=AF.Sin), eng="scalar", w=[ve])
                V(lambda e: e.tensor_scalar(out=G(28), in0=G(28), scalar1=0.25, scalar2=None, op0=ALU.add))
                V(lambda e: e.tensor_scalar(out=G(25), in0=G(28), scalar1=MAGIC, scalar2=None, op0=ALU.add))
                V(lambda e: e.tensor_scalar(out=G(25), in0=G(25), scalar1=-MAGIC, scalar2=None, op0=ALU.add))
                V(lambda e: e.tensor_tensor(out=G(25), in0=G(28), in1=G(25), op=ALU.subtract))
                V(lambda e: e.tensor_scalar(out=G(25), in0=G(25), scalar1=2 * PI, scalar2=None, op0=ALU.mult))
                V(lambda e: e.activation(out=stage[:, 3, :], in_=G(25), func=AF.Sin), eng="scalar", w=[ve])
                V(lambda e, st4=st4: e.tensor_tensor(
                    out=G(30), in0=vsave[:, st4, :].rearrange("p (h s) -> p h s", s=2)[:, :, 0],
                    in1=G(24), op=ALU.mult), w=[sct])
                V(lambda e, st4=st4: e.tensor_tensor(
                    out=G(31), in0=vsave[:, st4, :].rearrange("p (h s) -> p h s", s=2)[:, :, 1],
                    in1=G(24), op=ALU.mult))
                V(lambda e: e.tensor_tensor(out=G(32), in0=G(30), in1=stage[:, 3, :], op=ALU.mult))
                V(lambda e: e.tensor_tensor(out=G(33), in0=G(31), in1=stage[:, 4, :], op=ALU.mult))
                V(lambda e: e.tensor_tensor(out=G(32), in0=G(32), in1=G(33), op=ALU.subtract))
                V(lambda e: e.tensor_tensor(out=G(33), in0=G(30), in1=stage[:, 4, :], op=ALU.mult))
                V(lambda e: e.tensor_tensor(out=G(34), in0=G(31), in1=stage[:, 3, :], op=ALU.mult))
                V(lambda e: e.tensor_tensor(out=G(33), in0=G(33), in1=G(34), op=ALU.add))
                V(lambda e: e.tensor_copy(out=vt_sb[:].rearrange("p (h s) -> p h s", s=2)[:, :, 0], in_=G(32)))
                V(lambda e: e.tensor_copy(out=vt_sb[:].rearrange("p (h s) -> p h s", s=2)[:, :, 1], in_=G(33)))
                wvt = (ve, ve.count)
                tok0 = g * 512 + st4 * 128
                P.dma("gpsimd", vt_o[tok0:tok0 + 128, :], vt_sb[:], gp, wait=[wvt])
                P.dma("gpsimd", gcols[tok0:tok0 + 128, :, :], stage[:], gp, wait=[wvt, (sct, sct.count)])
                P.on("tensor", lambda e: e.transpose(pst[:], stage[:, 0:3, :].rearrange("p a b -> p (a b)"), eye_sb[:]),
                     wait=[(ve, ve.count), (sct, sct.count)], inc=te)
                P.on("scalar", lambda e: e.activation(out=ghm_sb[:], in_=pst[:], func=AF.Identity),
                     wait=[te, (gp, gp.count)], inc=sct)
                wgh = (sct, sct.count)
                bq = tok0 // L
                lt = tok0 % L
                P.dma("gpsimd", ghm[bq, :, :, lt:lt + 128], ghm_sb[:], gp, wait=[wgh])

        P.on("gpsimd", lambda e: e.engine_nop(), wait=[(gp, gp.count)], inc=None)
        P.play(block)
    return nc


def run_l0_front(per_core_x, wts, consts):
    nc = build_l0_front()
    inp = {}
    w = wts
    inp["w_zxg"] = _f32(w["L0_attn_in_proj_w"][:, 0:4096])
    inp["w_kv"] = _f32(w["L0_attn_in_proj_w"][:, 4096:6176])
    inp["b_zxg"] = _f32(w["L0_attn_in_proj_b"][0:4096].reshape(32, 128).T)
    inp["b_kv"] = _f32(np.broadcast_to(w["L0_attn_in_proj_b"][4096:6176], (128, 2080)))
    inp["rmsw"] = _f32(w["L0_attn_rms_w"].reshape(8, 128).T)
    cw = np.asarray(w["L0_attn_conv_w"], np.float32)
    inp["convw"] = _f32(cw.reshape(16, 128, 4).transpose(1, 2, 0).reshape(128, 64))
    inp["convb"] = _f32(w["L0_attn_conv_b"].reshape(16, 128).T)
    inp["gatew"] = _f32(w["L0_attn_gate_w"])
    inp["gateb"] = _f32(np.broadcast_to(w["L0_attn_gate_b"], (128, 112)))
    inp["ldsb"] = _f32(np.broadcast_to(w["L0_attn_log_dt_scale"], (128, 16)))
    inp["freqs"] = consts["freqs"]
    inp["qw"] = _f32(w["L0_attn_Q_w"])
    inp["eye32"] = consts["eye_f32"]
    inp["triu_i"] = consts["triu_incl"]
    inp["tril_s"] = consts["tril_strict"]
    in_maps = [dict(inp, x=_f32(xc)) for xc in per_core_x]
    res = run_bass_kernel_spmd(nc, in_maps, core_ids=list(range(NCORE)))
    return res.results



def build_l0_core():
    nc = bass.Bass("TRN2", target_bir_lowering=False, debug=False)
    ktm = nc.dram_tensor("ktm", [T, 2048], BF16, kind="ExternalInput").ap()
    qtm = nc.dram_tensor("qtm", [T, 2048], BF16, kind="ExternalInput").ap()
    vt_i = nc.dram_tensor("vt_i", [T, 32], F32, kind="ExternalInput").ap()
    gcols = nc.dram_tensor("gcols", [T, 9, 16], F32, kind="ExternalInput").ap()
    ghm = nc.dram_tensor("ghm", [BQ, 3, 16, L], F32, kind="ExternalInput").ap()
    ones_i = nc.dram_tensor("ones_i", [1, 128], F32, kind="ExternalInput").ap()
    msku_n_i = nc.dram_tensor("msku_n_i", [128, 128], BF16, kind="ExternalInput").ap()
    mskl_n_i = nc.dram_tensor("mskl_n_i", [128, 128], BF16, kind="ExternalInput").ap()
    msku_i_i = nc.dram_tensor("msku_i_i", [128, 128], BF16, kind="ExternalInput").ap()
    retr_o = nc.dram_tensor("retr_o", [T, 32], F32, kind="ExternalOutput").ap()

    with contextlib.ExitStack() as _es:
        S = lambda n, sh, dt: _es.enter_context(nc.sbuf_tensor(n, sh, dt))
        PSm = lambda n, sh: _es.enter_context(nc.psum_tensor(n, sh, F32))
        ones_sb = S("ones_sb", [1, 128], F32)
        msku_n = S("msku_n", [128, 128], BF16)
        mskl_n = S("mskl_n", [128, 128], BF16)
        msku_i = S("msku_i", [128, 128], BF16)
        kfm = S("kfm", [128, 16, 128], BF16)
        qfm = S("qfm", [128, 16, 128], BF16)
        ktm_sb = S("ktm_sb", [128, 2048], BF16)
        vt_sb = S("vt_sb", [128, 32], F32)
        gc_sb = S("gc_sb", [128, 9, 16], F32)
        rows_sb = S("rows_sb", [1, 6144], F32)
        scl_row = S("scl_row", [1, 64], F32)
        trg_row = S("trg_row", [1, 48], F32)
        scl_cols = S("scl_cols", [128, 64], F32)
        e_sb = S("e_sb", [128, 3, 128], BF16)
        epre = S("epre", [128, 128], F32)
        t_sb = S("t_sb", [128, 128], BF16)
        ptb = S("ptb", [128, 16, 7, 128], BF16)
        pb = S("pb", [128, 128], BF16)
        m2b = S("m2b", [128, 16, 128], BF16)
        wfp = S("wfp", [128, 2, 32], F32)
        wt1 = S("wt1", [128, 32], F32)
        wt2 = S("wt2", [128, 32], F32)
        scl2 = S("scl2", [128, 2, 32], F32)
        wbb = S("wbb", [128, 32], BF16)
        udb = S("udb", [128, 32], BF16)
        retr_sb = S("retr_sb", [128, 32], F32)
        zf = S("zf", [128, 2, 32], F32)
        zb = S("zb", [128, 2, 32], BF16)
        tmpA = S("tmpA", [128, 2], F32)
        tmpB = S("tmpB", [128, 2], F32)
        tmpC = S("tmpC", [128, 2], F32)
        negs = S("negs", [128, 16], F32)
        psE = PSm("psE", [128, 256])
        psA = PSm("psA", [128, 128])
        psB = PSm("psB", [128, 128])
        psP2 = PSm("psP2", [128, 256])
        pss = PSm("pss", [128, 256])
        s_ld = _es.enter_context(nc.semaphore("s_ld"))
        s_gp = _es.enter_context(nc.semaphore("s_gp"))
        s_te = _es.enter_context(nc.semaphore("s_te"))
        s_ve = _es.enter_context(nc.semaphore("s_ve"))
        s_sc = _es.enter_context(nc.semaphore("s_sc"))
        block = _es.enter_context(nc.Block())

        ld = Trk(s_ld); gp = Trk(s_gp); te = Trk(s_te); ve = Trk(s_ve); sct = Trk(s_sc)
        P = Prog(nc, self_trks={"vector": ve, "scalar": sct})

        for ap_, t_ in ((ones_i, ones_sb), (msku_n_i, msku_n), (mskl_n_i, mskl_n), (msku_i_i, msku_i)):
            P.dma("sync", t_[:], ap_[:], ld)
        w_consts = (ld, ld.count)

        P.on("vector", lambda e: e.memset(zf[:], 0.0))
        P.on("vector", lambda e: e.memset(zb[:], 0.0))

        for c in range(NCHS):
            for b in range(BQ):
                tok0 = b * L + c * CH
                lt = c * CH
                if c == 0 and b > 0:
                    pass
                wpre = [(te, te.count), (ve, ve.count), (sct, sct.count), (gp, gp.count)]
                for h in range(16):
                    P.dma("sync", kfm[:, h, :], ktm[tok0:tok0 + 128, h * 128:(h + 1) * 128], ld,
                          wait=(wpre if h == 0 else ()), transpose=True)
                for h in range(16):
                    P.dma("sync", qfm[:, h, :], qtm[tok0:tok0 + 128, h * 128:(h + 1) * 128], ld,
                          transpose=True)
                P.dma("sync", ktm_sb[:], ktm[tok0:tok0 + 128, :], ld)
                P.dma("sync", vt_sb[:], vt_i[tok0:tok0 + 128, :], ld)
                P.dma("sync", gc_sb[:], gcols[tok0:tok0 + 128, :, :], ld)
                P.dma("sync", rows_sb[0:1, :].rearrange("o (q h t) -> o q h t", q=3, h=16),
                      ghm[b:b + 1, :, :, lt:lt + 128], ld)
                w_ld = (ld, ld.count)

                GC = rows_sb[0:1, 127:2048:128]
                THC = rows_sb[0:1, 4096 + 127:6144:128]
                P.on("scalar", lambda e, GC=GC: e.activation(out=scl_row[0:1, 0:16], in_=GC, func=AF.Exp), wait=[w_ld])
                P.on("vector", lambda e, THC=THC: e.tensor_scalar(out=trg_row[0:1, 0:16], in0=THC, scalar1=1.0 / (2 * PI), scalar2=None, op0=ALU.mult), wait=[w_ld])
                for (dsti, add25) in ((32, False), (16, True)):
                    if add25:
                        P.on("vector", lambda e: e.tensor_scalar(out=trg_row[0:1, 0:16], in0=trg_row[0:1, 0:16], scalar1=0.25, scalar2=None, op0=ALU.add), wait=[sct])
                    P.on("vector", lambda e: e.tensor_scalar(out=trg_row[0:1, 16:32], in0=trg_row[0:1, 0:16], scalar1=MAGIC, scalar2=None, op0=ALU.add), wait=[sct])
                    P.on("vector", lambda e: e.tensor_scalar(out=trg_row[0:1, 16:32], in0=trg_row[0:1, 16:32], scalar1=-MAGIC, scalar2=None, op0=ALU.add))
                    P.on("vector", lambda e: e.tensor_tensor(out=trg_row[0:1, 16:32], in0=trg_row[0:1, 0:16], in1=trg_row[0:1, 16:32], op=ALU.subtract))
                    P.on("vector", lambda e: e.tensor_scalar(out=trg_row[0:1, 16:32], in0=trg_row[0:1, 16:32], scalar1=2 * PI, scalar2=None, op0=ALU.mult))
                    P.on("scalar", lambda e, dsti=dsti: e.activation(out=scl_row[0:1, dsti:dsti + 16], in_=trg_row[0:1, 16:32], func=AF.Sin), wait=[ve])
                P.on("vector", lambda e: e.tensor_scalar(out=scl_row[0:1, 48:64], in0=scl_row[0:1, 32:48], scalar1=-1.0, scalar2=None, op0=ALU.mult), wait=[sct])
                P.on("tensor", lambda e: e.matmul(pss[:, 160:224], ones_sb[:], scl_row[:], start=True, stop=True),
                     wait=[(ve, ve.count), (sct, sct.count), w_consts], inc=te)
                P.on("vector", lambda e: e.tensor_copy(out=scl_cols[:], in_=pss[:, 160:224]), wait=[te])
                P.on("vector", lambda e: e.tensor_scalar(out=negs[:], in0=gc_sb[:, 4, :], scalar1=-1.0, scalar2=None, op0=ALU.mult))
                w_scl = (ve, ve.count)

                if c == 0:
                    pass

                for h in range(16):
                    Gcol = gc_sb[:, 0, h:h + 1]
                    Gbcol = gc_sb[:, 1, h:h + 1]
                    coscol = gc_sb[:, 3, h:h + 1]
                    sincol = gc_sb[:, 4, h:h + 1]
                    betacol = gc_sb[:, 5, h:h + 1]
                    selcol = gc_sb[:, 6, h:h + 1]
                    mdeccol = gc_sb[:, 7, h:h + 1]
                    mcol = gc_sb[:, 8, h:h + 1]
                    growG = rows_sb[0:1, h * 128:(h + 1) * 128]
                    growGb = rows_sb[0:1, 2048 + h * 128:2048 + (h + 1) * 128]
                    P.on("tensor", lambda e, r=growGb: e.matmul(psE[:, 0:128], ones_sb[:], r, start=True, stop=True),
                         wait=[(ve, ve.count), (sct, sct.count)], inc=te)
                    P.on("tensor", lambda e, r=growG: e.matmul(psE[:, 128:256], ones_sb[:], r, start=True, stop=True), inc=te)
                    wE = (te, te.count)
                    P.on("vector", lambda e, Gcol=Gcol: e.tensor_scalar(out=epre[:], in0=psE[:, 0:128], scalar1=Gcol, scalar2=0.0, op0=ALU.subtract, op1=ALU.min), wait=[wE])
                    P.on("scalar", lambda e: e.activation(out=e_sb[:, 0, :], in_=epre[:], func=AF.Exp), wait=[ve])
                    P.on("vector", lambda e, Gbcol=Gbcol: e.tensor_scalar(out=epre[:], in0=psE[:, 128:256], scalar1=Gbcol, scalar2=0.0, op0=ALU.subtract, op1=ALU.max), wait=[sct])
                    P.on("scalar", lambda e: e.activation(out=e_sb[:, 1, :], in_=epre[:], func=AF.Exp, scale=-1.0), wait=[ve])
                    P.on("vector", lambda e, Gcol=Gcol: e.tensor_scalar(out=epre[:], in0=psE[:, 128:256], scalar1=Gcol, scalar2=0.0, op0=ALU.subtract, op1=ALU.min), wait=[sct])
                    P.on("scalar", lambda e: e.activation(out=e_sb[:, 2, :], in_=epre[:], func=AF.Exp), wait=[ve])
                    wEe = (sct, sct.count)
                    P.on("tensor", lambda e, h=h: e.matmul(psA[:], kfm[:, h, :], kfm[:, h, :], start=True, stop=True),
                         wait=[w_ld, (ve, ve.count)], inc=te)
                    P.on("tensor", lambda e, h=h: e.matmul(psB[:], kfm[:, h, :], qfm[:, h, :], start=True, stop=True), inc=te)
                    wG = (te, te.count)
                    P.on("vector", lambda e: e.tensor_tensor(out=t_sb[:], in0=psA[:], in1=e_sb[:, 0, :], op=ALU.mult), wait=[wG, wEe])
                    P.on("vector", lambda e, h=h: e.tensor_tensor(out=ptb[:, h, 0, :], in0=t_sb[:], in1=msku_n[:], op=ALU.mult))
                    P.on("vector", lambda e: e.tensor_tensor(out=t_sb[:], in0=psA[:], in1=e_sb[:, 1, :], op=ALU.mult))
                    P.on("vector", lambda e: e.tensor_tensor(out=pb[:], in0=t_sb[:], in1=mskl_n[:], op=ALU.mult))
                    P.on("vector", lambda e: e.tensor_tensor(out=t_sb[:], in0=psB[:], in1=e_sb[:, 2, :], op=ALU.mult))
                    P.on("vector", lambda e, h=h: e.tensor_tensor(out=m2b[:, h, :], in0=t_sb[:], in1=msku_i[:], op=ALU.mult))
                    wM = (ve, ve.count)
                    for k in range(1, 7):
                        P.on("tensor", lambda e, k=k, h=h: e.matmul(psP2[:, 0:128], ptb[:, h, k - 1, :], pb[:], start=True, stop=True),
                             wait=[wM if k == 1 else (ve, ve.count)], inc=te)
                        P.on("tensor", lambda e, k=k, h=h: e.matmul(psP2[:, 128:256], pb[:], ptb[:, h, k - 1, :], start=True, stop=True), inc=te)
                        P.on("vector", lambda e: e.tensor_copy(out=pb[:], in_=psP2[:, 0:128]), wait=[te])
                        P.on("vector", lambda e, k=k, h=h: e.tensor_copy(out=ptb[:, h, k, :], in_=psP2[:, 128:256]))
                    wSq = (ve, ve.count)
                w_allh = (ve, ve.count)
                for h in range(16):
                    P.on("tensor", lambda e, h=h, b=b: e.matmul(pss[:, 2 * h:2 * h + 2], kfm[:, h, :], zb[:, b, 2 * h:2 * h + 2], start=True, stop=True),
                         wait=([w_allh] if h == 0 else ()), inc=(te if h == 15 else None))
                    P.on("tensor", lambda e, h=h, b=b: e.matmul(pss[:, 32 + 2 * h:32 + 2 * h + 2], qfm[:, h, :], zb[:, b, 2 * h:2 * h + 2], start=True, stop=True), inc=(te if h == 15 else None))
                wZK = (te, te.count)
                mpair = gc_sb[:, 8, :].rearrange("p (h o) -> p h o", o=1)
                bpair = gc_sb[:, 5, :].rearrange("p (h o) -> p h o", o=1)
                mdpair = gc_sb[:, 7, :].rearrange("p (h o) -> p h o", o=1)
                def bcmul(out2, in2, colpair, w=()):
                    a2, b2_ = _bc(in2.rearrange("p (h s) -> p h s", s=2), colpair)
                    P.on("vector", lambda e, a2=a2, b2_=b2_, out2=out2: e.tensor_tensor(
                        out=out2.rearrange("p (h s) -> p h s", s=2), in0=a2, in1=b2_, op=ALU.mult), wait=list(w))
                bcmul(wt1[:], pss[:, 0:32], mpair, w=[wZK])
                P.on("vector", lambda e: e.tensor_tensor(out=wt2[:], in0=vt_sb[:], in1=wt1[:], op=ALU.subtract))
                bcmul(wfp[:, 0, :], wt2[:], bpair)
                P.on("vector", lambda e: e.tensor_copy(out=wbb[:], in_=wfp[:, 0, :]))
                for k in range(7):
                    for h in range(16):
                        P.on("tensor", lambda e, k=k, h=h: e.matmul(pss[:, 64 + 2 * h:64 + 2 * h + 2], ptb[:, h, k, :], wbb[:, 2 * h:2 * h + 2], start=True, stop=True),
                             wait=([(ve, ve.count)] if h == 0 else ()), inc=(te if h == 15 else None))
                    P.on("vector", lambda e, k=k: e.tensor_tensor(out=wfp[:, (k + 1) % 2, :], in0=pss[:, 64:96], in1=wfp[:, k % 2, :], op=ALU.add), wait=[te])
                    P.on("vector", lambda e: e.tensor_copy(out=wbb[:], in_=wfp[:, (k + 1) % 2, :]))
                wW = (ve, ve.count)
                ufin = wfp[:, 1, :]
                for h in range(16):
                    P.on("tensor", lambda e, h=h: e.matmul(pss[:, 96 + 2 * h:96 + 2 * h + 2], m2b[:, h, :], wbb[:, 2 * h:2 * h + 2], start=True, stop=True),
                         wait=([wW] if h == 0 else ()), inc=(te if h == 15 else None))
                wR = (te, te.count)
                bcmul(wt1[:], pss[:, 32:64], mpair, w=[wR])
                P.on("vector", lambda e: e.tensor_tensor(out=wt2[:], in0=pss[:, 96:128], in1=wt1[:], op=ALU.add))
                rr = wt2[:].rearrange("p (h s) -> p h s", s=2)[:, :, 0]
                ri = wt2[:].rearrange("p (h s) -> p h s", s=2)[:, :, 1]
                cosT16 = gc_sb[:, 3, :]
                sinT16 = gc_sb[:, 4, :]
                sel16 = gc_sb[:, 6, :]
                re_e = retr_sb[:].rearrange("p (h s) -> p h s", s=2)[:, :, 0]
                re_o = retr_sb[:].rearrange("p (h s) -> p h s", s=2)[:, :, 1]
                P.on("vector", lambda e, rr=rr: e.tensor_tensor(out=wt1[:, 0:16], in0=rr, in1=cosT16, op=ALU.mult))
                P.on("vector", lambda e, ri=ri: e.tensor_tensor(out=wt1[:, 16:32], in0=ri, in1=sinT16, op=ALU.mult))
                P.on("vector", lambda e: e.tensor_tensor(out=wt1[:, 0:16], in0=wt1[:, 0:16], in1=wt1[:, 16:32], op=ALU.add))
                P.on("vector", lambda e, re_e=re_e: e.tensor_tensor(out=re_e, in0=wt1[:, 0:16], in1=sel16, op=ALU.mult))
                P.on("vector", lambda e, ri=ri: e.tensor_tensor(out=wt1[:, 0:16], in0=ri, in1=cosT16, op=ALU.mult))
                P.on("vector", lambda e, rr=rr: e.tensor_tensor(out=wt1[:, 16:32], in0=rr, in1=sinT16, op=ALU.mult))
                P.on("vector", lambda e: e.tensor_tensor(out=wt1[:, 0:16], in0=wt1[:, 0:16], in1=wt1[:, 16:32], op=ALU.subtract))
                P.on("vector", lambda e, re_o=re_o: e.tensor_tensor(out=re_o, in0=wt1[:, 0:16], in1=sel16, op=ALU.mult))
                bcmul(wt1[:], ufin, mdpair)
                P.on("vector", lambda e: e.tensor_copy(out=udb[:], in_=wt1[:]))
                for h in range(16):
                    P.on("tensor", lambda e, h=h: e.matmul(pss[:, 128 + 2 * h:128 + 2 * h + 2], ktm_sb[:, h * 128:(h + 1) * 128], udb[:, 2 * h:2 * h + 2], start=True, stop=True),
                         wait=([(ve, ve.count)] if h == 0 else ()), inc=(te if h == 15 else None))
                wDZ = (te, te.count)
                for (si, c0) in ((0, 0), (1, 16), (2, 32)):
                    ap2 = scl_cols[:, c0:c0 + 16].rearrange("p (h o) -> p h o", o=1)
                    dst = [wt1, scl2[:, 0, :], scl2[:, 1, :]][si]
                    a2, b2_ = _bc(wt2[:].rearrange("p (h s) -> p h s", s=2), ap2)
                    P.on("vector", lambda e, b2_=b2_, dst=dst: e.scalar_tensor_tensor(
                        out=dst.rearrange("p (h s) -> p h s", s=2) if dst.shape[-1] == 32 else dst,
                        in0=b2_, scalar=1.0, in1=b2_, op0=ALU.bypass, op1=ALU.max),
                        wait=([wDZ, w_scl] if si == 0 else ()))
                P.on("vector", lambda e, b=b: e.tensor_tensor(out=wt2[:], in0=zf[:, b, :], in1=wt1[:], op=ALU.mult))
                P.on("vector", lambda e: e.tensor_tensor(out=wt2[:], in0=wt2[:], in1=pss[:, 128:160], op=ALU.add))
                tr_e = wt2[:].rearrange("p (h s) -> p h s", s=2)[:, :, 0]
                tr_o = wt2[:].rearrange("p (h s) -> p h s", s=2)[:, :, 1]
                zf_e = zf[:, b, :].rearrange("p (h s) -> p h s", s=2)[:, :, 0]
                zf_o = zf[:, b, :].rearrange("p (h s) -> p h s", s=2)[:, :, 1]
                cC16 = scl2[:, 0, :].rearrange("p (h s) -> p h s", s=2)[:, :, 0]
                sC16 = scl2[:, 1, :].rearrange("p (h s) -> p h s", s=2)[:, :, 0]
                P.on("vector", lambda e, tr_e=tr_e, cC16=cC16: e.tensor_tensor(out=wt1[:, 0:16], in0=tr_e, in1=cC16, op=ALU.mult))
                P.on("vector", lambda e, tr_o=tr_o, sC16=sC16: e.tensor_tensor(out=wt1[:, 16:32], in0=tr_o, in1=sC16, op=ALU.mult))
                P.on("vector", lambda e, zf_e=zf_e: e.tensor_tensor(out=zf_e, in0=wt1[:, 0:16], in1=wt1[:, 16:32], op=ALU.add))
                P.on("vector", lambda e, tr_o=tr_o, cC16=cC16: e.tensor_tensor(out=wt1[:, 0:16], in0=tr_o, in1=cC16, op=ALU.mult))
                P.on("vector", lambda e, tr_e=tr_e, sC16=sC16: e.tensor_tensor(out=wt1[:, 16:32], in0=tr_e, in1=sC16, op=ALU.mult))
                P.on("vector", lambda e, zf_o=zf_o: e.tensor_tensor(out=zf_o, in0=wt1[:, 0:16], in1=wt1[:, 16:32], op=ALU.subtract))
                P.on("vector", lambda e, b=b: e.tensor_copy(out=zb[:, b, :], in_=zf[:, b, :]))
                wRT = (ve, ve.count)
                P.dma("gpsimd", retr_o[tok0:tok0 + 128, :], retr_sb[:], gp, wait=[wRT])

        P.on("gpsimd", lambda e: e.engine_nop(), wait=[(gp, gp.count)], inc=None)
        P.play(block)
    return nc


def run_l0_core(front_results, consts):
    nc = build_l0_core()
    base = {
        "ones_i": _f32(np.ones((1, 128))),
        "msku_n_i": consts["mask_u_strict_neg"],
        "mskl_n_i": consts["mask_l_strict_neg"],
        "msku_i_i": consts["mask_u_incl"],
    }
    in_maps = []
    for r in front_results:
        m = dict(base)
        m["ktm"] = r["ktm"]; m["qtm"] = r["qtm"]; m["vt_i"] = r["vt_o"]
        m["gcols"] = r["gcols"]; m["ghm"] = r["ghm"]
        in_maps.append(m)
    res = run_bass_kernel_spmd(nc, in_maps, core_ids=list(range(NCORE)))
    return res.results


def _bc(ap3, col3):
    return bass.broadcast_tensor_aps(ap3, col3)



def build_l0_comb():
    nc = bass.Bass("TRN2", target_bir_lowering=False, debug=False)
    retr = nc.dram_tensor("retr", [T, 32], F32, kind="ExternalInput").ap()
    z_fm = nc.dram_tensor("z_fm", [16, 128, T], BF16, kind="ExternalInput").ap()
    xc_fm = nc.dram_tensor("xc_fm", [16, 128, T], BF16, kind="ExternalInput").ap()
    xn_tm = nc.dram_tensor("xn_tm", [T, DM], BF16, kind="ExternalInput").ap()
    wr = nc.dram_tensor("wr", [32, 2048], F32, kind="ExternalInput").ap()
    wo = nc.dram_tensor("wo", [2048, 1024], F32, kind="ExternalInput").ap()
    gnw = nc.dram_tensor("gnw", [128, 16], F32, kind="ExternalInput").ap()
    gnb = nc.dram_tensor("gnb", [128, 16], F32, kind="ExternalInput").ap()
    dpar = nc.dram_tensor("dpar", [128, 16], F32, kind="ExternalInput").ap()
    eye32 = nc.dram_tensor("eye32", [128, 128], F32, kind="ExternalInput").ap()
    ones1 = nc.dram_tensor("ones1", [1, 128], F32, kind="ExternalInput").ap()
    onecol = nc.dram_tensor("onecol", [128, 1], F32, kind="ExternalInput").ap()
    ones32 = nc.dram_tensor("ones32", [32, 1], F32, kind="ExternalInput").ap()
    s_fm = nc.dram_tensor("s_fm", [8, 128, T], BF16, kind="ExternalOutput").ap()

    with contextlib.ExitStack() as _es:
        S = lambda n, sh, dt: _es.enter_context(nc.sbuf_tensor(n, sh, dt))
        wr_sb = S("wr_sb", [32, 2048], F32)
        wo_sb = S("wo_sb", [128, 16, 1024], BF16)
        gnw_sb = S("gnw_sb", [128, 16], F32)
        gnb_sb = S("gnb_sb", [128, 16], F32)
        dp_sb = S("dp_sb", [128, 16], F32)
        eye_sb = S("eye_sb", [128, 128], F32)
        on1_sb = S("on1_sb", [1, 128], F32)
        onc_sb = S("onc_sb", [128, 1], F32)
        on32_sb = S("on32_sb", [32, 1], F32)
        rt_sb = S("rt_sb", [128, 2, 32], F32)
        rT_sb = S("rT_sb", [32, T], F32)
        G_sb = S("G_sb", [32, 2, 32], F32)
        cs_sb = S("cs_sb", [32, 2, 1], F32)
        GW_sb = S("GW_sb", [32, 2048], F32)
        prod_sb = S("prod_sb", [32, 2048], F32)
        srow = S("srow", [1, 4096], F32)
        trow = S("trow", [1, 64], F32)
        zbuf = S("zbuf", [128, 2, 512], BF16)
        xcb = S("xcb", [128, 2, 512], BF16)
        zs_sb = S("zs_sb", [128, 512], BF16)
        t32a = S("t32a", [128, 512], F32)
        gnc_sb = S("gnc_sb", [128, 64], F32)
        spre = S("spre", [128, 16, 512], BF16)
        xnf = S("xnf", [128, 8, 512], BF16)
        sout = S("sout", [128, 2, 512], BF16)
        ps_t = _es.enter_context(nc.psum_tensor("ps_t", [32, 128], F32))
        ps_g = _es.enter_context(nc.psum_tensor("ps_g", [32, 128], F32))
        ps_gw = _es.enter_context(nc.psum_tensor("ps_gw", [32, 512], F32))
        ps_row = _es.enter_context(nc.psum_tensor("ps_row", [1, 512], F32))
        ps_bc = _es.enter_context(nc.psum_tensor("ps_bc", [128, 64], F32))
        big0 = _es.enter_context(nc.psum_tensor("big0", [128, 512], F32))
        big1 = _es.enter_context(nc.psum_tensor("big1", [128, 512], F32))
        s_ld = _es.enter_context(nc.semaphore("s_ld"))
        s_gp = _es.enter_context(nc.semaphore("s_gp"))
        s_te = _es.enter_context(nc.semaphore("s_te"))
        s_ve = _es.enter_context(nc.semaphore("s_ve"))
        s_sc = _es.enter_context(nc.semaphore("s_sc"))
        block = _es.enter_context(nc.Block())
        ld = Trk(s_ld); gp = Trk(s_gp); te = Trk(s_te); ve = Trk(s_ve); sct = Trk(s_sc)
        P = Prog(nc, self_trks={"vector": ve, "scalar": sct})

        for ap_, t_ in ((wr, wr_sb), (gnw, gnw_sb), (gnb, gnb_sb), (dpar, dp_sb),
                        (eye32, eye_sb), (ones1, on1_sb), (onecol, onc_sb), (ones32, on32_sb)):
            P.dma("sync", t_[:], ap_[:], ld)
        for k in range(16):
            P.dma("gpsimd", wo_sb[:, k, :], wo[k * 128:(k + 1) * 128, :], gp)
        w_c = (ld, ld.count)
        w_w = (gp, gp.count)

        for i in range(32):
            b = i // 16
            par = i % 2
            P.dma("sync", rt_sb[:, par, :], retr[i * 128:(i + 1) * 128, :], ld,
                  wait=[(te, te.count)])
            w_rt = (ld, ld.count)
            P.on("tensor", lambda e, par=par: e.transpose(ps_t[:], rt_sb[:, par, :], eye_sb[:]),
                 wait=[w_rt, w_c, (sct, sct.count)], inc=te)
            P.on("scalar", lambda e, i=i: e.activation(out=rT_sb[:, i * 128:(i + 1) * 128], in_=ps_t[0:32, :], func=AF.Identity),
                 wait=[te], inc=sct)
            P.on("tensor", lambda e, par=par, b=b, i=i: e.matmul(
                ps_g[:, b * 32:(b + 1) * 32], rt_sb[:, par, :], rt_sb[:, par, :],
                start=(i % 16 == 0), stop=(i % 16 == 15), skip_group_check=True), inc=te)
            P.on("tensor", lambda e, par=par, b=b, i=i: e.matmul(
                ps_g[:, 64 + b:65 + b], rt_sb[:, par, :], onc_sb[:],
                start=(i % 16 == 0), stop=(i % 16 == 15), skip_group_check=True), inc=te)
        w_gram = (te, te.count)
        for b in range(BQ):
            P.on("vector", lambda e, b=b: e.tensor_copy(out=G_sb[:, b, :], in_=ps_g[:, b * 32:(b + 1) * 32]), wait=[w_gram])
            P.on("vector", lambda e, b=b: e.tensor_copy(out=cs_sb[:, b, :], in_=ps_g[:, 64 + b:65 + b]))

        for b in range(BQ):
            for j in range(4):
                P.on("tensor", lambda e, b=b, j=j: e.matmul(
                    ps_gw[:], G_sb[:, b, :], wr_sb[:, j * 512:(j + 1) * 512], start=True, stop=True),
                    wait=[(ve, ve.count)], inc=te)
                P.on("vector", lambda e, j=j: e.tensor_copy(out=GW_sb[:, j * 512:(j + 1) * 512], in_=ps_gw[:]), wait=[te])
            P.on("vector", lambda e: e.tensor_tensor(out=prod_sb[:], in0=GW_sb[:], in1=wr_sb[:], op=ALU.mult))
            for j in range(4):
                P.on("tensor", lambda e, j=j: e.matmul(
                    ps_row[:], on32_sb[:], prod_sb[:, j * 512:(j + 1) * 512], start=True, stop=True),
                    wait=[(ve, ve.count)], inc=te)
                P.on("vector", lambda e, j=j: e.tensor_copy(out=srow[0:1, j * 512:(j + 1) * 512], in_=ps_row[:]), wait=[te])
            for j in range(4):
                P.on("tensor", lambda e, b=b, j=j: e.matmul(
                    ps_row[:], cs_sb[:, b, :], wr_sb[:, j * 512:(j + 1) * 512], start=True, stop=True),
                    wait=[(ve, ve.count)], inc=te)
                P.on("vector", lambda e, j=j: e.tensor_copy(out=srow[0:1, 2048 + j * 512:2048 + (j + 1) * 512], in_=ps_row[:]), wait=[te])
            NRM = 1.0 / (L * 128.0)
            P.on("vector", lambda e: e.tensor_reduce(out=trow[0:1, 0:16], in_=srow[0:1, 0:2048].rearrange("o (g d) -> o g d", d=128), axis=AX.X, op=ALU.add))
            P.on("vector", lambda e: e.tensor_reduce(out=trow[0:1, 16:32], in_=srow[0:1, 2048:4096].rearrange("o (g d) -> o g d", d=128), axis=AX.X, op=ALU.add))
            P.on("vector", lambda e, NRM=NRM: e.tensor_scalar(out=trow[0:1, 0:16], in0=trow[0:1, 0:16], scalar1=NRM, scalar2=None, op0=ALU.mult))
            P.on("vector", lambda e, NRM=NRM: e.tensor_scalar(out=trow[0:1, 16:32], in0=trow[0:1, 16:32], scalar1=NRM, scalar2=None, op0=ALU.mult))
            P.on("vector", lambda e: e.tensor_tensor(out=trow[0:1, 32:48], in0=trow[0:1, 16:32], in1=trow[0:1, 16:32], op=ALU.mult))
            P.on("vector", lambda e: e.tensor_tensor(out=trow[0:1, 0:16], in0=trow[0:1, 0:16], in1=trow[0:1, 32:48], op=ALU.subtract))
            P.on("vector", lambda e: e.tensor_scalar(out=trow[0:1, 0:16], in0=trow[0:1, 0:16], scalar1=1e-5, scalar2=None, op0=ALU.add))
            P.on("scalar", lambda e: e.activation(out=trow[0:1, 0:16], in_=trow[0:1, 0:16], func=AF.Ln), wait=[ve])
            P.on("scalar", lambda e: e.activation(out=trow[0:1, 0:16], in_=trow[0:1, 0:16], func=AF.Exp, scale=-0.5))
            P.on("tensor", lambda e, b=b: e.matmul(ps_bc[:, 0:16], on1_sb[:], trow[0:1, 16:32], start=True, stop=True), wait=[(sct, sct.count)], inc=te)
            P.on("tensor", lambda e, b=b: e.matmul(ps_bc[:, 32:48], on1_sb[:], trow[0:1, 0:16], start=True, stop=True), inc=te)
            P.on("vector", lambda e, b=b: e.tensor_copy(out=gnc_sb[:, b * 32:b * 32 + 16], in_=ps_bc[:, 0:32][:, 0:16]), wait=[te])
            P.on("vector", lambda e, b=b: e.tensor_copy(out=gnc_sb[:, b * 32 + 16:b * 32 + 32], in_=ps_bc[:, 32:48]))

        for g in range(8):
            b = g // 4
            for dc in range(8):
                for st4 in range(4):
                    tok0 = g * 512 + st4 * 128
                    P.dma("sync", xnf[:, dc, st4 * 128:(st4 + 1) * 128],
                          xn_tm[tok0:tok0 + 128, dc * 128:(dc + 1) * 128], ld,
                          wait=([(te, te.count), (ve, ve.count)] if (dc == 0 and st4 == 0) else ()), transpose=True)
            w_xn = (ld, ld.count)
            for oc in range(16):
                par = oc % 2
                P.dma("gpsimd", zbuf[:, par, :], z_fm[oc, :, g * 512:(g + 1) * 512], gp,
                      wait=[(ve, ve.count)])
                P.dma("gpsimd", xcb[:, par, :], xc_fm[oc, :, g * 512:(g + 1) * 512], gp)
                w_z = (gp, gp.count)
                psY = [big0, big1][oc % 2]
                P.on("tensor", lambda e, oc=oc, g=g, psY=psY: e.matmul(
                    psY[:], wr_sb[:, oc * 128:(oc + 1) * 128], rT_sb[:, g * 512:(g + 1) * 512],
                    start=True, stop=True), wait=[(ve, ve.count), (sct, sct.count)], inc=te)
                w_y = (te, te.count)
                P.on("scalar", lambda e, par=par: e.activation(out=zs_sb[:], in_=zbuf[:, par, :], func=AF.Silu), wait=[w_z, (ve, ve.count)])
                P.on("vector", lambda e, psY=psY, b=b, oc=oc: e.tensor_scalar(
                    out=t32a[:], in0=psY[:], scalar1=gnc_sb[:, b * 32 + (oc // 1) % 16:b * 32 + oc % 16 + 1] if False else gnc_sb[:, b * 32 + oc:b * 32 + oc + 1],
                    scalar2=gnc_sb[:, b * 32 + 16 + oc:b * 32 + 16 + oc + 1],
                    op0=ALU.subtract, op1=ALU.mult), wait=[w_y])
                P.on("vector", lambda e, oc=oc: e.tensor_scalar(
                    out=t32a[:], in0=t32a[:], scalar1=gnw_sb[:, oc:oc + 1], scalar2=gnb_sb[:, oc:oc + 1],
                    op0=ALU.mult, op1=ALU.add))
                P.on("vector", lambda e: e.tensor_tensor(out=t32a[:], in0=t32a[:], in1=zs_sb[:], op=ALU.mult), wait=[sct])
                P.on("vector", lambda e, par=par, oc=oc: e.scalar_tensor_tensor(
                    out=spre[:, oc, :], in0=xcb[:, par, :], scalar=dp_sb[:, oc:oc + 1], in1=t32a[:],
                    op0=ALU.mult, op1=ALU.add))
            w_sp = (ve, ve.count)
            for od in range(8):
                psO = [big0, big1][od % 2]
                for k in range(16):
                    P.on("tensor", lambda e, od=od, k=k, psO=psO: e.matmul(
                        psO[:], wo_sb[:, k, od * 128:(od + 1) * 128], spre[:, k, :],
                        start=(k == 0), stop=(k == 15)),
                        wait=([w_sp, w_w, w_xn, (ve, ve.count)] if k == 0 else ()),
                        inc=(te if k == 15 else None))
                w_o = (te, te.count)
                P.on("vector", lambda e, od=od, psO=psO: e.tensor_tensor(
                    out=sout[:, od % 2, :], in0=psO[:], in1=xnf[:, od, :], op=ALU.add), wait=[w_o, (gp, gp.count)])
                w_so = (ve, ve.count)
                P.dma("gpsimd", s_fm[od, :, g * 512:(g + 1) * 512], sout[:, od % 2, :], gp, wait=[w_so])

        P.on("gpsimd", lambda e: e.engine_nop(), wait=[(gp, gp.count)], inc=None)
        P.play(block)
    return nc


def run_l0_comb(front_results, core_results, wts, consts):
    nc = build_l0_comb()
    base = {
        "wr": _f32(wts["L0_attn_readout_w"]),
        "wo": _f32(wts["L0_attn_out_w"]),
        "gnw": _f32(np.asarray(wts["L0_attn_gn_w"]).reshape(16, 128).T),
        "gnb": _f32(np.asarray(wts["L0_attn_gn_b"]).reshape(16, 128).T),
        "dpar": _f32(np.asarray(wts["L0_attn_D_param"]).reshape(16, 128).T),
        "eye32": consts["eye_f32"],
        "ones1": _f32(np.ones((1, 128))),
        "onecol": _f32(np.ones((128, 1))),
        "ones32": _f32(np.ones((32, 1))),
    }
    in_maps = []
    for fr, cr in zip(front_results, core_results):
        m = dict(base)
        m["retr"] = cr["retr_o"]
        m["z_fm"] = fr["z_fm"]; m["xc_fm"] = fr["xc_fm"]; m["xn_tm"] = fr["xn_tm"]
        in_maps.append(m)
    return run_bass_kernel_spmd(nc, in_maps, core_ids=list(range(NCORE))).results



def build_l0_ffn():
    nc = bass.Bass("TRN2", target_bir_lowering=False, debug=False)
    xn_tm = nc.dram_tensor("xn_tm", [T, DM], BF16, kind="ExternalInput").ap()
    s_fm = nc.dram_tensor("s_fm", [8, 128, T], BF16, kind="ExternalInput").ap()
    x_in = nc.dram_tensor("x_in", [T, DM], F32, kind="ExternalInput").ap()
    gw = nc.dram_tensor("gw", [DM, 8], F32, kind="ExternalInput").ap()
    w1 = nc.dram_tensor("w1", [8, DM, 2048], F32, kind="ExternalInput").ap()
    b1c = nc.dram_tensor("b1c", [128, 8, 16], F32, kind="ExternalInput").ap()
    l1g = nc.dram_tensor("l1g", [128, 8, 16], F32, kind="ExternalInput").ap()
    l1b = nc.dram_tensor("l1b", [128, 8, 16], F32, kind="ExternalInput").ap()
    w2 = nc.dram_tensor("w2", [8, 2048, DM], F32, kind="ExternalInput").ap()
    b2r = nc.dram_tensor("b2r", [8, DM], F32, kind="ExternalInput").ap()
    lngb = nc.dram_tensor("lngb", [128, DM], F32, kind="ExternalInput").ap()
    lnbb = nc.dram_tensor("lnbb", [128, DM], F32, kind="ExternalInput").ap()
    eye32 = nc.dram_tensor("eye32", [128, 128], F32, kind="ExternalInput").ap()
    ones1 = nc.dram_tensor("ones1", [1, 128], F32, kind="ExternalInput").ap()
    onebf = nc.dram_tensor("onebf", [128, 1], BF16, kind="ExternalInput").ap()
    x1_o = nc.dram_tensor("x1_o", [T, DM], F32, kind="ExternalOutput").ap()
    moe_fm = nc.dram_tensor("moe_fm", [8, 128, T], BF16, kind="Internal").ap()

    with contextlib.ExitStack() as _es:
        S = lambda n, sh, dt: _es.enter_context(nc.sbuf_tensor(n, sh, dt))
        gw_sb = S("gw_sb", [128, 8, 8], BF16)
        gwf_sb = S("gwf_sb", [128, 8, 8], F32)
        xg32 = S("xg32", [128, 1024], F32)
        xnfm32 = S("xnfm32", [128, 8, 128], F32)
        w1e = S("w1e", [128, 2, 8, 2048], BF16)
        w2e = S("w2e", [128, 16, 1024], BF16)
        b1_sb = S("b1_sb", [128, 8, 16], F32)
        l1g_sb = S("l1g_sb", [128, 8, 16], F32)
        l1b_sb = S("l1b_sb", [128, 8, 16], F32)
        b2_sb = S("b2_sb", [8, DM], BF16)
        lng_sb = S("lng_sb", [128, DM], F32)
        lnb_sb = S("lnb_sb", [128, DM], F32)
        eye_sb = S("eye_sb", [128, 128], F32)
        on1_sb = S("on1_sb", [1, 128], F32)
        onbf_sb = S("onbf_sb", [128, 1], BF16)
        xnf = S("xnf", [128, 8, 512], BF16)
        glog = S("glog", [128, 8], F32)
        gcolt = S("gcolt", [128, 8], F32)
        cmb = S("cmb", [128, 8], F32)
        cmbrow = S("cmbrow", [1, 8, 512], F32)
        cmbT8 = S("cmbT8", [8, 512], BF16)
        h_sb = S("h_sb", [128, 16, 512], BF16)
        hsq = S("hsq", [128, 16, 512], BF16)
        srow_f = S("srow_f", [1, 2, 512], F32)
        bmu_sb = S("bmu_sb", [128, 512], F32)
        brs_sb = S("brs_sb", [128, 512], F32)
        bcc_sb = S("bcc_sb", [128, 512], BF16)
        t32 = S("t32", [128, 512], F32)
        ysum = S("ysum", [128, 8, 512], F32)
        moes = S("moes", [128, 2, 512], BF16)
        mt = S("mt", [128, 8, 128], BF16)
        cols = S("cols4", [128, 16], F32)
        stm = S("stm", [128, 8, 128], BF16)
        pb_h = _es.enter_context(nc.psum_tensor("pb_h", [128, 512], F32))
        pb_aux = _es.enter_context(nc.psum_tensor("pb_aux", [128, 512], F32))
        pb_y = [_es.enter_context(nc.psum_tensor(f"pb_y{i}", [128, 512], F32)) for i in range(2)]
        pb_bc = _es.enter_context(nc.psum_tensor("pb_bc", [128, 512], F32))
        pb_s2 = _es.enter_context(nc.psum_tensor("pb_s2", [1, 512], F32))
        s_ld = _es.enter_context(nc.semaphore("s_ld"))
        s_gp = _es.enter_context(nc.semaphore("s_gp"))
        s_te = _es.enter_context(nc.semaphore("s_te"))
        s_ve = _es.enter_context(nc.semaphore("s_ve"))
        s_sc = _es.enter_context(nc.semaphore("s_sc"))
        block = _es.enter_context(nc.Block())
        ld = Trk(s_ld); gp = Trk(s_gp); te = Trk(s_te); ve = Trk(s_ve); sct = Trk(s_sc)
        P = Prog(nc, self_trks={"vector": ve, "scalar": sct})

        for ap_, t_ in ((b1c, b1_sb), (l1g, l1g_sb), (l1b, l1b_sb), (lngb, lng_sb),
                        (lnbb, lnb_sb), (eye32, eye_sb), (ones1, on1_sb)):
            P.dma("sync", t_[:], ap_[:], ld)
        P.dma("gpsimd", gw_sb[:], gw.rearrange("(k p) e -> p k e", p=128), gp)
        P.dma("sync", gwf_sb[:], gw.rearrange("(k p) e -> p k e", p=128), ld)
        P.dma("gpsimd", b2_sb[:], b2r[:], gp)
        P.dma("gpsimd", onbf_sb[:], onebf[:], gp)
        w_c = (ld, ld.count)
        w_cg = (gp, gp.count)

        w1_guard = {}
        for g in range(8):
            for dc in range(8):
                for st4 in range(4):
                    tok0 = g * 512 + st4 * 128
                    P.dma("sync", xnf[:, dc, st4 * 128:(st4 + 1) * 128],
                          xn_tm[tok0:tok0 + 128, dc * 128:(dc + 1) * 128], ld,
                          wait=([(te, te.count), (ve, ve.count)] if (dc == 0 and st4 == 0) else ()),
                          transpose=True)
            w_xn = (ld, ld.count)
            for st4 in range(4):
                tok0 = g * 512 + st4 * 128
                P.dma("gpsimd", xg32[:], x_in[tok0:tok0 + 128, :], gp,
                      wait=[(ve, ve.count), (sct, sct.count), (te, te.count)])
                w_xg = (gp, gp.count)
                P.on("vector", lambda e: e.tensor_scalar(
                    out=h_sb[:, 2:4, :].rearrange('p a b -> p (a b)'), in0=xg32[:], scalar1=1.0, scalar2=None,
                    op0=ALU.mult, op1=ALU.add, accum_out=cols[:, 0:1]), wait=[w_xg])
                P.on("vector", lambda e: e.scalar_tensor_tensor(
                    out=h_sb[:, 2:4, :].rearrange('p a b -> p (a b)'), in0=xg32[:], scalar=1.0, in1=xg32[:],
                    op0=ALU.bypass, op1=ALU.mult, accum_out=cols[:, 1:2]))
                P.on("vector", lambda e: e.tensor_scalar(out=cols[:, 2:3], in0=cols[:, 0:1], scalar1=1.0 / DM, scalar2=None, op0=ALU.mult))
                P.on("vector", lambda e: e.tensor_scalar(out=cols[:, 3:4], in0=cols[:, 1:2], scalar1=1.0 / DM, scalar2=None, op0=ALU.mult))
                P.on("vector", lambda e: e.tensor_scalar(out=cols[:, 4:5], in0=cols[:, 2:3], scalar1=cols[:, 2:3], scalar2=None, op0=ALU.mult))
                P.on("vector", lambda e: e.tensor_tensor(out=cols[:, 5:6], in0=cols[:, 3:4], in1=cols[:, 4:5], op=ALU.subtract))
                P.on("vector", lambda e: e.tensor_scalar(out=cols[:, 6:7], in0=cols[:, 5:6], scalar1=1e-5, scalar2=None, op0=ALU.add))
                P.on("scalar", lambda e: e.activation(out=cols[:, 7:8], in_=cols[:, 6:7], func=AF.Ln), wait=[ve])
                P.on("scalar", lambda e: e.activation(out=cols[:, 8:9], in_=cols[:, 7:8], func=AF.Exp, scale=-0.5))
                P.on("vector", lambda e: e.tensor_scalar(out=xg32[:], in0=xg32[:], scalar1=cols[:, 2:3], scalar2=cols[:, 8:9], op0=ALU.subtract, op1=ALU.mult), wait=[sct])
                wxn32 = (ve, ve.count)
                for k in range(8):
                    P.on("tensor", lambda e, k=k: e.transpose(pb_aux[:, 256:384], xg32[:, k * 128:(k + 1) * 128], eye_sb[:]),
                         wait=[wxn32, (sct, sct.count)], inc=te)
                    P.on("scalar", lambda e, k=k: e.activation(out=xnfm32[:, k, :], in_=pb_aux[:, 256:384], func=AF.Identity), wait=[te])
                w_fm32 = (sct, sct.count)
                for k in range(8):
                    P.on("tensor", lambda e, k=k, st4=st4: e.matmul(
                        pb_aux[:, 384:392], xnfm32[:, k, :], gwf_sb[:, k, :],
                        start=(k == 0), stop=(k == 7)),
                        wait=([w_fm32, w_c, (ve, ve.count)] if k == 0 else ()),
                        inc=(te if k == 7 else None))
                P.on("vector", lambda e: e.tensor_copy(out=glog[:], in_=pb_aux[:, 384:392]), wait=[te])
                P.on("vector", lambda e: e.reduce_max(out=cols[:, 0:1], in_=glog[:], axis=AX.X))
                P.on("vector", lambda e: e.tensor_scalar(out=cols[:, 1:2], in0=cols[:, 0:1], scalar1=-1.0, scalar2=None, op0=ALU.mult))
                P.on("scalar", lambda e: e.activation(out=gcolt[:], in_=glog[:], func=AF.Exp, bias=cols[:, 1:2]), wait=[ve])
                P.on("vector", lambda e: e.tensor_reduce(out=cols[:, 2:3], in_=gcolt[:], axis=AX.X, op=ALU.add), wait=[sct])
                P.on("vector", lambda e: e.reciprocal(out=cols[:, 3:4], in_=cols[:, 2:3]))
                P.on("vector", lambda e: e.tensor_scalar(out=gcolt[:], in0=gcolt[:], scalar1=cols[:, 3:4], scalar2=None, op0=ALU.mult))
                P.on("vector", lambda e: e.reduce_max(out=cols[:, 4:5], in_=gcolt[:], axis=AX.X))
                P.on("vector", lambda e: e.tensor_scalar(out=cmb[:], in0=gcolt[:], scalar1=cols[:, 4:5], scalar2=None, op0=ALU.is_ge))
                P.on("vector", lambda e: e.scalar_tensor_tensor(out=cmb[:], in0=cmb[:], scalar=-2.0, in1=gcolt[:], op0=ALU.mult, op1=ALU.add))
                P.on("vector", lambda e: e.reduce_max(out=cols[:, 5:6], in_=cmb[:], axis=AX.X))
                P.on("vector", lambda e: e.tensor_tensor(out=cols[:, 6:7], in0=cols[:, 4:5], in1=cols[:, 5:6], op=ALU.add))
                P.on("vector", lambda e: e.reciprocal(out=cols[:, 7:8], in_=cols[:, 6:7]))
                P.on("vector", lambda e: e.tensor_scalar(out=cmb[:], in0=gcolt[:], scalar1=cols[:, 5:6], scalar2=None, op0=ALU.is_ge))
                P.on("vector", lambda e: e.tensor_tensor(out=cmb[:], in0=cmb[:], in1=gcolt[:], op=ALU.mult))
                P.on("vector", lambda e: e.tensor_scalar(out=cmb[:], in0=cmb[:], scalar1=cols[:, 7:8], scalar2=None, op0=ALU.mult))
                wcmb = (ve, ve.count)
                P.on("tensor", lambda e: e.transpose(pb_aux[0:8, 128:256], cmb[:], eye_sb[:]),
                     wait=[wcmb, (sct, sct.count)], inc=te)
                P.on("scalar", lambda e, st4=st4: e.activation(out=cmbT8[:, st4 * 128:(st4 + 1) * 128], in_=pb_aux[0:8, 128:256], func=AF.Identity), wait=[te])
                for ei in range(8):
                    P.on("tensor", lambda e, ei=ei: e.transpose(pb_aux[0:1, 0:128], cmb[:, ei:ei + 1], eye_sb[:]),
                         wait=[(sct, sct.count)], inc=te)
                    P.on("scalar", lambda e, ei=ei, st4=st4: e.activation(out=cmbrow[0:1, ei, st4 * 128:(st4 + 1) * 128], in_=pb_aux[0:1, 0:128], func=AF.Identity), wait=[te])
            for ei in range(8):
                par = (g * 8 + ei) % 2
                if g == 0 and ei == 0:
                    P.dma("gpsimd", w1e[:, par, :, :], w1[ei].rearrange("(k p) f -> p k f", p=128), gp)
                w_we = w1_guard.get((g, ei), (gp, gp.count))
                ng, ne = (g, ei + 1) if ei < 7 else (g + 1, 0)
                if ng < 8:
                    P.dma("gpsimd", w1e[:, (g * 8 + ei + 1) % 2, :, :],
                          w1[ne].rearrange("(k p) f -> p k f", p=128), gp,
                          wait=[(te, te.count)])
                    w1_guard[(ng, ne)] = (gp, gp.count)
                P.dma("gpsimd", w2e[:], w2[ei].rearrange("(k p) f -> p k f", p=128), gp,
                      wait=[(te, te.count)])
                w_w2 = (gp, gp.count)
                hbanks = [pb_h, pb_aux]
                hguard = {0: None, 1: None}
                for hc in range(16):
                    hb = hbanks[hc % 2]
                    wl = [w_we, (ve, ve.count)] if hc == 0 else []
                    if hguard[hc % 2] is not None:
                        wl.append(hguard[hc % 2])
                    for k in range(8):
                        P.on("tensor", lambda e, k=k, hc=hc, hb=hb, par=par: e.matmul(
                            hb[:], w1e[:, par, k, hc * 128:(hc + 1) * 128], xnf[:, k, :],
                            start=(k == 0), stop=(k == 7)),
                            wait=(wl if k == 0 else ()),
                            inc=(te if k == 7 else None))
                    P.on("scalar", lambda e, hc=hc, ei=ei, hb=hb: e.activation(
                        out=h_sb[:, hc, :], in_=hb[:], func=AF.Identity,
                        bias=b1_sb[:, ei, hc:hc + 1]), wait=[(te, te.count)])
                    hguard[hc % 2] = (sct, sct.count)
                w_hsb = (sct, sct.count)
                for hc in range(16):
                    P.on("vector", lambda e, hc=hc: e.tensor_tensor(
                        out=hsq[:, hc, :], in0=h_sb[:, hc, :], in1=h_sb[:, hc, :], op=ALU.mult),
                        wait=([w_hsb, (te, te.count)] if hc == 0 else ()))
                w_sq = (ve, ve.count)
                for hc in range(16):
                    P.on("tensor", lambda e, hc=hc: e.matmul(
                        pb_bc[0:1, :], onbf_sb[:], h_sb[:, hc, :],
                        start=(hc == 0), stop=(hc == 15), skip_group_check=True),
                        wait=([w_sq, w_hsb] if hc == 0 else ()), inc=(te if hc == 15 else None))
                for hc in range(16):
                    P.on("tensor", lambda e, hc=hc: e.matmul(
                        pb_s2[:], onbf_sb[:], hsq[:, hc, :],
                        start=(hc == 0), stop=(hc == 15), skip_group_check=True),
                        inc=(te if hc == 15 else None))
                w_st = (te, te.count)
                P.on("vector", lambda e: e.tensor_scalar(out=srow_f[0:1, 0, :], in0=pb_bc[0:1, :], scalar1=1.0 / 2048, scalar2=None, op0=ALU.mult), wait=[w_st])
                P.on("vector", lambda e: e.tensor_scalar(out=srow_f[0:1, 1, :], in0=pb_s2[:], scalar1=1.0 / 2048, scalar2=None, op0=ALU.mult))
                P.on("vector", lambda e: e.tensor_tensor(out=t32[0:1, 0:512], in0=srow_f[0:1, 0, :], in1=srow_f[0:1, 0, :], op=ALU.mult))
                P.on("vector", lambda e: e.tensor_tensor(out=srow_f[0:1, 1, :], in0=srow_f[0:1, 1, :], in1=t32[0:1, 0:512], op=ALU.subtract))
                P.on("vector", lambda e: e.tensor_scalar(out=srow_f[0:1, 1, :], in0=srow_f[0:1, 1, :], scalar1=1e-5, scalar2=None, op0=ALU.add))
                P.on("scalar", lambda e: e.activation(out=srow_f[0:1, 1, :], in_=srow_f[0:1, 1, :], func=AF.Ln), wait=[ve])
                P.on("scalar", lambda e: e.activation(out=srow_f[0:1, 1, :], in_=srow_f[0:1, 1, :], func=AF.Exp, scale=-0.5))
                P.on("tensor", lambda e: e.matmul(pb_bc[:], on1_sb[:], srow_f[0:1, 0, :], start=True, stop=True),
                     wait=[(sct, sct.count), (ve, ve.count)], inc=te)
                P.on("vector", lambda e: e.tensor_copy(out=bmu_sb[:], in_=pb_bc[:]), wait=[te])
                P.on("tensor", lambda e: e.matmul(pb_bc[:], on1_sb[:], srow_f[0:1, 1, :], start=True, stop=True),
                     wait=[(ve, ve.count)], inc=te)
                P.on("vector", lambda e: e.tensor_copy(out=brs_sb[:], in_=pb_bc[:]), wait=[te])
                P.on("tensor", lambda e, ei=ei, g=g: e.matmul(pb_bc[:], on1_sb[:], cmbrow[0:1, ei, :], start=True, stop=True),
                     wait=[(ve, ve.count)], inc=te)
                P.on("vector", lambda e: e.tensor_copy(out=bcc_sb[:], in_=pb_bc[:]), wait=[te])
                ybanks = [pb_y[0], pb_y[1], pb_h, pb_aux, pb_bc]
                first = (ei == 0)
                if first:
                    for odi in range(5):
                        P.on("tensor", lambda e, odi=odi: e.matmul(
                            ybanks[odi][:], b2_sb[:, odi * 128:(odi + 1) * 128], cmbT8[:],
                            start=True, stop=False, skip_group_check=True),
                            wait=([(ve, ve.count), (sct, sct.count)] if odi == 0 else ()), inc=None)
                for hc in range(16):
                    P.on("vector", lambda e, hc=hc: e.tensor_tensor(out=t32[:], in0=h_sb[:, hc, :], in1=bmu_sb[:], op=ALU.subtract),
                         wait=([(te, te.count)] if hc == 0 else ()))
                    P.on("vector", lambda e, hc=hc: e.scalar_tensor_tensor(
                        out=hsq[:, hc, :], in0=t32[:], scalar=1.0, in1=brs_sb[:], op0=ALU.bypass, op1=ALU.mult))
                    P.on("scalar", lambda e, hc=hc, ei=ei: e.activation(
                        out=h_sb[:, hc, :], in_=hsq[:, hc, :], func=AF.Gelu,
                        scale=l1g_sb[:, ei, hc:hc + 1], bias=l1b_sb[:, ei, hc:hc + 1]),
                        wait=[(ve, ve.count)])
                    P.on("vector", lambda e, hc=hc: e.tensor_tensor(out=hsq[:, hc, :], in0=h_sb[:, hc, :], in1=bcc_sb[:], op=ALU.mult),
                         wait=[(sct, sct.count)])
                    wcm = (ve, ve.count)
                    for odi in range(5):
                        P.on("tensor", lambda e, odi=odi, hc=hc, first=first: e.matmul(
                            ybanks[odi][:], w2e[:, hc, odi * 128:(odi + 1) * 128], hsq[:, hc, :],
                            start=(hc == 0 and not first), stop=(hc == 15), skip_group_check=True),
                            wait=([wcm, w_w2] if odi == 0 else ()), inc=(te if (odi == 4 and hc == 15) else None))
                w_y5 = (te, te.count)
                w_hg = (ve, ve.count)
                for odi in range(5):
                    od = odi
                    if first:
                        P.on("vector", lambda e, od=od, odi=odi: e.tensor_copy(out=ysum[:, od, :], in_=ybanks[odi][:]), wait=[w_y5])
                    else:
                        P.on("vector", lambda e, od=od, odi=odi: e.tensor_tensor(out=ysum[:, od, :], in0=ysum[:, od, :], in1=ybanks[odi][:], op=ALU.add), wait=[w_y5])
                for od in range(5, 8):
                    psY = pb_y[od % 2]
                    if first:
                        P.on("tensor", lambda e, od=od, psY=psY: e.matmul(
                            psY[:], b2_sb[:, od * 128:(od + 1) * 128], cmbT8[:],
                            start=True, stop=False, skip_group_check=True),
                            wait=[(ve, ve.count)], inc=None)
                    for k in range(16):
                        P.on("tensor", lambda e, od=od, k=k, psY=psY, first=first: e.matmul(
                            psY[:], w2e[:, k, od * 128:(od + 1) * 128], hsq[:, k, :],
                            start=(k == 0 and not first), stop=(k == 15), skip_group_check=True),
                            wait=([(ve, ve.count)] if k == 0 else ()), inc=(te if k == 15 else None))
                    w_ymm = (te, te.count)
                    if first:
                        P.on("vector", lambda e, od=od, psY=psY: e.tensor_copy(out=ysum[:, od, :], in_=psY[:]), wait=[w_ymm])
                    else:
                        P.on("vector", lambda e, od=od, psY=psY: e.tensor_tensor(out=ysum[:, od, :], in0=ysum[:, od, :], in1=psY[:], op=ALU.add), wait=[w_ymm])
            for od in range(8):
                P.on("vector", lambda e, od=od: e.tensor_tensor(
                    out=moes[:, od % 2, :], in0=ysum[:, od, :], in1=xnf[:, od, :], op=ALU.add),
                    wait=[(gp, gp.count)])
                wmo = (ve, ve.count)
                P.dma("gpsimd", moe_fm[od, :, g * 512:(g + 1) * 512], moes[:, od % 2, :], gp, wait=[wmo])
        w_moe_all = (gp, gp.count)
        for i in range(32):
            for dc in range(8):
                P.dma("sync", mt[:, dc, :], moe_fm[dc, :, i * 128:(i + 1) * 128], ld,
                      wait=([w_moe_all, (ve, ve.count)] if dc == 0 else ()), transpose=True)
            for dc in range(8):
                P.dma("sync", stm[:, dc, :], s_fm[dc, :, i * 128:(i + 1) * 128], ld, transpose=True)
            P.dma("gpsimd", xg32[:], x_in[i * 128:(i + 1) * 128, :], gp, wait=[(ve, ve.count)])
            w_mt = (ld, ld.count)
            w_xf = (gp, gp.count)
            mt2 = mt[:].rearrange("p a b -> p (a b)")
            P.on("vector", lambda e: e.tensor_scalar(
                out=h_sb[:, 2:4, :].rearrange('p a b -> p (a b)'), in0=mt2, scalar1=1.0, scalar2=None,
                op0=ALU.mult, op1=ALU.add, accum_out=cols[:, 0:1]), wait=[w_mt])
            P.on("vector", lambda e: e.scalar_tensor_tensor(
                out=h_sb[:, 2:4, :].rearrange('p a b -> p (a b)'), in0=mt2, scalar=1.0, in1=mt2,
                op0=ALU.bypass, op1=ALU.mult, accum_out=cols[:, 1:2]))
            P.on("vector", lambda e: e.tensor_scalar(out=cols[:, 2:3], in0=cols[:, 0:1], scalar1=1.0 / DM, scalar2=None, op0=ALU.mult))
            P.on("vector", lambda e: e.tensor_scalar(out=cols[:, 3:4], in0=cols[:, 1:2], scalar1=1.0 / DM, scalar2=None, op0=ALU.mult))
            P.on("vector", lambda e: e.tensor_scalar(out=cols[:, 4:5], in0=cols[:, 2:3], scalar1=cols[:, 2:3], scalar2=None, op0=ALU.mult))
            P.on("vector", lambda e: e.tensor_tensor(out=cols[:, 5:6], in0=cols[:, 3:4], in1=cols[:, 4:5], op=ALU.subtract))
            P.on("vector", lambda e: e.tensor_scalar(out=cols[:, 6:7], in0=cols[:, 5:6], scalar1=1e-5, scalar2=None, op0=ALU.add))
            P.on("scalar", lambda e: e.activation(out=cols[:, 7:8], in_=cols[:, 6:7], func=AF.Ln), wait=[ve])
            P.on("scalar", lambda e: e.activation(out=cols[:, 8:9], in_=cols[:, 7:8], func=AF.Exp, scale=-0.5))
            P.on("vector", lambda e: e.tensor_scalar(out=ysum[:, 0:2, :].rearrange('p a b -> p (a b)'), in0=mt2, scalar1=cols[:, 2:3], scalar2=cols[:, 8:9], op0=ALU.subtract, op1=ALU.mult), wait=[sct])
            P.on("vector", lambda e: e.tensor_tensor(out=ysum[:, 0:2, :].rearrange('p a b -> p (a b)'), in0=ysum[:, 0:2, :].rearrange('p a b -> p (a b)'), in1=lng_sb[:], op=ALU.mult), wait=[w_c])
            P.on("vector", lambda e: e.tensor_tensor(out=ysum[:, 0:2, :].rearrange('p a b -> p (a b)'), in0=ysum[:, 0:2, :].rearrange('p a b -> p (a b)'), in1=lnb_sb[:], op=ALU.add))
            P.on("scalar", lambda e: e.activation(out=h_sb[:, 0:2, :].rearrange('p a b -> p (a b)'), in_=ysum[:, 0:2, :].rearrange('p a b -> p (a b)'), func=AF.Silu), wait=[ve])
            P.on("vector", lambda e: e.tensor_tensor(
                out=ysum[:, 0:2, :].rearrange('p a b -> p (a b)'), in0=stm[:].rearrange("p a b -> p (a b)"), in1=h_sb[:, 0:2, :].rearrange('p a b -> p (a b)'), op=ALU.mult),
                wait=[sct, w_mt])
            P.on("vector", lambda e: e.tensor_tensor(out=ysum[:, 0:2, :].rearrange('p a b -> p (a b)'), in0=ysum[:, 0:2, :].rearrange('p a b -> p (a b)'), in1=xg32[:], op=ALU.add), wait=[w_xf])
            wx1 = (ve, ve.count)
            P.dma("gpsimd", x1_o[i * 128:(i + 1) * 128, :], ysum[:, 0:2, :].rearrange('p a b -> p (a b)'), gp, wait=[wx1])

        P.on("gpsimd", lambda e: e.engine_nop(), wait=[(gp, gp.count)], inc=None)
        P.play(block)
    return nc


def run_l0_ffn(front_results, comb_results, per_core_x, wts, consts):
    nc = build_l0_ffn()
    b1 = np.asarray(wts["L0_ffn_b1"], np.float32)
    l1gv = np.asarray(wts["L0_ffn_ln1_g"], np.float32)
    l1bv = np.asarray(wts["L0_ffn_ln1_b"], np.float32)
    base = {
        "gw": _f32(wts["L0_ffn_gate_w"]),
        "w1": _f32(wts["L0_ffn_w1"]),
        "b1c": _f32(np.stack([b1[e].reshape(16, 128).T for e in range(8)], axis=1)),
        "l1g": _f32(np.stack([l1gv[e].reshape(16, 128).T for e in range(8)], axis=1)),
        "l1b": _f32(np.stack([l1bv[e].reshape(16, 128).T for e in range(8)], axis=1)),
        "w2": _f32(wts["L0_ffn_w2"]),
        "b2r": _f32(wts["L0_ffn_b2"]),
        "lngb": _f32(np.broadcast_to(wts["L0_ffn_ln_g"], (128, DM))),
        "lnbb": _f32(np.broadcast_to(wts["L0_ffn_ln_b"], (128, DM))),
        "eye32": consts["eye_f32"],
        "ones1": _f32(np.ones((1, 128))),
        "onebf": _bf(np.ones((128, 1))),
    }
    in_maps = []
    for fr, cb, xc in zip(front_results, comb_results, per_core_x):
        m = dict(base)
        m["xn_tm"] = fr["xn_tm"]; m["s_fm"] = cb["s_fm"]; m["x_in"] = _f32(xc)
        in_maps.append(m)
    return run_bass_kernel_spmd(nc, in_maps, core_ids=list(range(NCORE))).results



def build_l1_front():
    nc = bass.Bass("TRN2", target_bir_lowering=False, debug=False)
    x1 = nc.dram_tensor("x1", [T, DM], F32, kind="ExternalInput").ap()
    winT = nc.dram_tensor("winT", [DM, 4352], F32, kind="ExternalInput").ap()
    nw = nc.dram_tensor("nw", [128, 8], F32, kind="ExternalInput").ap()
    nb = nc.dram_tensor("nb", [128, 8], F32, kind="ExternalInput").ap()
    convw = nc.dram_tensor("convw", [128, 72], F32, kind="ExternalInput").ap()
    convb = nc.dram_tensor("convb", [128, 18], F32, kind="ExternalInput").ap()
    wdt = nc.dram_tensor("wdt", [DM, 32], F32, kind="ExternalInput").ap()
    dtb = nc.dram_tensor("dtb", [128, 32], F32, kind="ExternalInput").ap()
    alog = nc.dram_tensor("alog", [128, 32], F32, kind="ExternalInput").ap()
    eye32 = nc.dram_tensor("eye32", [128, 128], F32, kind="ExternalInput").ap()
    triu_i = nc.dram_tensor("triu_i", [128, 128], F32, kind="ExternalInput").ap()
    tril_s = nc.dram_tensor("tril_s", [128, 128], F32, kind="ExternalInput").ap()
    xn1_tm = nc.dram_tensor("xn1_tm", [T, DM], BF16, kind="ExternalOutput").ap()
    z2_fm = nc.dram_tensor("z2_fm", [16, 128, T], BF16, kind="ExternalOutput").ap()
    xs_fm = nc.dram_tensor("xs_fm", [16, 128, T], BF16, kind="ExternalOutput").ap()
    b_fm = nc.dram_tensor("b_fm", [128, T], BF16, kind="ExternalOutput").ap()
    c_fm = nc.dram_tensor("c_fm", [128, T], BF16, kind="ExternalOutput").ap()
    dtc = nc.dram_tensor("dtc", [T, 4, 32], F32, kind="ExternalOutput").ap()
    ahm = nc.dram_tensor("ahm", [BQ, 32, L], F32, kind="ExternalOutput").ap()
    xr2_tm = nc.dram_tensor("xr2_tm", [T, DM], BF16, kind="Internal").ap()

    with contextlib.ExitStack() as _es:
        S = lambda n, sh, dt: _es.enter_context(nc.sbuf_tensor(n, sh, dt))
        wzs = S("wzs", [128, 2, 8, 1024], BF16)
        wdt_sb = S("wdt_sb", [128, 8, 32], BF16)
        nw_sb = S("nw_sb", [128, 8], F32)
        nb_sb = S("nb_sb", [128, 8], F32)
        cw_sb = S("cw_sb", [128, 72], F32)
        cb_sb = S("cb_sb", [128, 18], F32)
        dtb_sb = S("dtb_sb", [128, 32], F32)
        na_sb = S("na_sb", [128, 32], F32)
        eye_sb = S("eye_sb", [128, 128], F32)
        triu_sb = S("triu_sb", [128, 128], F32)
        tril_sb = S("tril_sb", [128, 128], F32)
        diag_sb = S("diag_sb", [128, 72, 128], BF16)
        xt = S("xt", [128, 2, DM], F32)
        cols = S("cols5", [128, 16], F32)
        xnt = S("xnt", [128, 2, DM], BF16)
        xrt = S("xrt", [128, 2, DM], BF16)
        xw_fm = S("xw_fm", [128, 8, 512], BF16)
        xg_buf = S("xgb5", [128, 18, 515], BF16)
        z_sb = S("z_sb5", [128, 2, 512], BF16)
        o_sb = S("o_sb5", [128, 2, 512], BF16)
        gsc = S("gsc5", [128, 12, 32], F32)
        stage = S("stage5", [128, 4, 32], F32)
        ps0 = _es.enter_context(nc.psum_tensor("p50", [128, 512], F32))
        ps1 = _es.enter_context(nc.psum_tensor("p51", [128, 512], F32))
        psd = _es.enter_context(nc.psum_tensor("p5d", [128, 32], F32))
        psc = _es.enter_context(nc.psum_tensor("p5c", [128, 64], F32))
        pst = _es.enter_context(nc.psum_tensor("p5t", [32, 128], F32))
        s_ld = _es.enter_context(nc.semaphore("s_ld"))
        s_gp = _es.enter_context(nc.semaphore("s_gp"))
        s_te = _es.enter_context(nc.semaphore("s_te"))
        s_ve = _es.enter_context(nc.semaphore("s_ve"))
        s_sc = _es.enter_context(nc.semaphore("s_sc"))
        block = _es.enter_context(nc.Block())
        ld = Trk(s_ld); gp = Trk(s_gp); te = Trk(s_te); ve = Trk(s_ve); sct = Trk(s_sc)
        P = Prog(nc, self_trks={"vector": ve, "scalar": sct})

        for ap_, t_ in ((nw, nw_sb), (nb, nb_sb), (convw, cw_sb), (convb, cb_sb),
                        (dtb, dtb_sb), (alog, na_sb), (eye32, eye_sb), (triu_i, triu_sb),
                        (tril_s, tril_sb)):
            P.dma("sync", t_[:], ap_[:], ld)
        for k in range(8):
            P.dma("gpsimd", wdt_sb[:, k, :], wdt[k * 128:(k + 1) * 128, :], gp)
        w_c = (ld, ld.count)
        w_cg = (gp, gp.count)
        P.on("scalar", lambda e: e.activation(out=na_sb[:], in_=na_sb[:], func=AF.Exp), wait=[w_c])
        P.on("vector", lambda e: e.tensor_scalar(out=na_sb[:], in0=na_sb[:], scalar1=-1.0, scalar2=None, op0=ALU.mult), wait=[sct])
        for fc in range(18):
            for j in range(4):
                P.on("vector", lambda e, fc=fc, j=j: e.tensor_scalar(
                    out=diag_sb[:, fc * 4 + j, :], in0=eye_sb[:],
                    scalar1=cw_sb[:, j * 18 + fc:j * 18 + fc + 1], scalar2=None, op0=ALU.mult),
                    wait=[w_c], inc=ve)
        w_diag = (ve, ve.count)

        def c(i):
            return cols[:, i:i + 1]

        p0g = [None, None]
        for i in range(32):
            par = i % 2
            wl = [p0g[par]] if p0g[par] is not None else []
            P.dma("gpsimd", xt[:, par, :], x1[i * 128:(i + 1) * 128, :], gp, wait=wl)
            wx = (gp, gp.count)
            xti = xt[:, par, :]
            P.on("vector", lambda e, xti=xti, par=par: e.tensor_scalar(
                out=xnt[:, par, :], in0=xti, scalar1=1.0, scalar2=None,
                op0=ALU.mult, op1=ALU.add, accum_out=c(0)), wait=[wx], inc=ve)
            P.on("vector", lambda e, xti=xti, par=par: e.scalar_tensor_tensor(
                out=xrt[:, par, :], in0=xti, scalar=1.0, in1=xti,
                op0=ALU.bypass, op1=ALU.mult, accum_out=c(1)), inc=ve)
            P.on("vector", lambda e: e.tensor_scalar(out=c(2), in0=c(0), scalar1=1.0 / DM, scalar2=None, op0=ALU.mult), inc=ve)
            P.on("vector", lambda e: e.tensor_scalar(out=c(3), in0=c(1), scalar1=1.0 / DM, scalar2=None, op0=ALU.mult), inc=ve)
            P.on("vector", lambda e: e.tensor_scalar(out=c(4), in0=c(2), scalar1=c(2), scalar2=None, op0=ALU.mult), inc=ve)
            P.on("vector", lambda e: e.tensor_tensor(out=c(5), in0=c(3), in1=c(4), op=ALU.subtract), inc=ve)
            P.on("vector", lambda e: e.tensor_scalar(out=c(6), in0=c(5), scalar1=1e-5, scalar2=None, op0=ALU.add), inc=ve)
            P.on("scalar", lambda e: e.activation(out=c(7), in_=c(6), func=AF.Ln), wait=[ve], inc=sct)
            P.on("scalar", lambda e: e.activation(out=c(8), in_=c(7), func=AF.Exp, scale=-0.5), inc=sct)
            P.on("vector", lambda e: e.reciprocal(out=c(9), in_=c(6)), wait=[sct], inc=ve)
            P.on("vector", lambda e: e.tensor_tensor(out=c(10), in0=c(5), in1=c(9), op=ALU.mult), inc=ve)
            P.on("vector", lambda e: e.tensor_scalar(out=c(11), in0=c(10), scalar1=1e-5, scalar2=None, op0=ALU.add), inc=ve)
            P.on("scalar", lambda e: e.activation(out=c(12), in_=c(11), func=AF.Ln), wait=[ve], inc=sct)
            P.on("scalar", lambda e: e.activation(out=c(13), in_=c(12), func=AF.Exp, scale=-0.5), inc=sct)
            P.on("vector", lambda e: e.tensor_tensor(out=c(14), in0=c(8), in1=c(13), op=ALU.mult), wait=[sct], inc=ve)
            P.on("vector", lambda e, xti=xti, par=par: e.tensor_scalar(
                out=xnt[:, par, :], in0=xti, scalar1=c(2), scalar2=c(8), op0=ALU.subtract, op1=ALU.mult), inc=ve)
            P.on("vector", lambda e, xti=xti, par=par: e.tensor_scalar(
                out=xrt[:, par, :], in0=xti, scalar1=c(2), scalar2=c(14), op0=ALU.subtract, op1=ALU.mult), inc=ve)
            wv = (ve, ve.count)
            P.dma("gpsimd", xn1_tm[i * 128:(i + 1) * 128, :], xnt[:, par, :], gp, wait=[wv])
            P.dma("gpsimd", xr2_tm[i * 128:(i + 1) * 128, :], xrt[:, par, :], gp, wait=[wv])
            p0g[par] = (gp, gp.count)
        w_p0 = (gp, gp.count)

        psmain = [ps0, ps1]
        ps_guard = [None, None]
        psi = [0]

        def next_ps():
            i = psi[0] % 2
            psi[0] += 1
            return psmain[i], i

        def mm_acc(lhsTs, rhss, n, wait_extra=()):
            ps, i = next_ps()
            w = list(wait_extra)
            if ps_guard[i] is not None:
                w.append(ps_guard[i])
            for j in range(n):
                P.on("tensor", lambda e, a=lhsTs(j), b=rhss(j), ps=ps, j=j, n=n: e.matmul(
                    ps[:, 0:b.shape[-1]], a, b, start=(j == 0), stop=(j == n - 1)),
                    wait=(w if j == 0 else ()), inc=(te if j == n - 1 else None))
            return ps, i, (te, te.count)

        CBLK = [(0, 1024), (1024, 2048), (2048, 3072), (3072, 4096), (4096, 4352)]
        for g in range(8):
            wprev = [(te, te.count), (ve, ve.count)] if g > 0 else [w_p0]
            for dc in range(8):
                for st4 in range(4):
                    tok0 = g * 512 + st4 * 128
                    P.dma("sync", xw_fm[:, dc, st4 * 128:(st4 + 1) * 128],
                          xr2_tm[tok0:tok0 + 128, dc * 128:(dc + 1) * 128], ld,
                          wait=(wprev if (dc == 0 and st4 == 0) else ()), transpose=True)
            w_xr = (ld, ld.count)
            for dc in range(8):
                P.on("vector", lambda e, dc=dc: e.tensor_scalar(
                    out=xw_fm[:, dc, :], in0=xw_fm[:, dc, :], scalar1=nw_sb[:, dc:dc + 1],
                    scalar2=nb_sb[:, dc:dc + 1], op0=ALU.mult, op1=ALU.add), wait=[w_xr, w_c], inc=ve)
            w_rms = (ve, ve.count)
            if g % 4 == 0:
                P.on("vector", lambda e: e.memset(xg_buf[:, :, 0:3], 0.0), wait=[(te, te.count)], inc=ve)
            else:
                P.on("vector", lambda e: e.tensor_copy(out=xg_buf[:, :, 0:3], in_=xg_buf[:, :, 512:515]),
                     wait=[(te, te.count)], inc=ve)
            w_halo = (ve, ve.count)
            for bi, (c0, c1) in enumerate(CBLK[:5]):
                wpar = (g * 5 + bi) % 2
                ncol = c1 - c0
                P.dma("gpsimd", wzs[:, wpar, :, 0:ncol],
                      winT.rearrange("(k p) c -> p k c", p=128)[:, :, c0:c1],
                      gp, wait=[(te, te.count)])
                w_wz = (gp, gp.count)
                for oci in range(ncol // 128):
                    oc = (c0 // 128) + oci
                    ps, pi, wmm = mm_acc(
                        lambda k, oci=oci, wpar=wpar: wzs[:, wpar, k, oci * 128:(oci + 1) * 128],
                        lambda k: xw_fm[:, k, :], 8,
                        wait_extra=[w_rms, w_halo, w_wz])
                    if oc < 16:
                        zpar = oc % 2
                        P.on("scalar", lambda e, ps=ps, zpar=zpar: e.activation(
                            out=z_sb[:, zpar, :], in_=ps[:], func=AF.Identity),
                            wait=[wmm, (gp, gp.count)], inc=sct)
                        wz2 = (sct, sct.count)
                        P.dma("gpsimd", z2_fm[oc, :, g * 512:(g + 1) * 512], z_sb[:, zpar, :], gp, wait=[wz2])
                    else:
                        P.on("scalar", lambda e, ps=ps, oc=oc: e.activation(
                            out=xg_buf[:, oc - 16, 3:515], in_=ps[:], func=AF.Identity),
                            wait=[wmm], inc=sct)
                    ps_guard[pi] = (sct, sct.count)
            w_xg = (sct, sct.count)
            for fc in range(18):
                ps, pi, wmm = mm_acc(
                    lambda j, fc=fc: diag_sb[:, fc * 4 + j, :],
                    lambda j, fc=fc: xg_buf[:, fc, j:j + 512], 4,
                    wait_extra=[w_xg, w_diag])
                P.on("scalar", lambda e, ps=ps, fc=fc: e.activation(
                    out=o_sb[:, fc % 2, :], in_=ps[:], func=AF.Silu,
                    bias=cb_sb[:, fc:fc + 1]), wait=[wmm, (gp, gp.count)], inc=sct)
                wxc = (sct, sct.count)
                if fc < 16:
                    P.dma("gpsimd", xs_fm[fc, :, g * 512:(g + 1) * 512], o_sb[:, fc % 2, :], gp, wait=[wxc])
                elif fc == 16:
                    P.dma("gpsimd", b_fm[:, g * 512:(g + 1) * 512], o_sb[:, fc % 2, :], gp, wait=[wxc])
                else:
                    P.dma("gpsimd", c_fm[:, g * 512:(g + 1) * 512], o_sb[:, fc % 2, :], gp, wait=[wxc])
                ps_guard[pi] = wxc
            for st4 in range(4):
                for k in range(8):
                    P.on("tensor", lambda e, k=k, st4=st4: e.matmul(
                        psd[:], xw_fm[:, k, st4 * 128:(st4 + 1) * 128], wdt_sb[:, k, :],
                        start=(k == 0), stop=(k == 7)),
                        wait=([w_cg, (ve, ve.count), (sct, sct.count), (gp, gp.count)] if k == 0 else ()),
                        inc=(te if k == 7 else None))
                wdm = (te, te.count)
                Gv = lambda i: gsc[:, i, :]
                P.on("vector", lambda e: e.tensor_tensor(out=Gv(0), in0=psd[:], in1=dtb_sb[:], op=ALU.add), wait=[wdm])
                P.on("vector", lambda e: e.scalar_tensor_tensor(out=Gv(1), in0=Gv(0), scalar=-1.0, in1=Gv(0), op0=ALU.mult, op1=ALU.max))
                P.on("scalar", lambda e: e.activation(out=Gv(1), in_=Gv(1), func=AF.Exp, scale=-1.0), wait=[ve])
                P.on("scalar", lambda e: e.activation(out=Gv(1), in_=Gv(1), func=AF.Ln, bias=1.0))
                P.on("vector", lambda e: e.tensor_scalar(out=Gv(2), in0=Gv(0), scalar1=0.0, scalar2=None, op0=ALU.max), wait=[sct])
                P.on("vector", lambda e: e.tensor_tensor(out=stage[:, 0, :], in0=Gv(2), in1=Gv(1), op=ALU.add))
                P.on("vector", lambda e: e.tensor_tensor(out=Gv(3), in0=stage[:, 0, :], in1=na_sb[:], op=ALU.mult))
                wga = (ve, ve.count)
                P.on("tensor", lambda e: e.matmul(psc[:, 0:32], triu_sb[:], gsc[:, 3, :], start=True, stop=True),
                     wait=[wga, (sct, sct.count)], inc=te)
                P.on("tensor", lambda e: e.matmul(psc[:, 32:64], tril_sb[:], gsc[:, 3, :], start=True, stop=True), inc=te)
                wcs = (te, te.count)
                P.on("vector", lambda e: e.tensor_copy(out=stage[:, 1, :], in_=psc[:, 0:32]), wait=[wcs])
                P.on("scalar", lambda e: e.activation(out=stage[:, 2, :], in_=psc[:, 0:32], func=AF.Exp), wait=[wcs, ve])
                P.on("scalar", lambda e: e.activation(out=stage[:, 3, :], in_=psc[:, 32:64], func=AF.Exp))
                wst = (sct, sct.count)
                tok0 = g * 512 + st4 * 128
                P.dma("gpsimd", dtc[tok0:tok0 + 128, :, :], stage[:], gp, wait=[wst, (ve, ve.count)])
                P.on("tensor", lambda e: e.transpose(pst[:], stage[:, 1, :], eye_sb[:]),
                     wait=[(ve, ve.count), (sct, sct.count)], inc=te)
                P.on("scalar", lambda e: e.activation(out=gsc[:, 8, :].rearrange("p a -> p a") if False else gsc[0:32, 8:12, :].rearrange("p a b -> p (a b)"), in_=pst[:], func=AF.Identity),
                     wait=[te, (gp, gp.count)], inc=sct)
                wgh = (sct, sct.count)
                bq = tok0 // L
                lt = tok0 % L
                P.dma("gpsimd", ahm[bq, :, lt:lt + 128], gsc[0:32, 8:12, :].rearrange("p a b -> p (a b)"), gp, wait=[wgh])

        P.on("gpsimd", lambda e: e.engine_nop(), wait=[(gp, gp.count)], inc=None)
        P.play(block)
    return nc


def run_l1_front(x1_list, wts, consts):
    nc = build_l1_front()
    w = np.asarray(wts["L1_attn_in_proj_w"], np.float32)
    cw = np.asarray(wts["L1_attn_conv_w"], np.float32)
    base = {
        "winT": _f32(w[:4352].T),
        "wdt": _f32(w[4352:4384].T),
        "nw": _f32(np.asarray(wts["L1_attn_norm_w"]).reshape(8, 128).T),
        "nb": _f32(np.asarray(wts["L1_attn_norm_b"]).reshape(8, 128).T),
        "convw": _f32(cw.reshape(18, 128, 4).transpose(1, 2, 0).reshape(128, 72)),
        "convb": _f32(np.asarray(wts["L1_attn_conv_b"]).reshape(18, 128).T),
        "dtb": _f32(np.broadcast_to(wts["L1_attn_dt_bias"], (128, 32))),
        "alog": _f32(np.broadcast_to(wts["L1_attn_A_log"], (128, 32))),
        "eye32": consts["eye_f32"],
        "triu_i": consts["triu_incl"],
        "tril_s": consts["tril_strict"],
    }
    in_maps = [dict(base, x1=_f32(x1c)) for x1c in x1_list]
    return run_bass_kernel_spmd(nc, in_maps, core_ids=list(range(NCORE))).results



def build_l1_core():
    nc = bass.Bass("TRN2", target_bir_lowering=False, debug=False)
    xs_fm = nc.dram_tensor("xs_fm", [16, 128, T], BF16, kind="ExternalInput").ap()
    b_fm = nc.dram_tensor("b_fm", [128, T], BF16, kind="ExternalInput").ap()
    c_fm = nc.dram_tensor("c_fm", [128, T], BF16, kind="ExternalInput").ap()
    z2_fm = nc.dram_tensor("z2_fm", [16, 128, T], BF16, kind="ExternalInput").ap()
    dtc = nc.dram_tensor("dtc", [T, 4, 32], F32, kind="ExternalInput").ap()
    ahm = nc.dram_tensor("ahm", [BQ, 32, L], F32, kind="ExternalInput").ap()
    xn1_tm = nc.dram_tensor("xn1_tm", [T, DM], BF16, kind="ExternalInput").ap()
    dbc = nc.dram_tensor("dbc", [128, 32], F32, kind="ExternalInput").ap()
    rmsw = nc.dram_tensor("rmsw", [128, 16], F32, kind="ExternalInput").ap()
    woT = nc.dram_tensor("woT", [2048, DM], F32, kind="ExternalInput").ap()
    ones1 = nc.dram_tensor("ones1", [1, 128], F32, kind="ExternalInput").ap()
    onebf = nc.dram_tensor("onebf", [128, 1], BF16, kind="ExternalInput").ap()
    msku_i_i = nc.dram_tensor("msku_i_i", [128, 128], BF16, kind="ExternalInput").ap()
    s2_fm = nc.dram_tensor("s2_fm", [8, 128, T], BF16, kind="ExternalOutput").ap()
    ytm = nc.dram_tensor("ytm", [T, 2048], BF16, kind="Internal").ap()

    with contextlib.ExitStack() as _es:
        S = lambda n, sh, dt: _es.enter_context(nc.sbuf_tensor(n, sh, dt))
        msku_i = S("msku_i", [128, 128], BF16)
        on1_sb = S("on1_sb", [1, 128], F32)
        onbf_sb = S("onbf_sb", [128, 1], BF16)
        dbc_sb = S("dbc_sb", [128, 32], F32)
        rw_sb = S("rw_sb", [128, 16], F32)
        wo_sb = S("wo_sb", [128, 16, 1024], BF16)
        bfm = S("bfm", [128, 128], BF16)
        cfm = S("cfm", [128, 128], BF16)
        btm = S("btm", [128, 128], BF16)
        xstm = S("xstm", [128, 16, 128], BF16)
        xctm = S("xctm", [128, 2048], BF16)
        dtc_sb = S("dtc_sb", [128, 4, 32], F32)
        arow = S("arow", [1, 4096], F32)
        elrow = S("elrow", [1, 32], F32)
        elcols = S("elcols", [128, 32], F32)
        lh_sb = S("lh_sb", [128, 128], BF16)
        mh_sb = S("mh_sb", [128, 8, 128], BF16)
        cbm_sb = S("cbm_sb", [128, 128], BF16)
        epre6 = S("epre6", [128, 128], F32)
        bd_sb = S("bd_sb", [128, 128], BF16)
        prev_f = S("prev_f", [128, 2, 2048], F32)
        prev_b = S("prev_b", [128, 2, 2048], BF16)
        yt_sb = S("yt_sb", [128, 2048], BF16)
        t64 = S("t64", [128, 512], F32)
        y64 = S("y64", [128, 512], F32)
        yf_sb = S("yf_sb", [128, 2, 512], BF16)
        zb6 = S("zb6", [128, 2, 512], BF16)
        yg_sb = S("yg_sb", [128, 16, 512], BF16)
        sq6 = S("sq6", [128, 2, 512], BF16)
        srow6 = S("srow6", [1, 512], F32)
        br3 = S("br3", [128, 512], F32)
        xnf6 = S("xnf6", [128, 8, 512], BF16)
        so6 = S("so6", [128, 2, 512], BF16)
        psL = _es.enter_context(nc.psum_tensor("p6L", [128, 128], F32))
        psCB = _es.enter_context(nc.psum_tensor("p6CB", [128, 128], F32))
        psD = _es.enter_context(nc.psum_tensor("p6D", [128, 512], F32))
        psO = _es.enter_context(nc.psum_tensor("p6O", [128, 512], F32))
        psS = _es.enter_context(nc.psum_tensor("p6S", [128, 512], F32))
        psB = _es.enter_context(nc.psum_tensor("p6B", [128, 512], F32))
        psB2 = _es.enter_context(nc.psum_tensor("p6B2", [128, 512], F32))
        psR = _es.enter_context(nc.psum_tensor("p6R", [1, 512], F32))
        s_ld = _es.enter_context(nc.semaphore("s_ld"))
        s_gp = _es.enter_context(nc.semaphore("s_gp"))
        s_te = _es.enter_context(nc.semaphore("s_te"))
        s_ve = _es.enter_context(nc.semaphore("s_ve"))
        s_sc = _es.enter_context(nc.semaphore("s_sc"))
        block = _es.enter_context(nc.Block())
        ld = Trk(s_ld); gp = Trk(s_gp); te = Trk(s_te); ve = Trk(s_ve); sct = Trk(s_sc)
        P = Prog(nc, self_trks={"vector": ve, "scalar": sct})

        for ap_, t_ in ((msku_i_i, msku_i), (ones1, on1_sb), (dbc, dbc_sb), (rmsw, rw_sb)):
            P.dma("sync", t_[:], ap_[:], ld)
        P.dma("gpsimd", onbf_sb[:], onebf[:], gp)
        for k in range(16):
            P.dma("gpsimd", wo_sb[:, k, :], woT[k * 128:(k + 1) * 128, :], gp)
        w_c = (ld, ld.count)
        P.on("vector", lambda e: e.memset(prev_f[:], 0.0))
        P.on("vector", lambda e: e.memset(prev_b[:], 0.0))

        for ci in range(NCHS):
            for b in range(BQ):
                tok0 = b * L + ci * CH
                lt = ci * CH
                wpre = [(te, te.count), (ve, ve.count), (sct, sct.count), (gp, gp.count)]
                P.dma("sync", bfm[:], b_fm[:, tok0:tok0 + 128], ld, wait=wpre)
                P.dma("sync", cfm[:], c_fm[:, tok0:tok0 + 128], ld)
                P.dma("sync", btm[:], b_fm[:, tok0:tok0 + 128], ld, transpose=True)
                for ch in range(16):
                    P.dma("sync", xstm[:, ch, :], xs_fm[ch, :, tok0:tok0 + 128], ld, transpose=True)
                P.dma("sync", dtc_sb[:], dtc[tok0:tok0 + 128, :, :], ld)
                P.dma("sync", arow[0:1, :].rearrange("o (h t) -> o h t", h=32), ahm[b:b + 1, :, lt:lt + 128], ld)
                w_ld = (ld, ld.count)
                P.on("scalar", lambda e: e.activation(out=elrow[:], in_=arow[0:1, 127:4096:128], func=AF.Exp), wait=[w_ld])
                P.on("tensor", lambda e: e.matmul(psL[:, 0:32], on1_sb[:], elrow[:], start=True, stop=True),
                     wait=[(sct, sct.count), w_c, (ve, ve.count)], inc=te)
                P.on("vector", lambda e: e.tensor_copy(out=elcols[:], in_=psL[:, 0:32]), wait=[te])
                a3 = xctm[:].rearrange("p (h q) -> p h q", h=32)
                d3 = dtc_sb[:, 0, :].rearrange("p h -> p h 1" if False else "p (h o) -> p h o", o=1)
                aa, bb = _bc(xstm[:].rearrange("p a b -> p (a b)").rearrange("p (h q) -> p h q", h=32), d3)
                P.on("vector", lambda e, aa=aa, bb=bb, a3=a3: e.tensor_tensor(out=a3, in0=aa, in1=bb, op=ALU.mult), wait=[w_ld])
                w_xc = (ve, ve.count)
                P.on("tensor", lambda e: e.matmul(psCB[:], bfm[:], cfm[:], start=True, stop=True),
                     wait=[w_ld, (ve, ve.count)], inc=te)
                w_cb = (te, te.count)
                P.on("vector", lambda e: e.tensor_tensor(out=cbm_sb[:], in0=psCB[:], in1=msku_i[:], op=ALU.mult), wait=[w_cb])
                w_cbm = (ve, ve.count)
                xsfl = xstm[:].rearrange("p a b -> p (a b)")
                for hg in range(4):
                    for hh in range(8):
                        h = hg * 8 + hh
                        acol = dtc_sb[:, 1, h:h + 1]
                        P.on("tensor", lambda e, h=h: e.matmul(
                            psL[:], on1_sb[:], arow[0:1, h * 128:(h + 1) * 128], start=True, stop=True),
                            wait=[(ve, ve.count), (sct, sct.count)], inc=te)
                        P.on("vector", lambda e, acol=acol: e.tensor_scalar(
                            out=epre6[:], in0=psL[:], scalar1=acol, scalar2=0.0, op0=ALU.subtract, op1=ALU.min), wait=[te])
                        P.on("scalar", lambda e, hh=hh: e.activation(out=lh_sb[:], in_=epre6[:], func=AF.Exp), wait=[ve])
                        P.on("vector", lambda e, hh=hh: e.tensor_tensor(out=mh_sb[:, hh, :], in0=cbm_sb[:], in1=lh_sb[:], op=ALU.mult), wait=[sct, w_cbm])
                    wmh = (ve, ve.count)
                    for hh in range(8):
                        h = hg * 8 + hh
                        P.on("tensor", lambda e, h=h, hh=hh: e.matmul(
                            psD[:, hh * 64:(hh + 1) * 64], mh_sb[:, hh, :], xctm[:, h * 64:(h + 1) * 64], start=True, stop=True),
                            wait=([wmh, w_xc] if hh == 0 else ()), inc=(te if hh == 7 else None))
                        P.on("tensor", lambda e, h=h, hh=hh, b=b: e.matmul(
                            psO[:, hh * 64:(hh + 1) * 64], cfm[:], prev_b[:, b, h * 64:(h + 1) * 64], start=True, stop=True),
                            inc=(te if hh == 7 else None))
                    wdo = (te, te.count)
                    eap = dtc_sb[:, 2, hg * 8:(hg + 1) * 8].rearrange("p (h o) -> p h o", o=1)
                    dap = dbc_sb[:, hg * 8:(hg + 1) * 8].rearrange("p (h o) -> p h o", o=1)
                    sl = slice(hg * 512, (hg + 1) * 512)
                    a2, b2_ = _bc(psO[:].rearrange("p (h q) -> p h q", h=8), eap)
                    P.on("vector", lambda e, a2=a2, b2_=b2_: e.tensor_tensor(
                        out=t64[:].rearrange("p (h q) -> p h q", h=8), in0=a2, in1=b2_, op=ALU.mult), wait=[wdo])
                    P.on("vector", lambda e: e.tensor_tensor(out=y64[:], in0=t64[:], in1=psD[:], op=ALU.add))
                    a3, b3_ = _bc(xsfl[:, sl].rearrange("p (h q) -> p h q", h=8), dap)
                    P.on("vector", lambda e, a3=a3, b3_=b3_: e.tensor_tensor(
                        out=t64[:].rearrange("p (h q) -> p h q", h=8), in0=a3, in1=b3_, op=ALU.mult))
                    P.on("vector", lambda e, sl=sl: e.tensor_tensor(out=yt_sb[:, sl], in0=t64[:], in1=y64[:], op=ALU.add))
                wy = (ve, ve.count)
                P.dma("gpsimd", ytm[tok0:tok0 + 128, :], yt_sb[:], gp, wait=[wy])
                for hg in range(4):
                    for hh in range(8):
                        h = hg * 8 + hh
                        dscol = dtc_sb[:, 3, h:h + 1]
                        P.on("vector", lambda e, dscol=dscol: e.tensor_scalar(
                            out=bd_sb[:], in0=btm[:], scalar1=dscol, scalar2=None, op0=ALU.mult))
                        wbd = (ve, ve.count)
                        P.on("tensor", lambda e, h=h, hh=hh: e.matmul(
                            psS[:, hh * 64:(hh + 1) * 64], bd_sb[:], xctm[:, h * 64:(h + 1) * 64], start=True, stop=True),
                            wait=[wbd], inc=te)
                    wps = (te, te.count)
                    elap = elcols[:, hg * 8:(hg + 1) * 8].rearrange("p (h o) -> p h o", o=1)
                    sl = slice(hg * 512, (hg + 1) * 512)
                    a4, b4_ = _bc(prev_f[:, b, sl].rearrange("p (h q) -> p h q", h=8), elap)
                    P.on("vector", lambda e, a4=a4, b4_=b4_: e.tensor_tensor(
                        out=t64[:].rearrange("p (h q) -> p h q", h=8), in0=a4, in1=b4_, op=ALU.mult), wait=[wps])
                    P.on("vector", lambda e, sl=sl, b=b: e.tensor_tensor(
                        out=prev_f[:, b, sl], in0=t64[:], in1=psS[:], op=ALU.add))
                    P.on("vector", lambda e, sl=sl, b=b: e.tensor_copy(out=prev_b[:, b, sl], in_=prev_f[:, b, sl]))

        w_ssm = (gp, gp.count)
        for g in range(8):
            for dc in range(8):
                for st4 in range(4):
                    tok0 = g * 512 + st4 * 128
                    P.dma("sync", xnf6[:, dc, st4 * 128:(st4 + 1) * 128],
                          xn1_tm[tok0:tok0 + 128, dc * 128:(dc + 1) * 128], ld,
                          wait=([(te, te.count), (ve, ve.count)] if (dc == 0 and st4 == 0) else ()),
                          transpose=True)
            w_xn = (ld, ld.count)
            for ch in range(16):
                par = ch % 2
                for st4 in range(4):
                    tok0 = g * 512 + st4 * 128
                    P.dma("sync", yf_sb[:, par, st4 * 128:(st4 + 1) * 128],
                          ytm[tok0:tok0 + 128, ch * 128:(ch + 1) * 128], ld,
                          wait=([w_ssm, (ve, ve.count)] if st4 == 0 else ()), transpose=True)
                P.dma("gpsimd", zb6[:, par, :], z2_fm[ch, :, g * 512:(g + 1) * 512], gp,
                      wait=[(ve, ve.count), (sct, sct.count)])
                w_yz = [(ld, ld.count), (gp, gp.count)]
                P.on("scalar", lambda e, par=par: e.activation(out=sq6[:, 0, :], in_=zb6[:, par, :], func=AF.Silu),
                     wait=w_yz + [(ve, ve.count)])
                P.on("vector", lambda e, par=par, ch=ch: e.tensor_tensor(
                    out=yg_sb[:, ch, :], in0=yf_sb[:, par, :], in1=sq6[:, 0, :], op=ALU.mult), wait=[sct] + w_yz)
                P.on("vector", lambda e, ch=ch: e.tensor_tensor(
                    out=sq6[:, 1, :], in0=yg_sb[:, ch, :], in1=yg_sb[:, ch, :], op=ALU.mult))
                P.on("tensor", lambda e, ch=ch: e.matmul(
                    psR[:], onbf_sb[:], sq6[:, 1, :], start=(ch == 0), stop=(ch == 15), skip_group_check=True),
                    wait=[(ve, ve.count)], inc=te)
            w_sr = (te, te.count)
            P.on("vector", lambda e: e.tensor_scalar(out=srow6[:], in0=psR[:], scalar1=1.0 / 2048, scalar2=None, op0=ALU.mult), wait=[w_sr])
            P.on("vector", lambda e: e.tensor_scalar(out=srow6[:], in0=srow6[:], scalar1=1e-5, scalar2=None, op0=ALU.add))
            P.on("scalar", lambda e: e.activation(out=srow6[:], in_=srow6[:], func=AF.Ln), wait=[ve])
            P.on("scalar", lambda e: e.activation(out=srow6[:], in_=srow6[:], func=AF.Exp, scale=-0.5))
            P.on("tensor", lambda e: e.matmul(psB2[:], on1_sb[:], srow6[:], start=True, stop=True),
                 wait=[(sct, sct.count), (ve, ve.count)], inc=te)
            P.on("vector", lambda e: e.tensor_copy(out=br3[:], in_=psB2[:]), wait=[te])
            for ch in range(16):
                P.on("vector", lambda e, ch=ch: e.tensor_tensor(out=sq6[:, 0, :], in0=yg_sb[:, ch, :], in1=br3[:], op=ALU.mult))
                P.on("vector", lambda e, ch=ch: e.tensor_scalar(
                    out=yg_sb[:, ch, :], in0=sq6[:, 0, :], scalar1=rw_sb[:, ch:ch + 1], scalar2=None, op0=ALU.mult))
            w_yg = (ve, ve.count)
            for od in range(8):
                psY = [psB, psB2][od % 2]
                for k in range(16):
                    P.on("tensor", lambda e, od=od, k=k, psY=psY: e.matmul(
                        psY[:], wo_sb[:, k, od * 128:(od + 1) * 128], yg_sb[:, k, :],
                        start=(k == 0), stop=(k == 15), skip_group_check=True),
                        wait=([w_yg, (ve, ve.count)] if k == 0 else ()), inc=(te if k == 15 else None))
                w_om = (te, te.count)
                P.on("vector", lambda e, od=od, psY=psY: e.tensor_tensor(
                    out=so6[:, od % 2, :], in0=psY[:], in1=xnf6[:, od, :], op=ALU.add),
                    wait=[w_om, (gp, gp.count)])
                wso = (ve, ve.count)
                P.dma("gpsimd", s2_fm[od, :, g * 512:(g + 1) * 512], so6[:, od % 2, :], gp, wait=[wso])

        P.on("gpsimd", lambda e: e.engine_nop(), wait=[(gp, gp.count)], inc=None)
        P.play(block)
    return nc


def run_l1_core(l1f_results, wts, consts):
    nc = build_l1_core()
    base = {
        "dbc": _f32(np.broadcast_to(wts["L1_attn_D"], (128, 32))),
        "rmsw": _f32(np.asarray(wts["L1_attn_rms_w"]).reshape(16, 128).T),
        "woT": _f32(np.asarray(wts["L1_attn_out_proj_w"]).T),
        "ones1": _f32(np.ones((1, 128))),
        "onebf": _bf(np.ones((128, 1))),
        "msku_i_i": consts["mask_u_incl"],
    }
    in_maps = []
    for r in l1f_results:
        m = dict(base)
        for k in ("xs_fm", "b_fm", "c_fm", "z2_fm", "dtc", "ahm", "xn1_tm"):
            m[k] = r[k]
        in_maps.append(m)
    return run_bass_kernel_spmd(nc, in_maps, core_ids=list(range(NCORE))).results



def build_l1_ffn():
    nc = bass.Bass("TRN2", target_bir_lowering=False, debug=False)
    x1 = nc.dram_tensor("x1", [T, DM], F32, kind="ExternalInput").ap()
    s2_fm = nc.dram_tensor("s2_fm", [8, 128, T], BF16, kind="ExternalInput").ap()
    wg = nc.dram_tensor("wg", [2816, 1024], F32, kind="ExternalInput").ap()
    wu = nc.dram_tensor("wu", [2816, 1024], F32, kind="ExternalInput").ap()
    wd = nc.dram_tensor("wd", [1024, 2816], F32, kind="ExternalInput").ap()
    out_o = nc.dram_tensor("out_o", [T, DM], F32, kind="ExternalOutput").ap()
    xq_tm = nc.dram_tensor("xq_tm", [T, DM], BF16, kind="Internal").ap()
    wgq = nc.dram_tensor("wgq", [2816, 1024], BF16, kind="Internal").ap()
    wuq = nc.dram_tensor("wuq", [2816, 1024], BF16, kind="Internal").ap()
    wdq = nc.dram_tensor("wdq", [1024, 2816], BF16, kind="Internal").ap()
    h_fm = nc.dram_tensor("h_fm", [22, 128, T], BF16, kind="Internal").ap()
    hq_tm = nc.dram_tensor("hq_tm", [T, 2816], BF16, kind="Internal").ap()
    ff_fm = nc.dram_tensor("ff_fm", [8, 128, T], BF16, kind="Internal").ap()

    with contextlib.ExitStack() as _es:
        S = lambda n, sh, dt: _es.enter_context(nc.sbuf_tensor(n, sh, dt))
        wgT = S("wgT", [128, 8, 2816], BF16)
        wuT = S("wuT", [128, 8, 2816], BF16)
        wdT = S("wdT", [128, 22, 1024], BF16)
        qsrc = S("qsrc", [128, 2816], F32)
        qt1 = S("qt1", [128, 2816], F32)
        qt2 = S("qt2", [128, 2816], F32)
        qbm = S("qbm", [128, 176], F32)
        qrs = S("qrs", [128, 176], F32)
        qdst = S("qdst", [128, 2816], BF16)
        cols = S("cols7", [128, 16], F32)
        hfb = S("hfb", [128, 2, 512], BF16)
        htm = S("htm", [128, 22, 128], BF16)
        hq_fm = S("hq_fm", [128, 22, 512], BF16)
        ffb = S("ffb", [128, 2, 512], BF16)
        xf7 = S("xf7", [128, DM], F32)
        big0 = _es.enter_context(nc.psum_tensor("p70", [128, 512], F32))
        big1 = _es.enter_context(nc.psum_tensor("p71", [128, 512], F32))
        s_ld = _es.enter_context(nc.semaphore("s_ld"))
        s_gp = _es.enter_context(nc.semaphore("s_gp"))
        s_te = _es.enter_context(nc.semaphore("s_te"))
        s_ve = _es.enter_context(nc.semaphore("s_ve"))
        s_sc = _es.enter_context(nc.semaphore("s_sc"))
        block = _es.enter_context(nc.Block())
        s_pc = _es.enter_context(nc.semaphore("s_pc"))
        ld = Trk(s_ld); gp = Trk(s_gp); te = Trk(s_te); ve = Trk(s_ve); sct = Trk(s_sc)
        pc = Trk(s_pc)
        P = Prog(nc, self_trks={"vector": ve, "scalar": sct})

        def emit_qdq(srcap, dstap, W, wait0=()):
            NB = W // 16
            halves = [("vector", ve, 0, W, 0, NB)]
            def both(fn, w=()):
                for engname, trk, c0, c1, b0, b1 in halves:
                    if c1 <= c0:
                        continue
                    P.on(engname, fn(c0, c1, b0, b1), wait=list(w), inc=(None if engname == "vector" else trk))
            both(lambda c0, c1, b0, b1: (lambda e: e.scalar_tensor_tensor(
                out=qt1[:, c0:c1], in0=srcap[:, c0:c1], scalar=-1.0, in1=srcap[:, c0:c1],
                op0=ALU.mult, op1=ALU.max)), w=list(wait0))
            P.on("vector", lambda e, W=W, NB=NB: e.tensor_reduce(
                out=qbm[:, 0:NB], in_=qt1[:, 0:W].rearrange("p (n b) -> p n b", b=16),
                axis=AX.X, op=ALU.max))
            P.on("vector", lambda e: e.tensor_scalar(out=qbm[:, 0:NB], in0=qbm[:, 0:NB], scalar1=1.0 / 6.0, scalar2=1e-12, op0=ALU.mult, op1=ALU.max))
            P.on("vector", lambda e: e.reciprocal(out=qrs[:, 0:NB], in_=qbm[:, 0:NB]))
            wrs = (ve, ve.count)
            def bcm(dst, srct, colt):
                def mk(c0, c1, b0, b1):
                    a2, b2_ = _bc(srct[:, c0:c1].rearrange("p (n b) -> p n b", b=16),
                                  colt[:, b0:b1].rearrange("p (n o) -> p n o", o=1))
                    return lambda e: e.tensor_tensor(out=dst[:, c0:c1].rearrange("p (n b) -> p n b", b=16), in0=a2, in1=b2_, op=ALU.mult)
                return mk
            both(bcm(qt2, srcap, qrs), w=[wrs])
            both(lambda c0, c1, b0, b1: (lambda e: e.tensor_scalar(out=qsrc[:, c0:c1], in0=qt2[:, c0:c1], scalar1=2.249, scalar2=-2.249, op0=ALU.min, op1=ALU.max)))
            both(lambda c0, c1, b0, b1: (lambda e: e.tensor_scalar(out=qsrc[:, c0:c1], in0=qsrc[:, c0:c1], scalar1=2.0, scalar2=MAGIC, op0=ALU.mult, op1=ALU.add)))
            both(lambda c0, c1, b0, b1: (lambda e: e.tensor_scalar(out=qsrc[:, c0:c1], in0=qsrc[:, c0:c1], scalar1=-MAGIC, scalar2=None, op0=ALU.add)))
            both(lambda c0, c1, b0, b1: (lambda e: e.tensor_scalar(out=qsrc[:, c0:c1], in0=qsrc[:, c0:c1], scalar1=0.5, scalar2=None, op0=ALU.mult)))
            for th in (2.5, 3.5, 5.0, 5.0):
                both(lambda c0, c1, b0, b1, th=th: (lambda e: e.scalar_tensor_tensor(
                    out=qsrc[:, c0:c1], in0=qt2[:, c0:c1], scalar=th, in1=qsrc[:, c0:c1], op0=ALU.is_gt, op1=ALU.add)))
            both(lambda c0, c1, b0, b1: (lambda e: e.tensor_scalar(out=qt1[:, c0:c1], in0=qt2[:, c0:c1], scalar1=-1.0, scalar2=None, op0=ALU.mult)))
            both(lambda c0, c1, b0, b1: (lambda e: e.tensor_scalar(out=qt2[:, c0:c1], in0=qt1[:, c0:c1], scalar1=2.5, scalar2=None, op0=ALU.is_gt)))
            for th in (3.5, 5.0, 5.0):
                both(lambda c0, c1, b0, b1, th=th: (lambda e: e.scalar_tensor_tensor(
                    out=qt2[:, c0:c1], in0=qt1[:, c0:c1], scalar=th, in1=qt2[:, c0:c1], op0=ALU.is_gt, op1=ALU.add)))
            both(lambda c0, c1, b0, b1: (lambda e: e.tensor_tensor(out=qsrc[:, c0:c1], in0=qsrc[:, c0:c1], in1=qt2[:, c0:c1], op=ALU.subtract)))
            both(bcm(dstap, qsrc, qbm))
            return [(ve, ve.count), (pc, pc.count)]

        wq_guard = {}
        for (wname, wsrc, wdst, ntile, wid) in (("wg", wg, wgq, 22, 1024), ("wu", wu, wuq, 22, 1024), ("wd", wd, wdq, 8, 2816)):
            for ti in range(ntile):
                P.dma("gpsimd", qsrc[:, 0:wid], wsrc[ti * 128:(ti + 1) * 128, :], gp,
                      wait=[(ve, ve.count)])
                wq = emit_qdq(qsrc[:, 0:wid], qdst[:, 0:wid], wid, wait0=[(gp, gp.count)])
                P.dma("gpsimd", wdst[ti * 128:(ti + 1) * 128, :], qdst[:, 0:wid], gp, wait=wq)
            wq_guard[wname] = (gp, gp.count)
            if wname == "wg":
                for k in range(8):
                    for oc in range(22):
                        P.dma("sync", wgT[:, k, oc * 128:(oc + 1) * 128],
                              wgq[oc * 128:(oc + 1) * 128, k * 128:(k + 1) * 128], ld,
                              wait=([wq_guard["wg"]] if (k == 0 and oc == 0) else ()), transpose=True)
            elif wname == "wu":
                for k in range(8):
                    for oc in range(22):
                        P.dma("sync", wuT[:, k, oc * 128:(oc + 1) * 128],
                              wuq[oc * 128:(oc + 1) * 128, k * 128:(k + 1) * 128], ld,
                              wait=([wq_guard["wu"]] if (k == 0 and oc == 0) else ()), transpose=True)
            else:
                for k in range(22):
                    for oc in range(8):
                        P.dma("sync", wdT[:, k, oc * 128:(oc + 1) * 128],
                              wdq[oc * 128:(oc + 1) * 128, k * 128:(k + 1) * 128], ld,
                              wait=([wq_guard["wd"]] if (k == 0 and oc == 0) else ()), transpose=True)
        w_wT = (ld, ld.count)
        xq_guard = {}
        h_guard = {}
        hq_guard = {}
        def c(i):
            return cols[:, i:i + 1]
        for i in range(32):
            P.dma("gpsimd", xf7[:], x1[i * 128:(i + 1) * 128, :], gp, wait=[(ve, ve.count)])
            wx = (gp, gp.count)
            P.on("vector", lambda e: e.tensor_scalar(
                out=qt1[:, 0:DM], in0=xf7[:], scalar1=1.0, scalar2=None,
                op0=ALU.mult, op1=ALU.add, accum_out=c(0)), wait=[wx])
            P.on("vector", lambda e: e.scalar_tensor_tensor(
                out=qt2[:, 0:DM], in0=xf7[:], scalar=1.0, in1=xf7[:],
                op0=ALU.bypass, op1=ALU.mult, accum_out=c(1)))
            P.on("vector", lambda e: e.tensor_scalar(out=c(2), in0=c(0), scalar1=1.0 / DM, scalar2=None, op0=ALU.mult))
            P.on("vector", lambda e: e.tensor_scalar(out=c(3), in0=c(1), scalar1=1.0 / DM, scalar2=None, op0=ALU.mult))
            P.on("vector", lambda e: e.tensor_scalar(out=c(4), in0=c(2), scalar1=c(2), scalar2=None, op0=ALU.mult))
            P.on("vector", lambda e: e.tensor_tensor(out=c(5), in0=c(3), in1=c(4), op=ALU.subtract))
            P.on("vector", lambda e: e.tensor_scalar(out=c(6), in0=c(5), scalar1=1e-5, scalar2=None, op0=ALU.add))
            P.on("scalar", lambda e: e.activation(out=c(7), in_=c(6), func=AF.Ln), wait=[ve])
            P.on("scalar", lambda e: e.activation(out=c(8), in_=c(7), func=AF.Exp, scale=-0.5))
            P.on("vector", lambda e: e.tensor_scalar(out=qsrc[:, 0:DM], in0=xf7[:], scalar1=c(2), scalar2=c(8), op0=ALU.subtract, op1=ALU.mult), wait=[sct])
            wq = emit_qdq(qsrc[:, 0:DM], qdst[:, 0:DM], DM)
            P.dma("gpsimd", xq_tm[i * 128:(i + 1) * 128, :], qdst[:, 0:DM], gp, wait=wq)
            xq_guard[i] = (gp, gp.count)
        w_b = (gp, gp.count)
        for g in range(8):
            for dc in range(8):
                for st4 in range(4):
                    tok0 = g * 512 + st4 * 128
                    P.dma("sync", hq_fm[:, dc, st4 * 128:(st4 + 1) * 128],
                          xq_tm[tok0:tok0 + 128, dc * 128:(dc + 1) * 128], ld,
                          wait=([xq_guard[g * 4 + 3], (te, te.count), (ve, ve.count)] if (dc == 0 and st4 == 0) else ()),
                          transpose=True)
            w_xq = (ld, ld.count)
            for oc in range(22):
                for k in range(8):
                    P.on("tensor", lambda e, k=k, oc=oc: e.matmul(
                        big0[:], wgT[:, k, oc * 128:(oc + 1) * 128], hq_fm[:, k, :],
                        start=(k == 0), stop=(k == 7)),
                        wait=([w_xq, w_wT, (sct, sct.count), (ve, ve.count)] if k == 0 else ()),
                        inc=(te if k == 7 else None))
                for k in range(8):
                    P.on("tensor", lambda e, k=k, oc=oc: e.matmul(
                        big1[:], wuT[:, k, oc * 128:(oc + 1) * 128], hq_fm[:, k, :],
                        start=(k == 0), stop=(k == 7)), inc=(te if k == 7 else None))
                wmm = (te, te.count)
                P.on("scalar", lambda e: e.activation(out=qdst[:, 0:512], in_=big0[:], func=AF.Silu), wait=[wmm, (ve, ve.count)])
                P.on("vector", lambda e: e.tensor_copy(out=qdst[:, 512:1024], in_=big1[:]), wait=[wmm])
                P.on("vector", lambda e, oc=oc: e.tensor_tensor(out=hfb[:, oc % 2, :], in0=qdst[:, 0:512], in1=qdst[:, 512:1024], op=ALU.mult),
                     wait=[sct, (gp, gp.count)])
                whf = (ve, ve.count)
                P.dma("gpsimd", h_fm[oc, :, g * 512:(g + 1) * 512], hfb[:, oc % 2, :], gp, wait=[whf])
            h_guard[g] = (gp, gp.count)
        w_h = (gp, gp.count)
        for i in range(32):
            for ch in range(22):
                P.dma("sync", htm[:, ch, :], h_fm[ch, :, i * 128:(i + 1) * 128], ld,
                      wait=([h_guard[i // 4], (ve, ve.count)] if ch == 0 else ()), transpose=True)
            w_ht = (ld, ld.count)
            P.on("vector", lambda e: e.tensor_copy(out=qsrc[:], in_=htm[:].rearrange("p a b -> p (a b)")), wait=[w_ht])
            wq = emit_qdq(qsrc[:], qdst[:], 2816)
            P.dma("gpsimd", hq_tm[i * 128:(i + 1) * 128, :], qdst[:], gp, wait=wq)
            hq_guard[i] = (gp, gp.count)
        w_hq = (gp, gp.count)
        for g in range(8):
            for ch in range(22):
                for st4 in range(4):
                    tok0 = g * 512 + st4 * 128
                    P.dma("sync", hq_fm[:, ch, st4 * 128:(st4 + 1) * 128],
                          hq_tm[tok0:tok0 + 128, ch * 128:(ch + 1) * 128], ld,
                          wait=([hq_guard[g * 4 + 3], (te, te.count)] if (ch == 0 and st4 == 0) else ()),
                          transpose=True)
            w_hf = (ld, ld.count)
            for od in range(8):
                psY = [big0, big1][od % 2]
                for k in range(22):
                    P.on("tensor", lambda e, od=od, k=k, psY=psY: e.matmul(
                        psY[:], wdT[:, k, od * 128:(od + 1) * 128], hq_fm[:, k, :],
                        start=(k == 0), stop=(k == 21)),
                        wait=([w_hf, (ve, ve.count)] if k == 0 else ()), inc=(te if k == 21 else None))
                wdm = (te, te.count)
                P.on("vector", lambda e, od=od, psY=psY: e.tensor_copy(out=ffb[:, od % 2, :], in_=psY[:]),
                     wait=[wdm, (gp, gp.count)])
                wff = (ve, ve.count)
                P.dma("gpsimd", ff_fm[od, :, g * 512:(g + 1) * 512], ffb[:, od % 2, :], gp, wait=[wff])
        w_ff = (gp, gp.count)
        for i in range(32):
            for dc in range(8):
                P.dma("sync", htm[:, dc, :], ff_fm[dc, :, i * 128:(i + 1) * 128], ld,
                      wait=([w_ff, (ve, ve.count), (sct, sct.count)] if dc == 0 else ()), transpose=True)
            for dc in range(8):
                P.dma("sync", htm[:, 8 + dc, :], s2_fm[dc, :, i * 128:(i + 1) * 128], ld, transpose=True)
            P.dma("gpsimd", xf7[:], x1[i * 128:(i + 1) * 128, :], gp, wait=[(ve, ve.count)])
            w_f = (ld, ld.count)
            w_x = (gp, gp.count)
            P.on("scalar", lambda e: e.activation(out=qdst[:, 0:DM], in_=htm[:, 0:8, :].rearrange("p a b -> p (a b)"), func=AF.Silu),
                 wait=[w_f])
            P.on("vector", lambda e: e.tensor_tensor(out=qt2[:, 0:DM], in0=htm[:, 8:16, :].rearrange("p a b -> p (a b)"), in1=qdst[:, 0:DM], op=ALU.mult),
                 wait=[sct, w_f])
            P.on("vector", lambda e: e.tensor_tensor(out=qt2[:, 0:DM], in0=qt2[:, 0:DM], in1=xf7[:], op=ALU.add), wait=[w_x])
            wo7 = (ve, ve.count)
            P.dma("gpsimd", out_o[i * 128:(i + 1) * 128, :], qt2[:, 0:DM], gp, wait=[wo7])

        P.on("gpsimd", lambda e: e.engine_nop(), wait=[(gp, gp.count)], inc=None)
        P.play(block)
    return nc


def run_l1_ffn(x1_list, l1c_results, wts):
    nc = build_l1_ffn()
    base = {
        "wg": _f32(wts["L1_ffn_w_gate"]),
        "wu": _f32(wts["L1_ffn_w_up"]),
        "wd": _f32(wts["L1_ffn_w_down"]),
    }
    in_maps = []
    for x1c, cr in zip(x1_list, l1c_results):
        m = dict(base)
        m["x1"] = _f32(x1c)
        m["s2_fm"] = cr["s2_fm"]
        in_maps.append(m)
    return run_bass_kernel_spmd(nc, in_maps, core_ids=list(range(NCORE))).results



def kernel(**inputs):
    consts = make_consts()
    x = np.asarray(inputs["L0_attn_x"])
    per_core = [np.ascontiguousarray(x[2 * i:2 * i + 2].reshape(T, DM)) for i in range(NCORE)]
    fr = run_l0_front(per_core, inputs, consts)
    cr = run_l0_core(fr, consts)
    cb = run_l0_comb(fr, cr, inputs, consts)
    f4 = run_l0_ffn(fr, cb, per_core, inputs, consts)
    x1_list = [np.asarray(r["x1_o"]) for r in f4]
    l1f = run_l1_front(x1_list, inputs, consts)
    l1c = run_l1_core(l1f, inputs, consts)
    l1o = run_l1_ffn(x1_list, l1c, inputs)
    out = np.concatenate([np.asarray(r["out_o"]).reshape(2, L, DM) for r in l1o], axis=0)
    return out.astype(np.float32)
